# version 08
# baseline (speedup 1.0000x reference)
import math
from contextlib import ExitStack

import numpy as np

import concourse.bass as bass
import concourse.mybir as mybir
from concourse.bass_utils import run_bass_kernel_spmd

F32 = mybir.dt.float32
BF16 = mybir.dt.bfloat16
I32 = mybir.dt.int32
AF = mybir.ActivationFunctionType
ALU = mybir.AluOpType
AX = mybir.AxisListType

D = 2048
DC = 16
NH = 16
HM = 1408
HMC = 11
HS = 2816
NEXP = 10
B_SHIFT = 24.0
SCALE = 1.0 / math.sqrt(128.0)
LN_EPS = 1e-5
PIH = math.pi / 2.0

ENGINES = ("sync", "tensor", "vector", "scalar", "gpsimd")
ROPEB = [("ropebf", i) for i in range(4)]


class Prog:

    def __init__(self, nc):
        self.nc = nc
        self.thunks = {e: [] for e in ENGINES}
        self.sem_handles = {}
        self.counts = {}
        self.clocks = {e: {} for e in ENGINES}
        self.last_write = {}
        self.readers = {}
        self.stats = {e: 0 for e in ENGINES}

    def add_sem(self, name, handle):
        self.sem_handles[name] = handle
        self.counts[name] = 0

    def emit(self, eng, fn, reads=(), writes=(), dma_sem=None):
        need = {}

        def _need(tok):
            if tok is not None:
                s, v = tok
                if need.get(s, 0) < v:
                    need[s] = v

        if eng == "scalar":
            reads = list(reads) + [("cpool",)]
        for r in reads:
            _need(self.last_write.get(r))
        for r in writes:
            _need(self.last_write.get(r))
            for s, v in self.readers.get(r, {}).items():
                _need((s, v))
        clk = self.clocks[eng]
        waits = []
        for s, v in need.items():
            if clk.get(s, 0) < v:
                assert v <= self.counts[s], f"wait on future value {s}>={v}"
                waits.append((s, v))
                clk[s] = v
        if dma_sem is not None:
            semname, inc = dma_sem, 16
        else:
            semname, inc = "e_" + eng, 1
        self.counts[semname] += inc
        tok = (semname, self.counts[semname])
        for r in writes:
            self.last_write[r] = tok
            self.readers[r] = {}
        for r in reads:
            d = self.readers.setdefault(r, {})
            if d.get(semname, 0) < tok[1]:
                d[semname] = tok[1]
        semh = self.sem_handles[semname]
        handles = self.sem_handles
        self.stats[eng] += 1

        def thunk(engobj, waits=tuple(waits), fn=fn, semh=semh, inc=inc, handles=handles):
            for s, v in waits:
                engobj.wait_ge(handles[s], v)
            fn(engobj).then_inc(semh, inc)

        self.thunks[eng].append(thunk)
        return tok

    def raw(self, eng, fn):
        self.thunks[eng].append(fn)

    def set_write_token(self, region, tok):
        self.last_write[region] = tok
        self.readers[region] = {}


def bpages(name, c0, c1):
    return [(name, p) for p in range(c0 // 1024, (c1 + 1023) // 1024)]


def ps0_banks(c0, c1):
    return [("PS0", b) for b in range(c0 // 512, (c1 + 511) // 512)]


def _recip_lp(nc, v, out, in_):
    with nc.allow_low_precision(reason="bf16 softmax reciprocal used as matmul operand"):
        return v.reciprocal(out=out, in_=in_)


class Ctx:
    pass


def build_program(S=2048, debug=()):
    C = Ctx()
    C.S = S
    C.TC = S // 128
    C.QB = min(1024, S)
    C.NQB = S // C.QB
    C.TH = min(1024, S)
    C.NHALF = S // C.TH
    C.THC = C.TH // 128

    nc = bass.Bass("TRN2", target_bir_lowering=False, debug=False)
    C.nc = nc

    C.x_d = nc.dram_tensor("x", [S, D], F32, kind="ExternalInput").ap()
    C.wq_d = nc.dram_tensor("L0_attn_Wq", [D, D], F32, kind="ExternalInput").ap()
    C.wk_d = nc.dram_tensor("L0_attn_Wk", [D, D], F32, kind="ExternalInput").ap()
    C.wv_d = nc.dram_tensor("L0_attn_Wv", [D, D], F32, kind="ExternalInput").ap()
    C.wo_d = nc.dram_tensor("L0_attn_Wo", [D, D], F32, kind="ExternalInput").ap()
    C.gw_d = nc.dram_tensor("L0_ffn_gate_w", [8, D], F32, kind="ExternalInput").ap()
    C.wg_d = nc.dram_tensor("L0_ffn_w_gate", [8, D, HM], F32, kind="ExternalInput").ap()
    C.wu_d = nc.dram_tensor("L0_ffn_w_up", [8, D, HM], F32, kind="ExternalInput").ap()
    C.wd_d = nc.dram_tensor("L0_ffn_w_down", [8, HM, D], F32, kind="ExternalInput").ap()
    C.sgw_d = nc.dram_tensor("L0_ffn_shgate_w", [D], F32, kind="ExternalInput").ap()
    C.swg_d = nc.dram_tensor("L0_ffn_shw_gate", [D, HS], F32, kind="ExternalInput").ap()
    C.swu_d = nc.dram_tensor("L0_ffn_shw_up", [D, HS], F32, kind="ExternalInput").ap()
    C.swd_d = nc.dram_tensor("L0_ffn_shw_down", [HS, D], F32, kind="ExternalInput").ap()
    C.out_d = nc.dram_tensor("out", [S, D], F32, kind="ExternalOutput").ap()
    C.dbg = {}
    dbg_shapes = {
        "xnt": [128, 32768], "apre": [128, 32768], "resattn": [128, 32768],
        "res": [128, 32768], "combine": [128, C.TC * 8], "sgate": [128, C.TC],
        "kt": [128, 2048], "qt": [128, 2048], "vsb": [128, 2048],
        "cos": [128, 2048], "sin": [128, 2048], "hpr": [128, 11264],
    }
    debug = set(debug)
    if "kt" in debug:
        debug |= {"qt", "vsb"}
    if "cos" in debug:
        debug.add("sin")
    if "combine" in debug:
        debug.add("sgate")
    for name in sorted(debug):
        C.dbg[name] = nc.dram_tensor("dbg_" + name, dbg_shapes[name], F32, kind="ExternalOutput").ap()

    st = ExitStack()

    def sb(name, shape, dtype):
        return st.enter_context(nc.sbuf_tensor(name, shape, dtype))

    C.BIG1 = sb("BIG1", [128, 32768], BF16)
    C.BIG2 = sb("BIG2", [128, 32768], BF16)
    C.BIG3 = sb("BIG3", [128, 32768], BF16)
    C.cosT = sb("cosT", [128, 2048], BF16)
    C.sinS = sb("sinS", [128, 2048], BF16)
    C.ident = sb("ident", [128, 128], BF16)
    C.mask128 = sb("mask128", [128, 128], BF16)
    C.ones_c = sb("ones_c", [128, 1], BF16)
    C.ones_r = sb("ones_r", [1, 128], BF16)
    C.cpool = sb("cpool", [128, 4], F32)
    C.invf = sb("invf", [64, 1], F32)
    C.gstage = sb("gstage", [9, 2048], BF16)
    C.gate_wT = sb("gate_wT", [128, DC * 9], BF16)
    C.combine = sb("combine", [128, C.TC * 8], F32)
    C.sgate = sb("sgate", [128, C.TC], F32)
    C.sglog = sb("sglog", [128, C.TC], F32)
    C.rt = sb("rt", [128, 64], F32)
    C.lgb = sb("lgb", [128, 144], F32)
    C.rotM = sb("rotM", [128, 128], BF16)
    C.lnst = sb("lnst", [128, 96], F32)

    C.PS0 = st.enter_context(nc.psum_tensor("PS0", [128, 2048], F32))
    C.PS1 = st.enter_context(nc.psum_tensor("PS1", [128, 1024], F32))
    C.PSTa = st.enter_context(nc.psum_tensor("PSTa", [128, 1024], BF16))
    C.PSTb = st.enter_context(nc.psum_tensor("PSTb", [128, 1024], BF16))

    semnames = (["e_" + e for e in ENGINES]
                + ["xt0", "xt1", "st0", "st1", "wg0", "wg1", "wu0", "wu1",
                   "wdp0", "wdp1", "gsem", "mrsem", "osem", "dbgsem"])
    P = Prog(nc)
    C.P = P
    for sn in semnames:
        P.add_sem(sn, st.enter_context(nc.semaphore(sn)))

    nc.const_aps.register(F32, float(-B_SHIFT), C.cpool[:, 0:1])
    nc.const_aps.register(F32, float(LN_EPS), C.cpool[:, 1:2])
    nc.const_aps.register(F32, float(PIH), C.cpool[:, 2:3])
    nc.const_aps.register(F32, 0.0, C.cpool[:, 3:4])

    C.cosF = C.BIG2[:, 0:4096].bitcast(F32)
    C.sinF = C.BIG2[:, 4096:8192].bitcast(F32)
    C.xtile = [C.BIG2[:, 8192 + k * 4096: 12288 + k * 4096].bitcast(F32) for k in range(2)]
    C.xtile_regions = [bpages("BIG2", 8192 + k * 4096, 12288 + k * 4096) for k in range(2)]
    C.xnsc = [C.BIG2[:, 16384 + k * 2048: 18432 + k * 2048] for k in range(2)]
    C.xnsc_regions = [bpages("BIG2", 16384 + k * 2048, 18432 + k * 2048) for k in range(2)]
    C.lndump = C.BIG2[:, 20480:22528]
    C.io_j = C.BIG2[:, 22528:22784].bitcast(I32)
    C.io_i = C.BIG2[:, 22784:23040].bitcast(I32)
    C.outstage = [C.BIG2[:, 23040 + k * 4096: 27136 + k * 4096].bitcast(F32) for k in range(2)]
    C.outstage_regions = [bpages("BIG2", 23040 + k * 4096, 27136 + k * 4096) for k in range(2)]

    C.WT = {"k": C.BIG3[:, 0:2048], "q": C.BIG3[:, 2048:4096], "v": C.BIG3[:, 4096:6144]}
    C.WT_regions = {"k": bpages("BIG3", 0, 2048), "q": bpages("BIG3", 2048, 4096),
                    "v": bpages("BIG3", 4096, 6144)}
    C.stg = [C.BIG3[:, 6144:8192], C.BIG3[:, 8192:10240]]
    C.stg_regions = [bpages("BIG3", 6144, 8192), bpages("BIG3", 8192, 10240)]
    C.k_raw = C.BIG3[:, 10240:12288]
    C.k_raw_regions = bpages("BIG3", 10240, 12288)
    C.t1b = C.BIG3[:, 12288:14336]
    C.t1b_regions = bpages("BIG3", 12288, 14336)
    C.vT_sb = C.BIG3[:, 14336:16384]
    C.vT_regions = bpages("BIG3", 14336, 16384)
    C.kT = C.BIG3[:, 16384:18432]
    C.kT_regions = bpages("BIG3", 16384, 18432)
    C.qT = C.BIG3[:, 18432:20480]
    C.qT_regions = bpages("BIG3", 18432, 20480)
    C.V_sb = C.BIG3[:, 20480:22528]
    C.V_regions = bpages("BIG3", 20480, 22528)
    C.probsT = [C.BIG3[:, 22528:23552], C.BIG3[:, 23552:24576]]
    C.probsT_regions = [bpages("BIG3", 22528, 23552), bpages("BIG3", 23552, 24576)]
    C.denF = C.BIG3[:, 24576:26624].bitcast(F32)
    C.den_regions = bpages("BIG3", 24576, 26624)
    C.rcp = [C.BIG3[:, 26624:27648], C.BIG3[:, 29696:30720], C.BIG3[:, 30720:31744]]
    C.rcp_regions = [bpages("BIG3", 26624, 27648), bpages("BIG3", 29696, 30720),
                     bpages("BIG3", 30720, 31744)]
    C.rep_sb = [C.BIG3[:, 27648:28672], C.BIG3[:, 28672:29696]]
    C.rep_regions = [bpages("BIG3", 27648, 28672), bpages("BIG3", 28672, 29696)]

    C.WoT = [C.BIG1[:, 0:8192], C.BIG1[:, 8192:16384]]
    C.WoT_regions = [bpages("BIG1", 0, 8192), bpages("BIG1", 8192, 16384)]
    C.wstg = [C.BIG1[:, 16384:18432], C.BIG1[:, 18432:20480]]
    C.wstg_regions = [bpages("BIG1", 16384, 18432), bpages("BIG1", 18432, 20480)]

    C.wgslot = [C.BIG2[:, 11264:13312], C.BIG2[:, 13312:15360]]
    C.wgslot_regions = [bpages("BIG2", 11264, 13312), bpages("BIG2", 13312, 15360)]
    C.wuslot = [C.BIG2[:, 15360:17408], C.BIG2[:, 17408:19456]]
    C.wuslot_regions = [bpages("BIG2", 15360, 17408), bpages("BIG2", 17408, 19456)]
    C.gsil = [C.BIG2[:, 19456:20480], C.BIG2[:, 20480:21504]]
    C.gsil_regions = [bpages("BIG2", 19456, 20480), bpages("BIG2", 20480, 21504)]

    def hpr_view(hc, c0, c1):
        return C.BIG2[:, hc * 1024 + c0: hc * 1024 + c1]

    def hpr_regions(hc):
        return bpages("BIG2", hc * 1024, (hc + 1) * 1024)

    def wd_view(area, hc, c0, c1):
        base = 21504 + area * 5632 + hc * 512
        return C.BIG2[:, base + c0: base + c1]

    def wd_area_view(area):
        return C.BIG2[:, 21504 + area * 5632: 21504 + area * 5632 + 5632]

    C.wd_area_view = wd_area_view
    C.wd_pages = bpages("BIG2", 21504, 32768)

    def wd_regions(area):
        return [("wda", area)]

    def xnt(dc, t0, t1):
        return C.BIG1[:, dc * 2048 + t0: dc * 2048 + t1]

    def xnt_regions(dc):
        return bpages("BIG1", dc * 2048, dc * 2048 + S)

    def apre(h, t0, t1):
        return C.BIG2[:, h * 2048 + t0: h * 2048 + t1]

    def apre_regions(h):
        return bpages("BIG2", h * 2048, h * 2048 + S)

    def resv(tcg, c0, c1):
        return C.BIG3[:, tcg * 2048 + c0: tcg * 2048 + c1]

    def res_regions(tcg):
        return bpages("BIG3", tcg * 2048, (tcg + 1) * 2048)

    def xnt_cols_regions(t0, t1):
        out = []
        for dc in range(DC):
            out += bpages("BIG1", dc * 2048 + t0, dc * 2048 + t1)
        return out

    C.xnt_cols_regions = xnt_cols_regions
    C.hpr_view, C.hpr_regions = hpr_view, hpr_regions
    C.wd_view, C.wd_regions = wd_view, wd_regions
    C.xnt, C.xnt_regions = xnt, xnt_regions
    C.apre, C.apre_regions = apre, apre_regions
    C.resv, C.res_regions = resv, res_regions
    C.XNT_ALL = [r for dc in range(DC) for r in xnt_regions(dc)]

    block = st.enter_context(nc.Block())
    with st:
        _emit_all(C)
        for e in ENGINES:
            def runner(engobj, thunks=P.thunks[e]):
                for t in thunks:
                    t(engobj)
            getattr(block, e)(runner)
    return nc, C



def _emit_all(C):
    P = C.P
    _emit_consts(C)
    _load_wslice(C, 0, "k", C.wk_d, 0)
    _load_wslice(C, 0, "q", C.wq_d, 1)
    _load_wslice(C, 0, "v", C.wv_d, 0)
    _emit_ln(C, first=True)
    for tc in range(C.TC):
        _emit_routing_mm(C, tc)
    for tc in range(C.TC):
        _emit_routing_tc(C, tc)
    _emit_routing_final(C)
    if "xnt" in C.dbg:
        P.emit("gpsimd", lambda s: s.dma_start(
            out=C.dbg["xnt"].rearrange("p (c q) -> p c q", c=16, q=2048)[:, :, 0:C.S],
            in_=C.BIG1[:, :].rearrange("p (c q) -> p c q", c=16, q=2048)[:, :, 0:C.S]),
               reads=C.XNT_ALL + [("dbgser",)], writes=[("dbgser",)], dma_sem="dbgsem")
    if "combine" in C.dbg:
        P.emit("gpsimd", lambda s: s.dma_start(out=C.dbg["combine"], in_=C.combine[:]),
               reads=[("combine",)] + [("dbgser",)], writes=[("dbgser",)], dma_sem="dbgsem")
        P.emit("gpsimd", lambda s: s.dma_start(out=C.dbg["sgate"], in_=C.sgate[:]),
               reads=[("sgate",)] + [("dbgser",)], writes=[("dbgser",)], dma_sem="dbgsem")
    for h in range(NH):
        _emit_head(C, h)
    _emit_headnorm(C, NH - 1)

    if "apre" in C.dbg:
        P.emit("gpsimd", lambda s: s.dma_start(
            out=C.dbg["apre"].rearrange("p (c q) -> p c q", c=16, q=2048)[:, :, 0:C.S],
            in_=C.BIG2[:, :].rearrange("p (c q) -> p c q", c=16, q=2048)[:, :, 0:C.S]),
               reads=[r for hh in range(NH) for r in C.apre_regions(hh)] + [("dbgser",)], writes=[("dbgser",)], dma_sem="dbgsem")
    if "kt" in C.dbg:
        P.emit("gpsimd", lambda s: s.dma_start(out=C.dbg["kt"][:, 0:C.S], in_=C.kT[:, 0:C.S]), reads=C.kT_regions + [("dbgser",)], writes=[("dbgser",)], dma_sem="dbgsem")
        P.emit("gpsimd", lambda s: s.dma_start(out=C.dbg["qt"][:, 0:C.S], in_=C.qT[:, 0:C.S]), reads=C.qT_regions + [("dbgser",)], writes=[("dbgser",)], dma_sem="dbgsem")
        P.emit("gpsimd", lambda s: s.dma_start(out=C.dbg["vsb"][:, 0:C.S], in_=C.V_sb[:, 0:C.S]), reads=C.V_regions + [("dbgser",)], writes=[("dbgser",)], dma_sem="dbgsem")
    _emit_wo(C)
    if "resattn" in C.dbg:
        P.emit("gpsimd", lambda s: s.dma_start(out=C.dbg["resattn"][:, 0:C.TC * 2048], in_=C.BIG3[:, 0:C.TC * 2048]),
               reads=[r for t in range(C.TC) for r in C.res_regions(t)] + [("dbgser",)], writes=[("dbgser",)], dma_sem="dbgsem")
    _emit_ln(C, first=False)
    for half in range(C.NHALF):
        for e in range(NEXP):
            _emit_expert(C, half, e)
    if "hpr" in C.dbg:
        P.emit("gpsimd", lambda s: s.dma_start(out=C.dbg["hpr"], in_=C.BIG2[:, 0:11264]),
               reads=[r for hc in range(HMC) for r in C.hpr_regions(hc)] + [("dbgser",)], writes=[("dbgser",)], dma_sem="dbgsem")
    if "res" in C.dbg:
        P.emit("gpsimd", lambda s: s.dma_start(out=C.dbg["res"][:, 0:C.TC * 2048], in_=C.BIG3[:, 0:C.TC * 2048]),
               reads=[r for t in range(C.TC) for r in C.res_regions(t)] + [("dbgser",)], writes=[("dbgser",)], dma_sem="dbgsem")
    _emit_final(C)


def _emit_consts(C):
    P, nc = C.P, C.nc
    P.emit("gpsimd", lambda g: g.iota(C.io_j[:], pattern=[[1, 128]], base=0, channel_multiplier=0),
           writes=[("io", "j")])
    P.emit("gpsimd", lambda g: g.iota(C.io_i[:], pattern=[[0, 128]], base=0, channel_multiplier=1),
           writes=[("io", "i")])
    P.emit("gpsimd", lambda g: g.memset(C.ones_c[:], 1.0), writes=[("ones",)])
    P.emit("gpsimd", lambda g: g.memset(C.ones_r[:], 1.0), writes=[("onesr",)])
    for col, val in ((0, -B_SHIFT), (1, LN_EPS), (2, PIH), (3, 0.0)):
        P.emit("vector", lambda v, col=col, val=val: v.memset(C.cpool[:, col:col + 1], float(val)),
               writes=[("cpool",)])
    P.emit("vector", lambda v: v.tensor_tensor(out=C.ident[:], in0=C.io_j[:], in1=C.io_i[:], op=ALU.is_equal),
           reads=[("io", "j"), ("io", "i")], writes=[("ident",)])
    P.emit("vector", lambda v: v.tensor_tensor(out=C.mask128[:], in0=C.io_j[:], in1=C.io_i[:], op=ALU.is_ge),
           reads=[("io", "j"), ("io", "i")], writes=[("mask",)])

    S = C.S
    cosF, sinF = C.cosF, C.sinF
    CFR, SFR = [("cosF",)], [("sinF",)]
    xt0 = C.xtile[0]
    xt1 = C.xtile[1]
    tAr = C.xtile_regions[0] + C.xtile_regions[1]
    P.emit("scalar", lambda s: s.activation(out=C.invf[:], in_=C.io_i[0:64, 0:1], func=AF.Exp,
                                            scale=-math.log(10000.0) / 64.0),
           reads=[("io", "i")], writes=[("invf",)])
    P.emit("vector", lambda v: v.memset(cosF[0:64, 0:1], 1.0), writes=CFR)
    P.emit("vector", lambda v: v.memset(sinF[0:64, 0:1], 0.0), writes=SFR)
    P.emit("scalar", lambda s: s.activation(out=cosF[0:64, 1:2], in_=C.invf[:], func=AF.Sin, bias=PIH),
           reads=[("invf",), ("cpool",)], writes=CFR)
    P.emit("scalar", lambda s: s.activation(out=sinF[0:64, 1:2], in_=C.invf[:], func=AF.Sin),
           reads=[("invf",)], writes=SFR)
    L = 1
    while L < S - 1:
        n = min(L, S - 1 - L)
        P.emit("vector", lambda v, n=n, L=L: v.tensor_scalar(
            out=xt0[0:64, 0:n], in0=cosF[0:64, 1:1 + n], scalar1=cosF[0:64, L:L + 1], scalar2=None, op0=ALU.mult),
            reads=CFR, writes=tAr)
        P.emit("vector", lambda v, n=n, L=L: v.tensor_scalar(
            out=xt1[0:64, 0:n], in0=sinF[0:64, 1:1 + n], scalar1=sinF[0:64, L:L + 1], scalar2=None, op0=ALU.mult),
            reads=SFR + CFR, writes=tAr)
        P.emit("vector", lambda v, n=n, L=L: v.tensor_tensor(
            out=cosF[0:64, L + 1:L + 1 + n], in0=xt0[0:64, 0:n], in1=xt1[0:64, 0:n], op=ALU.subtract),
            reads=tAr, writes=CFR)
        P.emit("vector", lambda v, n=n, L=L: v.tensor_scalar(
            out=xt0[0:64, 0:n], in0=sinF[0:64, 1:1 + n], scalar1=cosF[0:64, L:L + 1], scalar2=None, op0=ALU.mult),
            reads=SFR + CFR, writes=tAr)
        P.emit("vector", lambda v, n=n, L=L: v.tensor_scalar(
            out=xt1[0:64, 0:n], in0=cosF[0:64, 1:1 + n], scalar1=sinF[0:64, L:L + 1], scalar2=None, op0=ALU.mult),
            reads=SFR + CFR, writes=tAr)
        P.emit("vector", lambda v, n=n, L=L: v.tensor_tensor(
            out=sinF[0:64, L + 1:L + 1 + n], in0=xt0[0:64, 0:n], in1=xt1[0:64, 0:n], op=ALU.add),
            reads=tAr, writes=SFR)
        L += n
    P.emit("gpsimd", lambda g: g.dma_start(out=C.cosT[0:64, 0:S], in_=cosF[0:64, 0:S]),
           reads=CFR, writes=[("ropebf", 0)], dma_sem="mrsem")
    P.emit("gpsimd", lambda g: g.dma_start(out=C.cosT[64:128, 0:S], in_=cosF[0:64, 0:S]),
           reads=CFR, writes=[("ropebf", 1)], dma_sem="mrsem")
    P.emit("gpsimd", lambda g: g.dma_start(out=C.sinS[0:64, 0:S], in_=sinF[0:64, 0:S]),
           reads=SFR, writes=[("ropebf", 2)], dma_sem="mrsem")
    P.emit("gpsimd", lambda g: g.dma_start(out=C.sinS[64:128, 0:S], in_=sinF[0:64, 0:S]),
           reads=SFR, writes=[("ropebf", 3)], dma_sem="mrsem")
    P.emit("vector", lambda v: v.tensor_scalar(out=xt0[:, 0:128], in0=C.io_j[:], scalar1=64.0,
                                               scalar2=None, op0=ALU.add), reads=[("io", "j")], writes=tAr)
    P.emit("vector", lambda v: v.tensor_tensor(out=xt0[:, 128:256], in0=C.io_i[:], in1=xt0[:, 0:128],
                                               op=ALU.is_equal), reads=[("io", "i")] + tAr, writes=tAr)
    P.emit("vector", lambda v: v.tensor_scalar(out=xt0[:, 256:384], in0=C.io_j[:], scalar1=-64.0,
                                               scalar2=None, op0=ALU.add), reads=[("io", "j")], writes=tAr)
    P.emit("vector", lambda v: v.tensor_tensor(out=xt0[:, 384:512], in0=C.io_i[:], in1=xt0[:, 256:384],
                                               op=ALU.is_equal), reads=[("io", "i")] + tAr, writes=tAr)
    P.emit("vector", lambda v: v.tensor_tensor(out=C.rotM[:], in0=xt0[:, 384:512], in1=xt0[:, 128:256],
                                               op=ALU.subtract), reads=tAr, writes=[("rotM",)])
    if "cos" in C.dbg:
        P.emit("gpsimd", lambda s: s.dma_start(out=C.dbg["cos"][:, 0:C.S], in_=C.cosT[:, 0:C.S]), reads=ROPEB + [("dbgser",)], writes=[("dbgser",)], dma_sem="dbgsem")
        P.emit("gpsimd", lambda s: s.dma_start(out=C.dbg["sin"][:, 0:C.S], in_=C.sinS[:, 0:C.S]), reads=ROPEB + [("dbgser",)], writes=[("dbgser",)], dma_sem="dbgsem")

    P.emit("gpsimd", lambda g: g.dma_start(out=C.gstage[0:8, :], in_=C.gw_d),
           writes=[("gstage", 0)], dma_sem="gsem")
    P.emit("gpsimd", lambda g: g.dma_start(out=C.gstage[8:9, :], in_=C.sgw_d.rearrange("(o d) -> o d", o=1)),
           writes=[("gstage", 1)], dma_sem="gsem")
    for dc in range(DC):
        pst, preg = (C.PSTa, ("PSTa",)) if dc % 2 == 0 else (C.PSTb, ("PSTb",))
        P.emit("tensor", lambda t, dc=dc, pst=pst: t.transpose(
            pst[:, 0:9], C.gstage[0:9, dc * 128:(dc + 1) * 128], C.ident[0:9, 0:9]),
            reads=[("gstage", 0), ("gstage", 1), ("ident",)], writes=[preg])
        P.emit("vector", lambda v, dc=dc, pst=pst: v.tensor_copy(
            out=C.gate_wT[:, dc * 9:(dc + 1) * 9], in_=pst[:, 0:9]),
            reads=[preg], writes=[("gate_wT",)])


def _emit_ln(C, first=True, interleave=None):
    P, S, TC = C.P, C.S, C.TC
    lnst = C.lnst
    big1_3d = C.BIG1[:, :].rearrange("p (c q) -> p c q", c=16, q=2048)
    for tc in range(TC):
        sl = tc % 2
        xv = C.xtile[sl]
        xreg = C.xtile_regions[sl]
        c = sl * 32
        P.emit("sync", lambda s, tc=tc, xv=xv: s.dma_start(out=xv[:], in_=C.x_d[tc * 128:(tc + 1) * 128, :]),
               writes=xreg, dma_sem="xt%d" % sl)
        if first:
            for bi in range(4):
                P.emit("vector", lambda v, xv=xv, c=c, bi=bi: v.bn_stats(
                    out=lnst[:, c + bi * 6: c + (bi + 1) * 6], in_=xv[:, bi * 512:(bi + 1) * 512]),
                    reads=xreg, writes=[("lnst", sl, bi)])
            P.emit("vector", lambda v, c=c: v.bn_aggr(out=lnst[:, c + 24:c + 26], in_=lnst[:, c:c + 24]),
                   reads=[("lnst", sl, bi) for bi in range(4)], writes=[("lnst", sl, 4)])
            P.emit("scalar", lambda s, c=c: s.activation(out=lnst[:, c + 26:c + 27], in_=lnst[:, c + 25:c + 26],
                                                         func=AF.Ln, bias=LN_EPS),
                   reads=[("lnst", sl, 4)], writes=[("lnst", sl, 5)])
            P.emit("scalar", lambda s, c=c, tc=tc: s.activation(
                out=lnst[:, 64 + tc * 2:64 + tc * 2 + 1], in_=lnst[:, c + 26:c + 27],
                func=AF.Exp, scale=-0.5),
                reads=[("lnst", sl, 5)], writes=[("lnstc", tc, 0)])
            P.emit("vector", lambda v, c=c, tc=tc: v.tensor_tensor(
                out=lnst[:, 64 + tc * 2 + 1:64 + tc * 2 + 2], in0=lnst[:, c + 24:c + 25],
                in1=lnst[:, 64 + tc * 2:64 + tc * 2 + 1], op=ALU.mult),
                reads=[("lnst", sl, 4), ("lnstc", tc, 0)], writes=[("lnstc", tc, 1)])
            P.emit("vector", lambda v, tc=tc: v.tensor_scalar(
                out=lnst[:, 64 + tc * 2 + 1:64 + tc * 2 + 2],
                in0=lnst[:, 64 + tc * 2 + 1:64 + tc * 2 + 2],
                scalar1=-1.0, scalar2=None, op0=ALU.mult),
                reads=[("lnstc", tc, 1)], writes=[("lnstc", tc, 1)])
        xnv = C.xnsc[sl]
        xnreg = C.xnsc_regions[sl]
        P.emit("scalar", lambda s, xv=xv, xnv=xnv, tc=tc: s.activation(
            out=xnv[:], in_=xv[:], func=AF.Identity,
            scale=lnst[:, 64 + tc * 2:64 + tc * 2 + 1],
            bias=lnst[:, 64 + tc * 2 + 1:64 + tc * 2 + 2]),
            reads=xreg + [("lnstc", tc, 0), ("lnstc", tc, 1)], writes=xnreg)
        for g4 in range(4):
            pst, preg = (C.PSTa, ("PSTa",)) if g4 % 2 == 0 else (C.PSTb, ("PSTb",))

            def tf(t, g4=g4, pst=pst, xnv=xnv):
                last = None
                for j in range(4):
                    dc = g4 * 4 + j
                    last = t.transpose(pst[:, j * 128:(j + 1) * 128],
                                       xnv[:, dc * 128:(dc + 1) * 128], C.ident[:])
                return last

            P.emit("tensor", tf, reads=xnreg + [("ident",)], writes=[preg])
            eng = "vector" if g4 % 2 == 0 else "scalar"

            def evf(e, g4=g4, pst=pst, tc=tc):
                dst = big1_3d[:, g4 * 4:(g4 + 1) * 4, tc * 128:(tc + 1) * 128]
                srcv = pst[:, 0:512].rearrange("p (c u) -> p c u", c=4, u=128)
                if hasattr(e, "tensor_copy"):
                    return e.tensor_copy(out=dst, in_=srcv)
                return e.copy(out=dst, in_=srcv)

            P.emit(eng, evf,
                   reads=[preg], writes=[r for j in range(4) for r in C.xnt_regions(g4 * 4 + j)])
        if interleave is not None:
            interleave(tc)


def _emit_routing_mm(C, tc):
    P = C.P
    PS1 = C.PS1
    xn_reads = C.xnt_cols_regions(tc * 128, (tc + 1) * 128)
    ro = (tc % 2) * 512

    def mmf(t, tc=tc, ro=ro):
        last = None
        for dc in range(DC):
            last = t.matmul(PS1[:, ro:ro + 9], C.xnt(dc, tc * 128, (tc + 1) * 128),
                            C.gate_wT[:, dc * 9:(dc + 1) * 9],
                            start=(dc == 0), stop=(dc == DC - 1))
        return last

    P.emit("tensor", mmf, reads=[("gate_wT",)] + xn_reads, writes=[("PS1", tc % 2)])
    P.emit("scalar", lambda s, ro=ro, tc=tc: s.activation(out=C.lgb[:, tc * 9:tc * 9 + 9],
                                                          in_=PS1[:, ro:ro + 9], func=AF.Identity),
           reads=[("PS1", tc % 2)], writes=[("lgb", tc)])


def _emit_routing_tc(C, tc):
    P = C.P
    rt = C.rt
    R = [("rt",)]
    if True:
        P.emit("vector", lambda v, tc=tc: v.tensor_copy(out=rt[:, 0:9], in_=C.lgb[:, tc * 9:tc * 9 + 9]),
               reads=[("lgb", tc)], writes=R)
        P.emit("vector", lambda v: v.tensor_reduce(out=rt[:, 16:17], in_=rt[:, 0:8], axis=AX.X, op=ALU.max),
               reads=R, writes=R)
        P.emit("vector", lambda v: v.tensor_scalar(out=rt[:, 17:18], in0=rt[:, 16:17], scalar1=-1.0,
                                                   scalar2=None, op0=ALU.mult), reads=R, writes=R)
        P.emit("scalar", lambda s: s.activation(out=rt[:, 24:32], in_=rt[:, 0:8], func=AF.Exp,
                                                bias=rt[:, 17:18]), reads=R, writes=R)
        P.emit("vector", lambda v: v.tensor_reduce(out=rt[:, 32:33], in_=rt[:, 24:32], axis=AX.X, op=ALU.max),
               reads=R, writes=R)
        P.emit("vector", lambda v: v.tensor_scalar(out=rt[:, 33:41], in0=rt[:, 24:32], scalar1=rt[:, 32:33],
                                                   scalar2=None, op0=ALU.is_ge), reads=R, writes=R)
        P.emit("vector", lambda v: v.tensor_reduce(out=rt[:, 41:42], in_=rt[:, 33:41], axis=AX.X, op=ALU.add),
               reads=R, writes=R)
        P.emit("vector", lambda v: v.tensor_scalar(out=rt[:, 42:50], in0=rt[:, 33:41], scalar1=2.0,
                                                   scalar2=None, op0=ALU.mult), reads=R, writes=R)
        P.emit("vector", lambda v: v.tensor_tensor(out=rt[:, 42:50], in0=rt[:, 24:32], in1=rt[:, 42:50],
                                                   op=ALU.subtract), reads=R, writes=R)
        P.emit("vector", lambda v: v.tensor_reduce(out=rt[:, 50:51], in_=rt[:, 42:50], axis=AX.X, op=ALU.max),
               reads=R, writes=R)
        P.emit("vector", lambda v: v.tensor_scalar(out=rt[:, 51:52], in0=rt[:, 41:42], scalar1=2.0,
                                                   scalar2=None, op0=ALU.is_ge), reads=R, writes=R)
        P.emit("vector", lambda v: v.tensor_tensor(out=rt[:, 52:53], in0=rt[:, 32:33], in1=rt[:, 50:51],
                                                   op=ALU.subtract), reads=R, writes=R)
        P.emit("vector", lambda v: v.tensor_tensor(out=rt[:, 52:53], in0=rt[:, 51:52], in1=rt[:, 52:53],
                                                   op=ALU.mult), reads=R, writes=R)
        P.emit("vector", lambda v: v.tensor_tensor(out=rt[:, 53:54], in0=rt[:, 50:51], in1=rt[:, 52:53],
                                                   op=ALU.add), reads=R, writes=R)
        P.emit("vector", lambda v: v.tensor_scalar(out=rt[:, 54:62], in0=rt[:, 24:32], scalar1=rt[:, 53:54],
                                                   scalar2=None, op0=ALU.is_ge), reads=R, writes=R)
        P.emit("vector", lambda v: v.tensor_tensor(out=rt[:, 54:62], in0=rt[:, 24:32], in1=rt[:, 54:62],
                                                   op=ALU.mult), reads=R, writes=R)
        P.emit("vector", lambda v: v.tensor_tensor(out=rt[:, 62:63], in0=rt[:, 32:33], in1=rt[:, 53:54],
                                                   op=ALU.add), reads=R, writes=R)
        P.emit("vector", lambda v: v.reciprocal(out=rt[:, 63:64], in_=rt[:, 62:63]), reads=R, writes=R)
        P.emit("vector", lambda v, tc=tc: v.tensor_scalar(out=C.combine[:, tc * 8:(tc + 1) * 8],
                                                          in0=rt[:, 54:62], scalar1=rt[:, 63:64],
                                                          scalar2=None, op0=ALU.mult),
               reads=R, writes=[("combine",)])
        P.emit("vector", lambda v, tc=tc: v.tensor_copy(out=C.sglog[:, tc:tc + 1], in_=rt[:, 8:9]),
               reads=R, writes=[("sglog",)])


def _emit_routing_final(C):
    C.P.emit("scalar", lambda s: s.activation(out=C.sgate[:], in_=C.sglog[:], func=AF.Sigmoid),
             reads=[("sglog",)], writes=[("sgate",)])


def _load_wslice(C, h, which, src, slot):
    P = C.P
    stg, streg = C.stg[slot], C.stg_regions[slot]
    P.emit("gpsimd", lambda g, src=src, stg=stg, h=h: g.dma_start(
        out=stg[:], in_=src[h * 128:(h + 1) * 128, :]),
        writes=streg, dma_sem="st%d" % slot)
    wt, wtreg = C.WT[which], C.WT_regions[which]
    for g4 in range(4):
        pst, preg = (C.PSTa, ("PSTa",)) if g4 % 2 == 0 else (C.PSTb, ("PSTb",))

        def tf(t, g4=g4, pst=pst, stg=stg):
            last = None
            for j in range(4):
                dc = g4 * 4 + j
                last = t.transpose(pst[:, j * 128:(j + 1) * 128],
                                   stg[:, dc * 128:(dc + 1) * 128], C.ident[:])
            return last

        P.emit("tensor", tf, reads=streg + [("ident",)], writes=[preg])
        if g4 % 2 == 0:
            P.emit("vector", lambda v, g4=g4, pst=pst, wt=wt: v.tensor_copy(
                out=wt[:, g4 * 512:(g4 + 1) * 512], in_=pst[:, 0:512]),
                reads=[preg], writes=wtreg)
        else:
            P.emit("scalar", lambda s, g4=g4, pst=pst, wt=wt: s.copy(
                out=wt[:, g4 * 512:(g4 + 1) * 512], in_=pst[:, 0:512]),
                reads=[preg], writes=wtreg)


def _emit_head(C, h):
    P, S, QB, NQB = C.P, C.S, C.QB, C.NQB
    PS0, PS1 = C.PS0, C.PS1
    NP = (S + 511) // 512

    def load_wslice(which, src, slot):
        _load_wslice(C, h, which, src, slot)

    def proj(which, out_buf, out_regions, rope):
        wt, wtreg = C.WT[which], C.WT_regions[which]
        for p in range(NP):
            n0, n1 = p * 512, min(S, (p + 1) * 512)

            def mmf(t, n0=n0, n1=n1, wt=wt):
                last = None
                for dc in range(DC):
                    last = t.matmul(PS0[:, n0:n1], wt[:, dc * 128:(dc + 1) * 128],
                                    C.xnt(dc, n0, n1), start=(dc == 0), stop=(dc == DC - 1))
                return last

            P.emit("tensor", mmf, reads=wtreg + C.xnt_cols_regions(n0, n1), writes=ps0_banks(n0, n1))
            P.emit("scalar", lambda s, n0=n0, n1=n1: s.activation(
                out=C.k_raw[:, n0:n1], in_=PS0[:, n0:n1], func=AF.Identity),
                reads=ps0_banks(n0, n1), writes=C.k_raw_regions)
        if rope:
            kr, krr = C.k_raw, C.k_raw_regions
            for p in range(NP):
                n0, n1 = p * 512, min(S, (p + 1) * 512)
                P.emit("tensor", lambda t, n0=n0, n1=n1: t.matmul(
                    PS0[:, n0:n1], C.rotM[:], kr[:, n0:n1], start=True, stop=True),
                    reads=[("rotM",)] + krr, writes=ps0_banks(n0, n1))
            P.emit("vector", lambda v: v.tensor_tensor(out=C.t1b[:, 0:S], in0=kr[:, 0:S],
                                                       in1=C.cosT[:, 0:S], op=ALU.mult),
                   reads=krr + ROPEB, writes=C.t1b_regions)
            for p in range(NP):
                n0, n1 = p * 512, min(S, (p + 1) * 512)
                P.emit("vector", lambda v, ob=out_buf, n0=n0, n1=n1: v.tensor_tensor(
                    out=ob[:, n0:n1], in0=PS0[:, n0:n1], in1=C.sinS[:, n0:n1], op=ALU.mult),
                    reads=ps0_banks(n0, n1) + ROPEB, writes=out_regions)
            P.emit("vector", lambda v, ob=out_buf: v.tensor_tensor(
                out=ob[:, 0:S], in0=ob[:, 0:S], in1=C.t1b[:, 0:S], op=ALU.add),
                reads=out_regions + C.t1b_regions, writes=out_regions)
        else:
            P.emit("vector", lambda v, ob=out_buf: v.tensor_copy(out=ob[:, 0:S], in_=C.k_raw[:, 0:S]),
                   reads=C.k_raw_regions, writes=out_regions)

    proj("k", C.kT, C.kT_regions, rope=True)
    proj("q", C.qT, C.qT_regions, rope=True)
    if h > 0:
        _emit_headnorm(C, h - 1)
    proj("v", C.vT_sb, C.vT_regions, rope=False)

    KCH = S // 128
    for g4 in range((KCH + 3) // 4):
        pst, preg = (C.PSTa, ("PSTa",)) if g4 % 2 == 0 else (C.PSTb, ("PSTb",))
        nblk = min(4, KCH - g4 * 4)

        def tf(t, g4=g4, pst=pst, nblk=nblk):
            last = None
            for j in range(nblk):
                cch = g4 * 4 + j
                last = t.transpose(pst[:, j * 128:(j + 1) * 128],
                                   C.vT_sb[:, cch * 128:(cch + 1) * 128], C.ident[:])
            return last

        P.emit("tensor", tf, reads=C.vT_regions + [("ident",)], writes=[preg])
        if g4 % 2 == 0:
            P.emit("vector", lambda v, g4=g4, pst=pst, nblk=nblk: v.tensor_copy(
                out=C.V_sb[:, g4 * 512:g4 * 512 + nblk * 128], in_=pst[:, 0:nblk * 128]),
                reads=[preg], writes=C.V_regions)
        else:
            P.emit("scalar", lambda s, g4=g4, pst=pst, nblk=nblk: s.copy(
                out=C.V_sb[:, g4 * 512:g4 * 512 + nblk * 128], in_=pst[:, 0:nblk * 128]),
                reads=[preg], writes=C.V_regions)
    if h + 1 < NH:
        load_wslice_next = lambda which, srcw, slot: _load_wslice(C, h + 1, which, srcw, slot)
        load_wslice_next("k", C.wk_d, 0)
        load_wslice_next("q", C.wq_d, 1)
        load_wslice_next("v", C.wv_d, 0)

    for qb in range(NQB):
        q0 = qb * QB
        nch = (q0 + QB) // 128
        scnt = 0

        def pieces_of(cc):
            out = []
            pstart = max(cc * 128, q0)
            while pstart < q0 + QB:
                pend = min(q0 + ((pstart - q0) // 512 + 1) * 512, q0 + QB)
                out.append((pstart, pend))
                pstart = pend
            return out

        for cc in range(nch + 1):
            if cc < nch:
                psl = cc % 2
                pb, pbreg = C.probsT[psl], C.probsT_regions[psl]
                pcs = pieces_of(cc)
                merged = (len(pcs) == 2 and pcs[0][0] == q0 and QB == 1024)
                for (p0, p1) in pcs:
                    sc_off = 1024 + (scnt % 2) * 512
                    scnt += 1
                    P.emit("tensor", lambda t, cc=cc, p0=p0, p1=p1, sc_off=sc_off: t.matmul(
                        PS0[:, sc_off:sc_off + (p1 - p0)], C.kT[:, cc * 128:(cc + 1) * 128],
                        C.qT[:, p0:p1], start=True, stop=True),
                        reads=C.kT_regions + C.qT_regions, writes=[("PS0", sc_off // 512)])
                    if not merged:
                        P.emit("scalar", lambda s, p0=p0, p1=p1, sc_off=sc_off, pb=pb, q0=q0: s.activation(
                            out=pb[:, p0 - q0:p1 - q0], in_=PS0[:, sc_off:sc_off + (p1 - p0)],
                            func=AF.Exp, scale=SCALE, bias=-B_SHIFT),
                            reads=[("PS0", sc_off // 512)], writes=pbreg)
                if merged:
                    P.emit("scalar", lambda s, pb=pb: s.activation(
                        out=pb[:, 0:1024], in_=PS0[:, 1024:2048],
                        func=AF.Exp, scale=SCALE, bias=-B_SHIFT),
                        reads=[("PS0", 2), ("PS0", 3)], writes=pbreg)
                if cc * 128 >= q0:
                    d0 = cc * 128 - q0
                    P.emit("vector", lambda v, d0=d0, pb=pb: v.tensor_tensor(
                        out=pb[:, d0:d0 + 128], in0=pb[:, d0:d0 + 128], in1=C.mask128[:], op=ALU.mult),
                        reads=pbreg + [("mask",)], writes=pbreg)
            if cc >= 1:
                ccp = cc - 1
                pb, pbreg = C.probsT[ccp % 2], C.probsT_regions[ccp % 2]
                for (p0, p1) in pieces_of(ccp):
                    bank = (p0 - q0) // 512
                    last_c = min(nch - 1, (q0 + (bank + 1) * 512) // 128 - 1)
                    P.emit("tensor", lambda t, ccp=ccp, p0=p0, p1=p1, pb=pb, q0=q0, last_c=last_c: t.matmul(
                        PS0[:, p0 - q0:p1 - q0], C.V_sb[:, ccp * 128:(ccp + 1) * 128],
                        pb[:, p0 - q0:p1 - q0], start=(ccp == 0), stop=(ccp == last_c)),
                        reads=C.V_regions + pbreg, writes=[("PS0", bank)])
                    P.emit("tensor", lambda t, ccp=ccp, p0=p0, p1=p1, pb=pb, q0=q0, last_c=last_c:
                           t.matmul(PS1[0:1, (p0 - q0):(p1 - q0)],
                                    C.ones_c[:], pb[:, p0 - q0:p1 - q0],
                                    start=(ccp == 0), stop=(ccp == last_c)),
                           reads=[("ones",)] + pbreg, writes=[("PS1", (p0 - q0) // 512)])
        P.emit("scalar", lambda s, q0=q0, h=h, QB=QB: s.activation(
            out=C.apre(h, q0, q0 + QB), in_=PS0[:, 0:QB], func=AF.Identity),
            reads=ps0_banks(0, QB), writes=C.apre_regions(h))
        P.emit("scalar", lambda s, QB=QB: s.activation(
            out=C.denF[0:1, 0:QB], in_=PS1[0:1, 0:QB], func=AF.Identity),
            reads=[("PS1", 0), ("PS1", 1)], writes=C.den_regions)
        slot = (h * NQB + qb) % 3
        P.emit("vector", lambda v, QB=QB, slot=slot: _recip_lp(
            C.nc, v, C.rcp[slot][0:1, 0:QB], C.denF[0:1, 0:QB]),
            reads=C.den_regions, writes=C.rcp_regions[slot])


def _emit_headnorm(C, h):
    P, QB, NQB = C.P, C.QB, C.NQB
    PS0 = C.PS0
    for qb in range(NQB):
        q0 = qb * QB
        slot = (h * NQB + qb) % 3
        npiece = max(1, QB // 512)
        w = min(512, QB)
        for gpi in range(npiece):
            P.emit("tensor", lambda t, gpi=gpi, w=w, slot=slot: t.matmul(
                C.PS1[:, gpi * 512:gpi * 512 + w], C.ones_r[:],
                C.rcp[slot][0:1, gpi * 512:gpi * 512 + w], start=True, stop=True),
                reads=[("onesr",)] + C.rcp_regions[slot], writes=[("PS1", gpi)])
            rsl, rreg = C.rep_sb[gpi % 2], C.rep_regions[gpi % 2]
            P.emit("scalar", lambda s, rsl=rsl, w=w, gpi=gpi: s.activation(
                out=rsl[:, 0:w], in_=C.PS1[:, gpi * 512:gpi * 512 + w], func=AF.Identity),
                reads=[("PS1", gpi)], writes=rreg)
            P.emit("vector", lambda v, gpi=gpi, rsl=rsl, w=w, q0=q0, h=h: v.tensor_tensor(
                out=C.apre(h, q0 + gpi * 512, q0 + gpi * 512 + w),
                in0=C.apre(h, q0 + gpi * 512, q0 + gpi * 512 + w), in1=rsl[:, 0:w], op=ALU.mult),
                reads=C.apre_regions(h) + rreg, writes=C.apre_regions(h))


def _emit_wo(C):
    P, S, TC = C.P, C.S, C.TC
    PS0 = C.PS0
    apre_reads = [r for hh in range(NH) for r in C.apre_regions(hh)]
    for ns in range(4):
        wsl = ns % 2
        wot, wotreg = C.WoT[wsl], C.WoT_regions[wsl]
        wot3d = wot.rearrange("p (c u) -> p c u", c=16, u=512)
        for k in range(4):
            ssl = k % 2
            stg, streg = C.wstg[ssl], C.wstg_regions[ssl]
            P.emit("gpsimd", lambda g, ns=ns, k=k, stg=stg: g.dma_start(
                out=stg[:], in_=C.wo_d[ns * 512 + k * 128: ns * 512 + (k + 1) * 128, :]),
                writes=streg, dma_sem="st%d" % ssl)
            for g4 in range(4):
                pst, preg = (C.PSTa, ("PSTa",)) if g4 % 2 == 0 else (C.PSTb, ("PSTb",))

                def tf(t, g4=g4, pst=pst, stg=stg):
                    last = None
                    for j in range(4):
                        dc = g4 * 4 + j
                        last = t.transpose(pst[:, j * 128:(j + 1) * 128],
                                           stg[:, dc * 128:(dc + 1) * 128], C.ident[:])
                    return last

                P.emit("tensor", tf, reads=streg + [("ident",)], writes=[preg])
                eng = "vector" if g4 % 2 == 0 else "scalar"

                def evw(e, g4=g4, pst=pst, k=k, wot3d=wot3d):
                    dst = wot3d[:, g4 * 4:(g4 + 1) * 4, k * 128:(k + 1) * 128]
                    srcv = pst[:, 0:512].rearrange("p (c u) -> p c u", c=4, u=128)
                    if hasattr(e, "tensor_copy"):
                        return e.tensor_copy(out=dst, in_=srcv)
                    return e.copy(out=dst, in_=srcv)

                P.emit(eng, evw, reads=[preg], writes=wotreg)
        for tc in range(TC):
            bank = tc % 4

            def mmf(t, tc=tc, bank=bank, wot=wot):
                last = None
                for ic in range(DC):
                    last = t.matmul(PS0[:, bank * 512:(bank + 1) * 512],
                                    C.apre(ic, tc * 128, (tc + 1) * 128),
                                    wot[:, ic * 512:(ic + 1) * 512],
                                    start=(ic == 0), stop=(ic == DC - 1))
                return last

            P.emit("tensor", mmf, reads=wotreg + apre_reads, writes=[("PS0", bank)])
            P.emit("scalar", lambda s, tc=tc, ns=ns, bank=bank: s.activation(
                out=C.resv(tc, ns * 512, (ns + 1) * 512), in_=PS0[:, bank * 512:(bank + 1) * 512],
                func=AF.Identity),
                reads=[("PS0", bank)], writes=C.res_regions(tc))


def _emit_expert(C, half, e):
    P, S, TH, THC = C.P, C.S, C.TH, C.THC
    PS0 = C.PS0
    if e < 8:
        def wg_src(hc):
            return C.wg_d[e].rearrange("(dc p) h -> p dc h", p=128)[:, :, hc * 128:(hc + 1) * 128]

        def wu_src(hc):
            return C.wu_d[e].rearrange("(dc p) h -> p dc h", p=128)[:, :, hc * 128:(hc + 1) * 128]

        wd_base = C.wd_d[e]

        def scale_ap(tcg):
            return C.combine[:, tcg * 8 + e: tcg * 8 + e + 1]
        scale_reg = ("combine",)
    else:
        eh = e - 8

        def wg_src(hc, eh=eh):
            return C.swg_d.rearrange("(dc p) h -> p dc h", p=128)[:, :, eh * HM + hc * 128: eh * HM + (hc + 1) * 128]

        def wu_src(hc, eh=eh):
            return C.swu_d.rearrange("(dc p) h -> p dc h", p=128)[:, :, eh * HM + hc * 128: eh * HM + (hc + 1) * 128]

        wd_base = C.swd_d[eh * HM:(eh + 1) * HM, :]

        def scale_ap(tcg):
            return C.sgate[:, tcg:tcg + 1]
        scale_reg = ("sgate",)

    t0 = half * TH
    NPT = (TH + 511) // 512

    def emit_wd_load(q):
        area = q % 2
        first_bridge = C.__dict__.setdefault("_wd_bridged", False)

        def df(g, q=q, area=area, wd_base=wd_base):
            srcv = wd_base[:, q * 512:(q + 1) * 512]
            srcv = srcv.rearrange("(c p) d -> p c d", p=128)
            dst = C.wd_area_view(area).rearrange("p (c d) -> p c d", c=HMC)
            return g.dma_start(out=dst, in_=srcv)

        extra = [] if first_bridge else C.wd_pages
        C._wd_bridged = True
        P.emit("gpsimd", df, writes=C.wd_regions(area) + extra, dma_sem="wdp%d" % area)

    for hc in range(HMC):
        gs, gsr = C.wgslot[hc % 2], C.wgslot_regions[hc % 2]
        us, usr = C.wuslot[hc % 2], C.wuslot_regions[hc % 2]
        P.emit("gpsimd", lambda g, hc=hc, gs=gs: g.dma_start(
            out=gs.rearrange("p (dc h) -> p dc h", dc=16), in_=wg_src(hc)),
            writes=gsr, dma_sem="wg%d" % (hc % 2))
        P.emit("gpsimd", lambda g, hc=hc, us=us: g.dma_start(
            out=us.rearrange("p (dc h) -> p dc h", dc=16), in_=wu_src(hc)),
            writes=usr, dma_sem="wu%d" % (hc % 2))

        def gmm(t, gs=gs, t0=t0, NPT=NPT, TH=TH):
            last = None
            for dc in range(DC):
                for p in range(NPT):
                    n0 = t0 + p * 512
                    n1 = min(t0 + TH, n0 + 512)
                    last = t.matmul(PS0[:, p * 512: p * 512 + (n1 - n0)], gs[:, dc * 128:(dc + 1) * 128],
                                    C.xnt(dc, n0, n1), start=(dc == 0), stop=(dc == DC - 1))
            return last

        P.emit("tensor", gmm, reads=gsr + C.xnt_cols_regions(t0, t0 + TH), writes=ps0_banks(0, TH))

        def umm(t, us=us, t0=t0, NPT=NPT, TH=TH):
            last = None
            for dc in range(DC):
                for p in range(NPT):
                    n0 = t0 + p * 512
                    n1 = min(t0 + TH, n0 + 512)
                    last = t.matmul(PS0[:, 1024 + p * 512: 1024 + p * 512 + (n1 - n0)],
                                    us[:, dc * 128:(dc + 1) * 128],
                                    C.xnt(dc, n0, n1), start=(dc == 0), stop=(dc == DC - 1))
            return last

        P.emit("tensor", umm, reads=usr + C.xnt_cols_regions(t0, t0 + TH), writes=ps0_banks(1024, 1024 + TH))
        gso, gsor = C.gsil[hc % 2], C.gsil_regions[hc % 2]
        P.emit("scalar", lambda s, gso=gso, TH=TH: s.activation(out=gso[:, 0:TH], in_=PS0[:, 0:TH], func=AF.Silu),
               reads=ps0_banks(0, TH), writes=gsor)
        P.emit("vector", lambda v, hc=hc, gso=gso, TH=TH: v.tensor_tensor(
            out=C.hpr_view(hc, 0, TH), in0=gso[:, 0:TH], in1=PS0[:, 1024:1024 + TH], op=ALU.mult),
            reads=gsor + ps0_banks(1024, 1024 + TH), writes=C.hpr_regions(hc))
        if hc == 3:
            emit_wd_load(0)
        if hc == 6:
            emit_wd_load(1)

    hpr_reads = [r for hc in range(HMC) for r in C.hpr_regions(hc)]
    for q in range(4):
        area = q % 2
        if q >= 2:
            emit_wd_load(q)
        for tc in range(THC):
            tcg = half * THC + tc
            bank = tc % 4

            def dmm(t, tc=tc, bank=bank, area=area):
                last = None
                for hc in range(HMC):
                    last = t.matmul(PS0[:, bank * 512:(bank + 1) * 512],
                                    C.hpr_view(hc, tc * 128, (tc + 1) * 128),
                                    C.wd_view(area, hc, 0, 512),
                                    start=(hc == 0), stop=(hc == HMC - 1))
                return last

            P.emit("tensor", dmm, reads=C.wd_regions(area) + hpr_reads,
                   writes=[("PS0", bank)])
            gso, gsor = C.gsil[tc % 2], C.gsil_regions[tc % 2]
            P.emit("scalar", lambda s, bank=bank, tcg=tcg, gso=gso: s.activation(
                out=gso[:, 0:512], in_=PS0[:, bank * 512:(bank + 1) * 512], func=AF.Identity,
                scale=scale_ap(tcg)),
                reads=[("PS0", bank)] + [scale_reg], writes=gsor)
            P.emit("vector", lambda v, tcg=tcg, q=q, gso=gso: v.tensor_tensor(
                out=C.resv(tcg, q * 512, (q + 1) * 512),
                in0=C.resv(tcg, q * 512, (q + 1) * 512), in1=gso[:, 0:512], op=ALU.add),
                reads=C.res_regions(tcg) + gsor, writes=C.res_regions(tcg))


def _emit_final(C):
    P, TC = C.P, C.TC
    for tcg in range(TC):
        sl = tcg % 2
        xv = C.xtile[sl]
        xreg = C.xtile_regions[sl]
        ov = C.outstage[sl]
        oreg = C.outstage_regions[sl]
        P.emit("sync", lambda s, tcg=tcg, xv=xv: s.dma_start(
            out=xv[:], in_=C.x_d[tcg * 128:(tcg + 1) * 128, :]),
            writes=xreg, dma_sem="xt%d" % sl)
        extra = [("wda", 0), ("wda", 1)] if tcg == 0 else []
        P.emit("vector", lambda v, tcg=tcg, xv=xv, ov=ov: v.tensor_tensor(
            out=ov[:], in0=xv[:], in1=C.resv(tcg, 0, 2048), op=ALU.add),
            reads=xreg + C.res_regions(tcg) + extra, writes=oreg)
        P.emit("sync", lambda s, tcg=tcg, ov=ov: s.dma_start(
            out=C.out_d[tcg * 128:(tcg + 1) * 128, :], in_=ov[:]),
            reads=oreg, dma_sem="osem")
    osem_total = P.counts["osem"]
    handles = P.sem_handles
    P.raw("sync", lambda s: s.wait_ge(handles["osem"], osem_total))
    if P.counts["dbgsem"]:
        dbg_total = P.counts["dbgsem"]
        P.raw("gpsimd", lambda g: g.wait_ge(handles["dbgsem"], dbg_total))



_CACHE = {}


def _get_program(S):
    if S not in _CACHE:
        _CACHE[S] = build_program(S)
    return _CACHE[S]


def kernel(L0_attn_x, L0_attn_Wq, L0_attn_Wk, L0_attn_Wv, L0_attn_Wo,
           L0_ffn_gate_w, L0_ffn_w_gate, L0_ffn_w_up, L0_ffn_w_down,
           L0_ffn_shgate_w, L0_ffn_shw_gate, L0_ffn_shw_up, L0_ffn_shw_down):
    x = np.ascontiguousarray(np.asarray(L0_attn_x, dtype=np.float32))
    B = x.shape[0]
    weights = {
        "L0_attn_Wq": np.ascontiguousarray(np.asarray(L0_attn_Wq, np.float32)),
        "L0_attn_Wk": np.ascontiguousarray(np.asarray(L0_attn_Wk, np.float32)),
        "L0_attn_Wv": np.ascontiguousarray(np.asarray(L0_attn_Wv, np.float32)),
        "L0_attn_Wo": np.ascontiguousarray(np.asarray(L0_attn_Wo, np.float32)),
        "L0_ffn_gate_w": np.ascontiguousarray(np.asarray(L0_ffn_gate_w, np.float32)),
        "L0_ffn_w_gate": np.ascontiguousarray(np.asarray(L0_ffn_w_gate, np.float32)),
        "L0_ffn_w_up": np.ascontiguousarray(np.asarray(L0_ffn_w_up, np.float32)),
        "L0_ffn_w_down": np.ascontiguousarray(np.asarray(L0_ffn_w_down, np.float32)),
        "L0_ffn_shgate_w": np.ascontiguousarray(np.asarray(L0_ffn_shgate_w, np.float32)),
        "L0_ffn_shw_gate": np.ascontiguousarray(np.asarray(L0_ffn_shw_gate, np.float32)),
        "L0_ffn_shw_up": np.ascontiguousarray(np.asarray(L0_ffn_shw_up, np.float32)),
        "L0_ffn_shw_down": np.ascontiguousarray(np.asarray(L0_ffn_shw_down, np.float32)),
    }
    nc, _ = _get_program(x.shape[1])
    in_maps = [dict(weights, x=x[b]) for b in range(B)]
    res = run_bass_kernel_spmd(nc, in_maps, core_ids=list(range(B)))
    out = np.stack([np.asarray(r["out"], dtype=np.float32) for r in res.results], axis=0)
    return out


# version 10
# speedup vs baseline: 1.0470x; 1.0470x over previous
import math
from contextlib import ExitStack

import numpy as np

import concourse.bass as bass
import concourse.mybir as mybir
from concourse.bass_utils import run_bass_kernel_spmd

F32 = mybir.dt.float32
BF16 = mybir.dt.bfloat16
I32 = mybir.dt.int32
AF = mybir.ActivationFunctionType
ALU = mybir.AluOpType
AX = mybir.AxisListType

D = 2048
DC = 16
NH = 16
HM = 1408
HMC = 11
HS = 2816
NEXP = 10
B_SHIFT = 24.0
SCALE = 1.0 / math.sqrt(128.0)
LN_EPS = 1e-5
PIH = math.pi / 2.0

ENGINES = ("sync", "tensor", "vector", "scalar", "gpsimd")
ROPEB = [("ropebf", i) for i in range(4)]


class Prog:

    def __init__(self, nc):
        self.nc = nc
        self.thunks = {e: [] for e in ENGINES}
        self.sem_handles = {}
        self.counts = {}
        self.clocks = {e: {} for e in ENGINES}
        self.last_write = {}
        self.readers = {}
        self.stats = {e: 0 for e in ENGINES}

    def add_sem(self, name, handle):
        self.sem_handles[name] = handle
        self.counts[name] = 0

    def emit(self, eng, fn, reads=(), writes=(), dma_sem=None):
        need = {}

        def _need(tok):
            if tok is not None:
                s, v = tok
                if need.get(s, 0) < v:
                    need[s] = v

        if eng == "scalar":
            reads = list(reads) + [("cpool",)]
        for r in reads:
            _need(self.last_write.get(r))
        for r in writes:
            _need(self.last_write.get(r))
            for s, v in self.readers.get(r, {}).items():
                _need((s, v))
        clk = self.clocks[eng]
        waits = []
        for s, v in need.items():
            if clk.get(s, 0) < v:
                assert v <= self.counts[s], f"wait on future value {s}>={v}"
                waits.append((s, v))
                clk[s] = v
        if dma_sem is not None:
            semname, inc = dma_sem, 16
        else:
            semname, inc = "e_" + eng, 1
        self.counts[semname] += inc
        tok = (semname, self.counts[semname])
        for r in writes:
            self.last_write[r] = tok
            self.readers[r] = {}
        for r in reads:
            d = self.readers.setdefault(r, {})
            if d.get(semname, 0) < tok[1]:
                d[semname] = tok[1]
        semh = self.sem_handles[semname]
        handles = self.sem_handles
        self.stats[eng] += 1

        def thunk(engobj, waits=tuple(waits), fn=fn, semh=semh, inc=inc, handles=handles):
            for s, v in waits:
                engobj.wait_ge(handles[s], v)
            fn(engobj).then_inc(semh, inc)

        self.thunks[eng].append(thunk)
        return tok

    def raw(self, eng, fn):
        self.thunks[eng].append(fn)

    def set_write_token(self, region, tok):
        self.last_write[region] = tok
        self.readers[region] = {}


def bpages(name, c0, c1):
    return [(name, p) for p in range(c0 // 1024, (c1 + 1023) // 1024)]


def ps0_banks(c0, c1):
    return [("PS0", b) for b in range(c0 // 512, (c1 + 511) // 512)]


def _recip_lp(nc, v, out, in_):
    with nc.allow_low_precision(reason="bf16 softmax reciprocal used as matmul operand"):
        return v.reciprocal(out=out, in_=in_)


class Ctx:
    pass


def build_program(S=2048, debug=()):
    C = Ctx()
    C.S = S
    C.TC = S // 128
    C.QB = min(1024, S)
    C.NQB = S // C.QB
    C.TH = min(1024, S)
    C.NHALF = S // C.TH
    C.THC = C.TH // 128

    nc = bass.Bass("TRN2", target_bir_lowering=False, debug=False)
    C.nc = nc

    C.x_d = nc.dram_tensor("x", [S, D], F32, kind="ExternalInput").ap()
    C.wq_d = nc.dram_tensor("L0_attn_Wq", [D, D], F32, kind="ExternalInput").ap()
    C.wk_d = nc.dram_tensor("L0_attn_Wk", [D, D], F32, kind="ExternalInput").ap()
    C.wv_d = nc.dram_tensor("L0_attn_Wv", [D, D], F32, kind="ExternalInput").ap()
    C.wo_d = nc.dram_tensor("L0_attn_Wo", [D, D], F32, kind="ExternalInput").ap()
    C.gw_d = nc.dram_tensor("L0_ffn_gate_w", [8, D], F32, kind="ExternalInput").ap()
    C.wg_d = nc.dram_tensor("L0_ffn_w_gate", [8, D, HM], F32, kind="ExternalInput").ap()
    C.wu_d = nc.dram_tensor("L0_ffn_w_up", [8, D, HM], F32, kind="ExternalInput").ap()
    C.wd_d = nc.dram_tensor("L0_ffn_w_down", [8, HM, D], F32, kind="ExternalInput").ap()
    C.sgw_d = nc.dram_tensor("L0_ffn_shgate_w", [D], F32, kind="ExternalInput").ap()
    C.swg_d = nc.dram_tensor("L0_ffn_shw_gate", [D, HS], F32, kind="ExternalInput").ap()
    C.swu_d = nc.dram_tensor("L0_ffn_shw_up", [D, HS], F32, kind="ExternalInput").ap()
    C.swd_d = nc.dram_tensor("L0_ffn_shw_down", [HS, D], F32, kind="ExternalInput").ap()
    C.out_d = nc.dram_tensor("out", [S, D], F32, kind="ExternalOutput").ap()
    C.dbg = {}
    dbg_shapes = {
        "xnt": [128, 32768], "apre": [128, 32768], "resattn": [128, 32768],
        "res": [128, 32768], "combine": [128, C.TC * 8], "sgate": [128, C.TC],
        "kt": [128, 2048], "qt": [128, 2048], "vsb": [128, 2048],
        "cos": [128, 2048], "sin": [128, 2048], "hpr": [128, 11264],
    }
    debug = set(debug)
    if "kt" in debug:
        debug |= {"qt", "vsb"}
    if "cos" in debug:
        debug.add("sin")
    if "combine" in debug:
        debug.add("sgate")
    for name in sorted(debug):
        C.dbg[name] = nc.dram_tensor("dbg_" + name, dbg_shapes[name], F32, kind="ExternalOutput").ap()

    st = ExitStack()

    def sb(name, shape, dtype):
        return st.enter_context(nc.sbuf_tensor(name, shape, dtype))

    C.BIG1 = sb("BIG1", [128, 32768], BF16)
    C.BIG2 = sb("BIG2", [128, 32768], BF16)
    C.BIG3 = sb("BIG3", [128, 32768], BF16)
    C.cosT = sb("cosT", [128, 2048], BF16)
    C.sinS = sb("sinS", [128, 2048], BF16)
    C.ident = sb("ident", [128, 128], BF16)
    C.mask128 = sb("mask128", [128, 128], BF16)
    C.ones_c = sb("ones_c", [128, 1], BF16)
    C.ones_r = sb("ones_r", [1, 128], BF16)
    C.cpool = sb("cpool", [128, 4], F32)
    C.invf = sb("invf", [64, 1], F32)
    C.gstage = sb("gstage", [9, 2048], BF16)
    C.gate_wT = sb("gate_wT", [128, DC * 9], BF16)
    C.combine = sb("combine", [128, C.TC * 8], F32)
    C.sgate = sb("sgate", [128, C.TC], F32)
    C.sglog = sb("sglog", [128, C.TC], F32)
    C.rt = sb("rt", [128, 64], F32)
    C.lgb = sb("lgb", [128, 144], F32)
    C.rotM = sb("rotM", [128, 128], BF16)
    C.lnst = sb("lnst", [128, 96], F32)

    C.PS0 = st.enter_context(nc.psum_tensor("PS0", [128, 2048], F32))
    C.PS1 = st.enter_context(nc.psum_tensor("PS1", [128, 1024], F32))
    C.PSTa = st.enter_context(nc.psum_tensor("PSTa", [128, 1024], BF16))
    C.PSTb = st.enter_context(nc.psum_tensor("PSTb", [128, 1024], BF16))

    semnames = (["e_" + e for e in ENGINES]
                + ["xt0", "xt1", "st0", "st1", "wg0", "wg1", "wu0", "wu1",
                   "wdp0", "wdp1", "gsem", "mrsem", "osem", "dbgsem"])
    P = Prog(nc)
    C.P = P
    for sn in semnames:
        P.add_sem(sn, st.enter_context(nc.semaphore(sn)))

    nc.const_aps.register(F32, float(-B_SHIFT), C.cpool[:, 0:1])
    nc.const_aps.register(F32, float(LN_EPS), C.cpool[:, 1:2])
    nc.const_aps.register(F32, float(PIH), C.cpool[:, 2:3])
    nc.const_aps.register(F32, 0.0, C.cpool[:, 3:4])

    C.cosF = C.BIG2[:, 0:4096].bitcast(F32)
    C.sinF = C.BIG2[:, 4096:8192].bitcast(F32)
    C.xtile = [C.BIG2[:, 8192 + k * 4096: 12288 + k * 4096].bitcast(F32) for k in range(2)]
    C.xtile_regions = [bpages("BIG2", 8192 + k * 4096, 12288 + k * 4096) for k in range(2)]
    C.xnsc = [C.BIG2[:, 16384 + k * 2048: 18432 + k * 2048] for k in range(2)]
    C.xnsc_regions = [bpages("BIG2", 16384 + k * 2048, 18432 + k * 2048) for k in range(2)]
    C.lndump = C.BIG2[:, 20480:22528]
    C.io_j = C.BIG2[:, 22528:22784].bitcast(I32)
    C.io_i = C.BIG2[:, 22784:23040].bitcast(I32)
    C.outstage = [C.BIG2[:, 23040 + k * 4096: 27136 + k * 4096].bitcast(F32) for k in range(2)]
    C.outstage_regions = [bpages("BIG2", 23040 + k * 4096, 27136 + k * 4096) for k in range(2)]

    C.WT = {"k": C.BIG3[:, 0:2048], "q": C.BIG3[:, 2048:4096], "v": C.BIG3[:, 4096:6144]}
    C.WT_regions = {"k": bpages("BIG3", 0, 2048), "q": bpages("BIG3", 2048, 4096),
                    "v": bpages("BIG3", 4096, 6144)}
    C.stg = [C.BIG3[:, 6144:8192], C.BIG3[:, 8192:10240]]
    C.stg_regions = [bpages("BIG3", 6144, 8192), bpages("BIG3", 8192, 10240)]
    C.k_raw = C.BIG3[:, 10240:12288]
    C.k_raw_regions = bpages("BIG3", 10240, 12288)
    C.t1b = C.BIG3[:, 12288:14336]
    C.t1b_regions = bpages("BIG3", 12288, 14336)
    C.vT_sb = C.BIG3[:, 14336:16384]
    C.vT_regions = bpages("BIG3", 14336, 16384)
    C.kT = C.BIG3[:, 16384:18432]
    C.kT_regions = bpages("BIG3", 16384, 18432)
    C.qT = C.BIG3[:, 18432:20480]
    C.qT_regions = bpages("BIG3", 18432, 20480)
    C.V_sb = C.BIG3[:, 20480:22528]
    C.V_regions = bpages("BIG3", 20480, 22528)
    C.probsT = [C.BIG3[:, 22528:23552], C.BIG3[:, 23552:24576]]
    C.probsT_regions = [bpages("BIG3", 22528, 23552), bpages("BIG3", 23552, 24576)]
    C.denF = C.BIG3[:, 24576:26624].bitcast(F32)
    C.den_regions = bpages("BIG3", 24576, 26624)
    C.rcp = [C.BIG3[:, 26624:27648], C.BIG3[:, 29696:30720], C.BIG3[:, 30720:31744]]
    C.rcp_regions = [bpages("BIG3", 26624, 27648), bpages("BIG3", 29696, 30720),
                     bpages("BIG3", 30720, 31744)]
    C.rep_sb = [C.BIG3[:, 27648:28672], C.BIG3[:, 28672:29696]]
    C.rep_regions = [bpages("BIG3", 27648, 28672), bpages("BIG3", 28672, 29696)]

    C.WoT = [C.BIG1[:, 0:8192], C.BIG1[:, 8192:16384]]
    C.WoT_regions = [bpages("BIG1", 0, 8192), bpages("BIG1", 8192, 16384)]
    C.wstg = [C.BIG1[:, 16384:18432], C.BIG1[:, 18432:20480]]
    C.wstg_regions = [bpages("BIG1", 16384, 18432), bpages("BIG1", 18432, 20480)]

    C.wgslot = [C.BIG2[:, 11264:13312], C.BIG2[:, 13312:15360]]
    C.wgslot_regions = [bpages("BIG2", 11264, 13312), bpages("BIG2", 13312, 15360)]
    C.wuslot = [C.BIG2[:, 15360:17408], C.BIG2[:, 17408:19456]]
    C.wuslot_regions = [bpages("BIG2", 15360, 17408), bpages("BIG2", 17408, 19456)]
    C.gsil = [C.BIG2[:, 19456:20480], C.BIG2[:, 20480:21504]]
    C.gsil_regions = [bpages("BIG2", 19456, 20480), bpages("BIG2", 20480, 21504)]

    def hpr_view(hc, c0, c1):
        return C.BIG2[:, hc * 1024 + c0: hc * 1024 + c1]

    def hpr_regions(hc):
        return bpages("BIG2", hc * 1024, (hc + 1) * 1024)

    def wd_view(area, hc, c0, c1):
        base = 21504 + area * 5632 + hc * 512
        return C.BIG2[:, base + c0: base + c1]

    def wd_area_view(area):
        return C.BIG2[:, 21504 + area * 5632: 21504 + area * 5632 + 5632]

    C.wd_area_view = wd_area_view
    C.wd_pages = bpages("BIG2", 21504, 32768)

    def wd_regions(area):
        return [("wda", area)]

    def xnt(dc, t0, t1):
        return C.BIG1[:, dc * 2048 + t0: dc * 2048 + t1]

    def xnt_regions(dc):
        return bpages("BIG1", dc * 2048, dc * 2048 + S)

    def apre(h, t0, t1):
        return C.BIG2[:, h * 2048 + t0: h * 2048 + t1]

    def apre_regions(h):
        return bpages("BIG2", h * 2048, h * 2048 + S)

    def resv(tcg, c0, c1):
        return C.BIG3[:, tcg * 2048 + c0: tcg * 2048 + c1]

    def res_regions(tcg):
        return bpages("BIG3", tcg * 2048, (tcg + 1) * 2048)

    def xnt_cols_regions(t0, t1):
        out = []
        for dc in range(DC):
            out += bpages("BIG1", dc * 2048 + t0, dc * 2048 + t1)
        return out

    C.xnt_cols_regions = xnt_cols_regions
    C.hpr_view, C.hpr_regions = hpr_view, hpr_regions
    C.wd_view, C.wd_regions = wd_view, wd_regions
    C.xnt, C.xnt_regions = xnt, xnt_regions
    C.apre, C.apre_regions = apre, apre_regions
    C.resv, C.res_regions = resv, res_regions
    C.XNT_ALL = [r for dc in range(DC) for r in xnt_regions(dc)]

    block = st.enter_context(nc.Block())
    with st:
        _emit_all(C)
        for e in ENGINES:
            def runner(engobj, thunks=P.thunks[e]):
                for t in thunks:
                    t(engobj)
            getattr(block, e)(runner)
    return nc, C



def _emit_all(C):
    P = C.P
    _emit_consts(C)
    _load_wslice(C, 0, "k", C.wk_d, 0)
    _load_wslice(C, 0, "q", C.wq_d, 1)
    _load_wslice(C, 0, "v", C.wv_d, 0)
    _emit_ln(C, first=True)
    for tc in range(C.TC):
        _emit_routing_mm(C, tc)
    for tc in range(C.TC):
        _emit_routing_tc(C, tc)
    _emit_routing_final(C)
    if "xnt" in C.dbg:
        P.emit("gpsimd", lambda s: s.dma_start(
            out=C.dbg["xnt"].rearrange("p (c q) -> p c q", c=16, q=2048)[:, :, 0:C.S],
            in_=C.BIG1[:, :].rearrange("p (c q) -> p c q", c=16, q=2048)[:, :, 0:C.S]),
               reads=C.XNT_ALL + [("dbgser",)], writes=[("dbgser",)], dma_sem="dbgsem")
    if "combine" in C.dbg:
        P.emit("gpsimd", lambda s: s.dma_start(out=C.dbg["combine"], in_=C.combine[:]),
               reads=[("combine",)] + [("dbgser",)], writes=[("dbgser",)], dma_sem="dbgsem")
        P.emit("gpsimd", lambda s: s.dma_start(out=C.dbg["sgate"], in_=C.sgate[:]),
               reads=[("sgate",)] + [("dbgser",)], writes=[("dbgser",)], dma_sem="dbgsem")
    for h in range(NH):
        _emit_head(C, h)
    _emit_headnorm(C, NH - 1)

    if "apre" in C.dbg:
        P.emit("gpsimd", lambda s: s.dma_start(
            out=C.dbg["apre"].rearrange("p (c q) -> p c q", c=16, q=2048)[:, :, 0:C.S],
            in_=C.BIG2[:, :].rearrange("p (c q) -> p c q", c=16, q=2048)[:, :, 0:C.S]),
               reads=[r for hh in range(NH) for r in C.apre_regions(hh)] + [("dbgser",)], writes=[("dbgser",)], dma_sem="dbgsem")
    if "kt" in C.dbg:
        P.emit("gpsimd", lambda s: s.dma_start(out=C.dbg["kt"][:, 0:C.S], in_=C.kT[:, 0:C.S]), reads=C.kT_regions + [("dbgser",)], writes=[("dbgser",)], dma_sem="dbgsem")
        P.emit("gpsimd", lambda s: s.dma_start(out=C.dbg["qt"][:, 0:C.S], in_=C.qT[:, 0:C.S]), reads=C.qT_regions + [("dbgser",)], writes=[("dbgser",)], dma_sem="dbgsem")
        P.emit("gpsimd", lambda s: s.dma_start(out=C.dbg["vsb"][:, 0:C.S], in_=C.V_sb[:, 0:C.S]), reads=C.V_regions + [("dbgser",)], writes=[("dbgser",)], dma_sem="dbgsem")
    _emit_wo(C)
    if "resattn" in C.dbg:
        P.emit("gpsimd", lambda s: s.dma_start(out=C.dbg["resattn"][:, 0:C.TC * 2048], in_=C.BIG3[:, 0:C.TC * 2048]),
               reads=[r for t in range(C.TC) for r in C.res_regions(t)] + [("dbgser",)], writes=[("dbgser",)], dma_sem="dbgsem")
    _emit_ln(C, first=False)
    for half in range(C.NHALF):
        for e in range(NEXP):
            _emit_expert(C, half, e)
    if "hpr" in C.dbg:
        P.emit("gpsimd", lambda s: s.dma_start(out=C.dbg["hpr"], in_=C.BIG2[:, 0:11264]),
               reads=[r for hc in range(HMC) for r in C.hpr_regions(hc)] + [("dbgser",)], writes=[("dbgser",)], dma_sem="dbgsem")
    if "res" in C.dbg:
        P.emit("gpsimd", lambda s: s.dma_start(out=C.dbg["res"][:, 0:C.TC * 2048], in_=C.BIG3[:, 0:C.TC * 2048]),
               reads=[r for t in range(C.TC) for r in C.res_regions(t)] + [("dbgser",)], writes=[("dbgser",)], dma_sem="dbgsem")
    _emit_final(C)


def _emit_consts(C):
    P, nc = C.P, C.nc
    P.emit("gpsimd", lambda g: g.iota(C.io_j[:], pattern=[[1, 128]], base=0, channel_multiplier=0),
           writes=[("io", "j")])
    P.emit("gpsimd", lambda g: g.iota(C.io_i[:], pattern=[[0, 128]], base=0, channel_multiplier=1),
           writes=[("io", "i")])
    P.emit("gpsimd", lambda g: g.memset(C.ones_c[:], 1.0), writes=[("ones",)])
    P.emit("gpsimd", lambda g: g.memset(C.ones_r[:], 1.0), writes=[("onesr",)])
    for col, val in ((0, -B_SHIFT), (1, LN_EPS), (2, PIH), (3, 0.0)):
        P.emit("vector", lambda v, col=col, val=val: v.memset(C.cpool[:, col:col + 1], float(val)),
               writes=[("cpool",)])
    P.emit("vector", lambda v: v.tensor_tensor(out=C.ident[:], in0=C.io_j[:], in1=C.io_i[:], op=ALU.is_equal),
           reads=[("io", "j"), ("io", "i")], writes=[("ident",)])
    P.emit("vector", lambda v: v.tensor_tensor(out=C.mask128[:], in0=C.io_j[:], in1=C.io_i[:], op=ALU.is_ge),
           reads=[("io", "j"), ("io", "i")], writes=[("mask",)])

    S = C.S
    cosF, sinF = C.cosF, C.sinF
    CFR, SFR = [("cosF",)], [("sinF",)]
    xt0 = C.xtile[0]
    xt1 = C.xtile[1]
    tAr = C.xtile_regions[0] + C.xtile_regions[1]
    P.emit("scalar", lambda s: s.activation(out=C.invf[:], in_=C.io_i[0:64, 0:1], func=AF.Exp,
                                            scale=-math.log(10000.0) / 64.0),
           reads=[("io", "i")], writes=[("invf",)])
    P.emit("vector", lambda v: v.memset(cosF[0:64, 0:1], 1.0), writes=CFR)
    P.emit("vector", lambda v: v.memset(sinF[0:64, 0:1], 0.0), writes=SFR)
    P.emit("scalar", lambda s: s.activation(out=cosF[0:64, 1:2], in_=C.invf[:], func=AF.Sin, bias=PIH),
           reads=[("invf",), ("cpool",)], writes=CFR)
    P.emit("scalar", lambda s: s.activation(out=sinF[0:64, 1:2], in_=C.invf[:], func=AF.Sin),
           reads=[("invf",)], writes=SFR)
    L = 1
    while L < S - 1:
        n = min(L, S - 1 - L)
        P.emit("vector", lambda v, n=n, L=L: v.tensor_scalar(
            out=xt0[0:64, 0:n], in0=cosF[0:64, 1:1 + n], scalar1=cosF[0:64, L:L + 1], scalar2=None, op0=ALU.mult),
            reads=CFR, writes=tAr)
        P.emit("vector", lambda v, n=n, L=L: v.tensor_scalar(
            out=xt1[0:64, 0:n], in0=sinF[0:64, 1:1 + n], scalar1=sinF[0:64, L:L + 1], scalar2=None, op0=ALU.mult),
            reads=SFR + CFR, writes=tAr)
        P.emit("vector", lambda v, n=n, L=L: v.tensor_tensor(
            out=cosF[0:64, L + 1:L + 1 + n], in0=xt0[0:64, 0:n], in1=xt1[0:64, 0:n], op=ALU.subtract),
            reads=tAr, writes=CFR)
        P.emit("vector", lambda v, n=n, L=L: v.tensor_scalar(
            out=xt0[0:64, 0:n], in0=sinF[0:64, 1:1 + n], scalar1=cosF[0:64, L:L + 1], scalar2=None, op0=ALU.mult),
            reads=SFR + CFR, writes=tAr)
        P.emit("vector", lambda v, n=n, L=L: v.tensor_scalar(
            out=xt1[0:64, 0:n], in0=cosF[0:64, 1:1 + n], scalar1=sinF[0:64, L:L + 1], scalar2=None, op0=ALU.mult),
            reads=SFR + CFR, writes=tAr)
        P.emit("vector", lambda v, n=n, L=L: v.tensor_tensor(
            out=sinF[0:64, L + 1:L + 1 + n], in0=xt0[0:64, 0:n], in1=xt1[0:64, 0:n], op=ALU.add),
            reads=tAr, writes=SFR)
        L += n
    P.emit("gpsimd", lambda g: g.dma_start(out=C.cosT[0:64, 0:S], in_=cosF[0:64, 0:S]),
           reads=CFR, writes=[("ropebf", 0)], dma_sem="mrsem")
    P.emit("gpsimd", lambda g: g.dma_start(out=C.cosT[64:128, 0:S], in_=cosF[0:64, 0:S]),
           reads=CFR, writes=[("ropebf", 1)], dma_sem="mrsem")
    P.emit("gpsimd", lambda g: g.dma_start(out=C.sinS[0:64, 0:S], in_=sinF[0:64, 0:S]),
           reads=SFR, writes=[("ropebf", 2)], dma_sem="mrsem")
    P.emit("gpsimd", lambda g: g.dma_start(out=C.sinS[64:128, 0:S], in_=sinF[0:64, 0:S]),
           reads=SFR, writes=[("ropebf", 3)], dma_sem="mrsem")
    P.emit("vector", lambda v: v.tensor_scalar(out=xt0[:, 0:128], in0=C.io_j[:], scalar1=64.0,
                                               scalar2=None, op0=ALU.add), reads=[("io", "j")], writes=tAr)
    P.emit("vector", lambda v: v.tensor_tensor(out=xt0[:, 128:256], in0=C.io_i[:], in1=xt0[:, 0:128],
                                               op=ALU.is_equal), reads=[("io", "i")] + tAr, writes=tAr)
    P.emit("vector", lambda v: v.tensor_scalar(out=xt0[:, 256:384], in0=C.io_j[:], scalar1=-64.0,
                                               scalar2=None, op0=ALU.add), reads=[("io", "j")], writes=tAr)
    P.emit("vector", lambda v: v.tensor_tensor(out=xt0[:, 384:512], in0=C.io_i[:], in1=xt0[:, 256:384],
                                               op=ALU.is_equal), reads=[("io", "i")] + tAr, writes=tAr)
    P.emit("vector", lambda v: v.tensor_tensor(out=C.rotM[:], in0=xt0[:, 384:512], in1=xt0[:, 128:256],
                                               op=ALU.subtract), reads=tAr, writes=[("rotM",)])
    if "cos" in C.dbg:
        P.emit("gpsimd", lambda s: s.dma_start(out=C.dbg["cos"][:, 0:C.S], in_=C.cosT[:, 0:C.S]), reads=ROPEB + [("dbgser",)], writes=[("dbgser",)], dma_sem="dbgsem")
        P.emit("gpsimd", lambda s: s.dma_start(out=C.dbg["sin"][:, 0:C.S], in_=C.sinS[:, 0:C.S]), reads=ROPEB + [("dbgser",)], writes=[("dbgser",)], dma_sem="dbgsem")

    P.emit("gpsimd", lambda g: g.dma_start(out=C.gstage[0:8, :], in_=C.gw_d),
           writes=[("gstage", 0)], dma_sem="gsem")
    P.emit("gpsimd", lambda g: g.dma_start(out=C.gstage[8:9, :], in_=C.sgw_d.rearrange("(o d) -> o d", o=1)),
           writes=[("gstage", 1)], dma_sem="gsem")
    for dc in range(DC):
        pst, preg = (C.PSTa, ("PSTa",)) if dc % 2 == 0 else (C.PSTb, ("PSTb",))
        P.emit("tensor", lambda t, dc=dc, pst=pst: t.transpose(
            pst[:, 0:9], C.gstage[0:9, dc * 128:(dc + 1) * 128], C.ident[0:9, 0:9]),
            reads=[("gstage", 0), ("gstage", 1), ("ident",)], writes=[preg])
        P.emit("vector", lambda v, dc=dc, pst=pst: v.tensor_copy(
            out=C.gate_wT[:, dc * 9:(dc + 1) * 9], in_=pst[:, 0:9]),
            reads=[preg], writes=[("gate_wT",)])


def _emit_ln(C, first=True, interleave=None):
    P, S, TC = C.P, C.S, C.TC
    lnst = C.lnst
    big1_3d = C.BIG1[:, :].rearrange("p (c q) -> p c q", c=16, q=2048)
    for tc in range(TC):
        sl = tc % 2
        xv = C.xtile[sl]
        xreg = C.xtile_regions[sl]
        c = sl * 32
        P.emit("sync", lambda s, tc=tc, xv=xv: s.dma_start(out=xv[:], in_=C.x_d[tc * 128:(tc + 1) * 128, :]),
               writes=xreg, dma_sem="xt%d" % sl)
        if first:
            for bi in range(4):
                P.emit("vector", lambda v, xv=xv, c=c, bi=bi: v.bn_stats(
                    out=lnst[:, c + bi * 6: c + (bi + 1) * 6], in_=xv[:, bi * 512:(bi + 1) * 512]),
                    reads=xreg, writes=[("lnst", sl, bi)])
            P.emit("vector", lambda v, c=c: v.bn_aggr(out=lnst[:, c + 24:c + 26], in_=lnst[:, c:c + 24]),
                   reads=[("lnst", sl, bi) for bi in range(4)], writes=[("lnst", sl, 4)])
            P.emit("scalar", lambda s, c=c: s.activation(out=lnst[:, c + 26:c + 27], in_=lnst[:, c + 25:c + 26],
                                                         func=AF.Ln, bias=LN_EPS),
                   reads=[("lnst", sl, 4)], writes=[("lnst", sl, 5)])
            P.emit("scalar", lambda s, c=c, tc=tc: s.activation(
                out=lnst[:, 64 + tc * 2:64 + tc * 2 + 1], in_=lnst[:, c + 26:c + 27],
                func=AF.Exp, scale=-0.5),
                reads=[("lnst", sl, 5)], writes=[("lnstc", tc, 0)])
            P.emit("vector", lambda v, c=c, tc=tc: v.tensor_tensor(
                out=lnst[:, 64 + tc * 2 + 1:64 + tc * 2 + 2], in0=lnst[:, c + 24:c + 25],
                in1=lnst[:, 64 + tc * 2:64 + tc * 2 + 1], op=ALU.mult),
                reads=[("lnst", sl, 4), ("lnstc", tc, 0)], writes=[("lnstc", tc, 1)])
            P.emit("vector", lambda v, tc=tc: v.tensor_scalar(
                out=lnst[:, 64 + tc * 2 + 1:64 + tc * 2 + 2],
                in0=lnst[:, 64 + tc * 2 + 1:64 + tc * 2 + 2],
                scalar1=-1.0, scalar2=None, op0=ALU.mult),
                reads=[("lnstc", tc, 1)], writes=[("lnstc", tc, 1)])
        xnv = C.xnsc[sl]
        xnreg = C.xnsc_regions[sl]
        P.emit("scalar", lambda s, xv=xv, xnv=xnv, tc=tc: s.activation(
            out=xnv[:], in_=xv[:], func=AF.Identity,
            scale=lnst[:, 64 + tc * 2:64 + tc * 2 + 1],
            bias=lnst[:, 64 + tc * 2 + 1:64 + tc * 2 + 2]),
            reads=xreg + [("lnstc", tc, 0), ("lnstc", tc, 1)], writes=xnreg)
        for g4 in range(4):
            pst, preg = (C.PSTa, ("PSTa",)) if g4 % 2 == 0 else (C.PSTb, ("PSTb",))

            def tf(t, g4=g4, pst=pst, xnv=xnv):
                last = None
                for j in range(4):
                    dc = g4 * 4 + j
                    last = t.transpose(pst[:, j * 128:(j + 1) * 128],
                                       xnv[:, dc * 128:(dc + 1) * 128], C.ident[:])
                return last

            P.emit("tensor", tf, reads=xnreg + [("ident",)], writes=[preg])
            eng = "vector" if g4 % 2 == 0 else "scalar"

            def evf(e, g4=g4, pst=pst, tc=tc):
                dst = big1_3d[:, g4 * 4:(g4 + 1) * 4, tc * 128:(tc + 1) * 128]
                srcv = pst[:, 0:512].rearrange("p (c u) -> p c u", c=4, u=128)
                if hasattr(e, "tensor_copy"):
                    return e.tensor_copy(out=dst, in_=srcv)
                return e.copy(out=dst, in_=srcv)

            P.emit(eng, evf,
                   reads=[preg], writes=[r for j in range(4) for r in C.xnt_regions(g4 * 4 + j)])
        if interleave is not None:
            interleave(tc)


def _emit_routing_mm(C, tc):
    P = C.P
    PS1 = C.PS1
    xn_reads = C.xnt_cols_regions(tc * 128, (tc + 1) * 128)
    ro = (tc % 2) * 512

    def mmf(t, tc=tc, ro=ro):
        last = None
        for dc in range(DC):
            last = t.matmul(PS1[:, ro:ro + 9], C.xnt(dc, tc * 128, (tc + 1) * 128),
                            C.gate_wT[:, dc * 9:(dc + 1) * 9],
                            start=(dc == 0), stop=(dc == DC - 1))
        return last

    P.emit("tensor", mmf, reads=[("gate_wT",)] + xn_reads, writes=[("PS1", tc % 2)])
    P.emit("scalar", lambda s, ro=ro, tc=tc: s.activation(out=C.lgb[:, tc * 9:tc * 9 + 9],
                                                          in_=PS1[:, ro:ro + 9], func=AF.Identity),
           reads=[("PS1", tc % 2)], writes=[("lgb", tc)])


def _emit_routing_tc(C, tc):
    P = C.P
    rt = C.rt
    R = [("rt",)]
    if True:
        P.emit("vector", lambda v, tc=tc: v.tensor_copy(out=rt[:, 0:9], in_=C.lgb[:, tc * 9:tc * 9 + 9]),
               reads=[("lgb", tc)], writes=R)
        P.emit("vector", lambda v: v.tensor_reduce(out=rt[:, 16:17], in_=rt[:, 0:8], axis=AX.X, op=ALU.max),
               reads=R, writes=R)
        P.emit("vector", lambda v: v.tensor_scalar(out=rt[:, 17:18], in0=rt[:, 16:17], scalar1=-1.0,
                                                   scalar2=None, op0=ALU.mult), reads=R, writes=R)
        P.emit("scalar", lambda s: s.activation(out=rt[:, 24:32], in_=rt[:, 0:8], func=AF.Exp,
                                                bias=rt[:, 17:18]), reads=R, writes=R)
        P.emit("vector", lambda v: v.tensor_reduce(out=rt[:, 32:33], in_=rt[:, 24:32], axis=AX.X, op=ALU.max),
               reads=R, writes=R)
        P.emit("vector", lambda v: v.tensor_scalar(out=rt[:, 33:41], in0=rt[:, 24:32], scalar1=rt[:, 32:33],
                                                   scalar2=None, op0=ALU.is_ge), reads=R, writes=R)
        P.emit("vector", lambda v: v.tensor_reduce(out=rt[:, 41:42], in_=rt[:, 33:41], axis=AX.X, op=ALU.add),
               reads=R, writes=R)
        P.emit("vector", lambda v: v.tensor_scalar(out=rt[:, 42:50], in0=rt[:, 33:41], scalar1=2.0,
                                                   scalar2=None, op0=ALU.mult), reads=R, writes=R)
        P.emit("vector", lambda v: v.tensor_tensor(out=rt[:, 42:50], in0=rt[:, 24:32], in1=rt[:, 42:50],
                                                   op=ALU.subtract), reads=R, writes=R)
        P.emit("vector", lambda v: v.tensor_reduce(out=rt[:, 50:51], in_=rt[:, 42:50], axis=AX.X, op=ALU.max),
               reads=R, writes=R)
        P.emit("vector", lambda v: v.tensor_scalar(out=rt[:, 51:52], in0=rt[:, 41:42], scalar1=2.0,
                                                   scalar2=None, op0=ALU.is_ge), reads=R, writes=R)
        P.emit("vector", lambda v: v.tensor_tensor(out=rt[:, 52:53], in0=rt[:, 32:33], in1=rt[:, 50:51],
                                                   op=ALU.subtract), reads=R, writes=R)
        P.emit("vector", lambda v: v.tensor_tensor(out=rt[:, 52:53], in0=rt[:, 51:52], in1=rt[:, 52:53],
                                                   op=ALU.mult), reads=R, writes=R)
        P.emit("vector", lambda v: v.tensor_tensor(out=rt[:, 53:54], in0=rt[:, 50:51], in1=rt[:, 52:53],
                                                   op=ALU.add), reads=R, writes=R)
        P.emit("vector", lambda v: v.tensor_scalar(out=rt[:, 54:62], in0=rt[:, 24:32], scalar1=rt[:, 53:54],
                                                   scalar2=None, op0=ALU.is_ge), reads=R, writes=R)
        P.emit("vector", lambda v: v.tensor_tensor(out=rt[:, 54:62], in0=rt[:, 24:32], in1=rt[:, 54:62],
                                                   op=ALU.mult), reads=R, writes=R)
        P.emit("vector", lambda v: v.tensor_tensor(out=rt[:, 62:63], in0=rt[:, 32:33], in1=rt[:, 53:54],
                                                   op=ALU.add), reads=R, writes=R)
        P.emit("vector", lambda v: v.reciprocal(out=rt[:, 63:64], in_=rt[:, 62:63]), reads=R, writes=R)
        P.emit("vector", lambda v, tc=tc: v.tensor_scalar(out=C.combine[:, tc * 8:(tc + 1) * 8],
                                                          in0=rt[:, 54:62], scalar1=rt[:, 63:64],
                                                          scalar2=None, op0=ALU.mult),
               reads=R, writes=[("combine",)])
        P.emit("vector", lambda v, tc=tc: v.tensor_copy(out=C.sglog[:, tc:tc + 1], in_=rt[:, 8:9]),
               reads=R, writes=[("sglog",)])


def _emit_routing_final(C):
    C.P.emit("scalar", lambda s: s.activation(out=C.sgate[:], in_=C.sglog[:], func=AF.Sigmoid),
             reads=[("sglog",)], writes=[("sgate",)])


def _load_wslice(C, h, which, src, slot):
    P = C.P
    stg, streg = C.stg[slot], C.stg_regions[slot]
    P.emit("gpsimd", lambda g, src=src, stg=stg, h=h: g.dma_start(
        out=stg[:], in_=src[h * 128:(h + 1) * 128, :]),
        writes=streg, dma_sem="st%d" % slot)
    wt, wtreg = C.WT[which], C.WT_regions[which]
    for g4 in range(4):
        pst, preg = (C.PSTa, ("PSTa",)) if g4 % 2 == 0 else (C.PSTb, ("PSTb",))

        def tf(t, g4=g4, pst=pst, stg=stg):
            last = None
            for j in range(4):
                dc = g4 * 4 + j
                last = t.transpose(pst[:, j * 128:(j + 1) * 128],
                                   stg[:, dc * 128:(dc + 1) * 128], C.ident[:])
            return last

        P.emit("tensor", tf, reads=streg + [("ident",)], writes=[preg])
        if g4 % 2 == 0:
            P.emit("vector", lambda v, g4=g4, pst=pst, wt=wt: v.tensor_copy(
                out=wt[:, g4 * 512:(g4 + 1) * 512], in_=pst[:, 0:512]),
                reads=[preg], writes=wtreg)
        else:
            P.emit("scalar", lambda s, g4=g4, pst=pst, wt=wt: s.copy(
                out=wt[:, g4 * 512:(g4 + 1) * 512], in_=pst[:, 0:512]),
                reads=[preg], writes=wtreg)


def _emit_head(C, h):
    P, S, QB, NQB = C.P, C.S, C.QB, C.NQB
    PS0, PS1 = C.PS0, C.PS1
    NP = (S + 511) // 512

    def load_wslice(which, src, slot):
        _load_wslice(C, h, which, src, slot)

    def proj(which, out_buf, out_regions, rope):
        wt, wtreg = C.WT[which], C.WT_regions[which]
        for p in range(NP):
            n0, n1 = p * 512, min(S, (p + 1) * 512)

            def mmf(t, n0=n0, n1=n1, wt=wt):
                last = None
                for dc in range(DC):
                    last = t.matmul(PS0[:, n0:n1], wt[:, dc * 128:(dc + 1) * 128],
                                    C.xnt(dc, n0, n1), start=(dc == 0), stop=(dc == DC - 1))
                return last

            P.emit("tensor", mmf, reads=wtreg + C.xnt_cols_regions(n0, n1), writes=ps0_banks(n0, n1))
            P.emit("scalar", lambda s, n0=n0, n1=n1: s.activation(
                out=C.k_raw[:, n0:n1], in_=PS0[:, n0:n1], func=AF.Identity),
                reads=ps0_banks(n0, n1), writes=C.k_raw_regions)
        if rope:
            kr, krr = C.k_raw, C.k_raw_regions
            for p in range(NP):
                n0, n1 = p * 512, min(S, (p + 1) * 512)
                P.emit("tensor", lambda t, n0=n0, n1=n1: t.matmul(
                    PS0[:, n0:n1], C.rotM[:], kr[:, n0:n1], start=True, stop=True),
                    reads=[("rotM",)] + krr, writes=ps0_banks(n0, n1))
            P.emit("vector", lambda v: v.tensor_tensor(out=C.t1b[:, 0:S], in0=kr[:, 0:S],
                                                       in1=C.cosT[:, 0:S], op=ALU.mult),
                   reads=krr + ROPEB, writes=C.t1b_regions)
            for p in range(NP):
                n0, n1 = p * 512, min(S, (p + 1) * 512)
                P.emit("vector", lambda v, ob=out_buf, n0=n0, n1=n1: v.tensor_tensor(
                    out=ob[:, n0:n1], in0=PS0[:, n0:n1], in1=C.sinS[:, n0:n1], op=ALU.mult),
                    reads=ps0_banks(n0, n1) + ROPEB, writes=[out_regions[min(p // 2, len(out_regions) - 1)]])
            for hh in range((S + 1023) // 1024):
                a0, a1 = hh * 1024, min(S, (hh + 1) * 1024)
                P.emit("vector", lambda v, ob=out_buf, a0=a0, a1=a1: v.tensor_tensor(
                    out=ob[:, a0:a1], in0=ob[:, a0:a1], in1=C.t1b[:, a0:a1], op=ALU.add),
                    reads=[out_regions[hh]] + C.t1b_regions, writes=[out_regions[hh]])
        else:
            P.emit("vector", lambda v, ob=out_buf: v.tensor_copy(out=ob[:, 0:S], in_=C.k_raw[:, 0:S]),
                   reads=C.k_raw_regions, writes=out_regions)

    proj("k", C.kT, C.kT_regions, rope=True)
    proj("q", C.qT, C.qT_regions, rope=True)
    if h > 0:
        _emit_headnorm(C, h - 1)
    proj("v", C.vT_sb, C.vT_regions, rope=False)

    KCH = S // 128
    for g4 in range((KCH + 3) // 4):
        pst, preg = (C.PSTa, ("PSTa",)) if g4 % 2 == 0 else (C.PSTb, ("PSTb",))
        nblk = min(4, KCH - g4 * 4)

        def tf(t, g4=g4, pst=pst, nblk=nblk):
            last = None
            for j in range(nblk):
                cch = g4 * 4 + j
                last = t.transpose(pst[:, j * 128:(j + 1) * 128],
                                   C.vT_sb[:, cch * 128:(cch + 1) * 128], C.ident[:])
            return last

        P.emit("tensor", tf, reads=C.vT_regions + [("ident",)], writes=[preg])
        if g4 % 2 == 0:
            P.emit("vector", lambda v, g4=g4, pst=pst, nblk=nblk: v.tensor_copy(
                out=C.V_sb[:, g4 * 512:g4 * 512 + nblk * 128], in_=pst[:, 0:nblk * 128]),
                reads=[preg], writes=C.V_regions)
        else:
            P.emit("scalar", lambda s, g4=g4, pst=pst, nblk=nblk: s.copy(
                out=C.V_sb[:, g4 * 512:g4 * 512 + nblk * 128], in_=pst[:, 0:nblk * 128]),
                reads=[preg], writes=C.V_regions)
    if h + 1 < NH:
        load_wslice_next = lambda which, srcw, slot: _load_wslice(C, h + 1, which, srcw, slot)
        load_wslice_next("k", C.wk_d, 0)
        load_wslice_next("q", C.wq_d, 1)
        load_wslice_next("v", C.wv_d, 0)

    for qb in range(NQB):
        q0 = qb * QB
        nch = (q0 + QB) // 128
        scnt = 0

        def pieces_of(cc):
            out = []
            pstart = max(cc * 128, q0)
            while pstart < q0 + QB:
                pend = min(q0 + ((pstart - q0) // 512 + 1) * 512, q0 + QB)
                out.append((pstart, pend))
                pstart = pend
            return out

        for cc in range(nch + 1):
            if cc < nch:
                psl = cc % 2
                pb, pbreg = C.probsT[psl], C.probsT_regions[psl]
                pcs = pieces_of(cc)
                merged = (len(pcs) == 2 and pcs[0][0] == q0 and QB == 1024)
                for (p0, p1) in pcs:
                    sc_off = 1024 + (scnt % 2) * 512
                    scnt += 1
                    P.emit("tensor", lambda t, cc=cc, p0=p0, p1=p1, sc_off=sc_off: t.matmul(
                        PS0[:, sc_off:sc_off + (p1 - p0)], C.kT[:, cc * 128:(cc + 1) * 128],
                        C.qT[:, p0:p1], start=True, stop=True),
                        reads=[C.kT_regions[min(cc // 8, len(C.kT_regions) - 1)],
                               C.qT_regions[min(p0 // 1024, len(C.qT_regions) - 1)]],
                        writes=[("PS0", sc_off // 512)])
                    if not merged:
                        P.emit("scalar", lambda s, p0=p0, p1=p1, sc_off=sc_off, pb=pb, q0=q0: s.activation(
                            out=pb[:, p0 - q0:p1 - q0], in_=PS0[:, sc_off:sc_off + (p1 - p0)],
                            func=AF.Exp, scale=SCALE, bias=-B_SHIFT),
                            reads=[("PS0", sc_off // 512)], writes=pbreg)
                if merged:
                    P.emit("scalar", lambda s, pb=pb: s.activation(
                        out=pb[:, 0:1024], in_=PS0[:, 1024:2048],
                        func=AF.Exp, scale=SCALE, bias=-B_SHIFT),
                        reads=[("PS0", 2), ("PS0", 3)], writes=pbreg)
                if cc * 128 >= q0:
                    d0 = cc * 128 - q0
                    P.emit("vector", lambda v, d0=d0, pb=pb: v.tensor_tensor(
                        out=pb[:, d0:d0 + 128], in0=pb[:, d0:d0 + 128], in1=C.mask128[:], op=ALU.mult),
                        reads=pbreg + [("mask",)], writes=pbreg)
            if cc >= 1:
                ccp = cc - 1
                pb, pbreg = C.probsT[ccp % 2], C.probsT_regions[ccp % 2]
                for (p0, p1) in pieces_of(ccp):
                    bank = (p0 - q0) // 512
                    last_c = min(nch - 1, (q0 + (bank + 1) * 512) // 128 - 1)
                    P.emit("tensor", lambda t, ccp=ccp, p0=p0, p1=p1, pb=pb, q0=q0, last_c=last_c: t.matmul(
                        PS0[:, p0 - q0:p1 - q0], C.V_sb[:, ccp * 128:(ccp + 1) * 128],
                        pb[:, p0 - q0:p1 - q0], start=(ccp == 0), stop=(ccp == last_c)),
                        reads=C.V_regions + pbreg, writes=[("PS0", bank)])
                    P.emit("tensor", lambda t, ccp=ccp, p0=p0, p1=p1, pb=pb, q0=q0, last_c=last_c:
                           t.matmul(PS1[0:1, (p0 - q0):(p1 - q0)],
                                    C.ones_c[:], pb[:, p0 - q0:p1 - q0],
                                    start=(ccp == 0), stop=(ccp == last_c)),
                           reads=[("ones",)] + pbreg, writes=[("PS1", (p0 - q0) // 512)])
        P.emit("scalar", lambda s, q0=q0, h=h, QB=QB: s.activation(
            out=C.apre(h, q0, q0 + QB), in_=PS0[:, 0:QB], func=AF.Identity),
            reads=ps0_banks(0, QB), writes=C.apre_regions(h))
        P.emit("scalar", lambda s, QB=QB: s.activation(
            out=C.denF[0:1, 0:QB], in_=PS1[0:1, 0:QB], func=AF.Identity),
            reads=[("PS1", 0), ("PS1", 1)], writes=C.den_regions)
        slot = (h * NQB + qb) % 3
        P.emit("vector", lambda v, QB=QB, slot=slot: _recip_lp(
            C.nc, v, C.rcp[slot][0:1, 0:QB], C.denF[0:1, 0:QB]),
            reads=C.den_regions, writes=C.rcp_regions[slot])


def _emit_headnorm(C, h):
    P, QB, NQB = C.P, C.QB, C.NQB
    PS0 = C.PS0
    for qb in range(NQB):
        q0 = qb * QB
        slot = (h * NQB + qb) % 3
        npiece = max(1, QB // 512)
        w = min(512, QB)
        for gpi in range(npiece):
            P.emit("tensor", lambda t, gpi=gpi, w=w, slot=slot: t.matmul(
                C.PS1[:, gpi * 512:gpi * 512 + w], C.ones_r[:],
                C.rcp[slot][0:1, gpi * 512:gpi * 512 + w], start=True, stop=True),
                reads=[("onesr",)] + C.rcp_regions[slot], writes=[("PS1", gpi)])
            rsl, rreg = C.rep_sb[gpi % 2], C.rep_regions[gpi % 2]
            P.emit("scalar", lambda s, rsl=rsl, w=w, gpi=gpi: s.activation(
                out=rsl[:, 0:w], in_=C.PS1[:, gpi * 512:gpi * 512 + w], func=AF.Identity),
                reads=[("PS1", gpi)], writes=rreg)
            P.emit("vector", lambda v, gpi=gpi, rsl=rsl, w=w, q0=q0, h=h: v.tensor_tensor(
                out=C.apre(h, q0 + gpi * 512, q0 + gpi * 512 + w),
                in0=C.apre(h, q0 + gpi * 512, q0 + gpi * 512 + w), in1=rsl[:, 0:w], op=ALU.mult),
                reads=C.apre_regions(h) + rreg, writes=C.apre_regions(h))


def _emit_wo(C):
    P, S, TC = C.P, C.S, C.TC
    PS0 = C.PS0
    apre_reads = [r for hh in range(NH) for r in C.apre_regions(hh)]
    for ns in range(4):
        wsl = ns % 2
        wot, wotreg = C.WoT[wsl], C.WoT_regions[wsl]
        wot3d = wot.rearrange("p (c u) -> p c u", c=16, u=512)
        for k in range(4):
            ssl = k % 2
            stg, streg = C.wstg[ssl], C.wstg_regions[ssl]
            P.emit("gpsimd", lambda g, ns=ns, k=k, stg=stg: g.dma_start(
                out=stg[:], in_=C.wo_d[ns * 512 + k * 128: ns * 512 + (k + 1) * 128, :]),
                writes=streg, dma_sem="st%d" % ssl)
            for g4 in range(4):
                pst, preg = (C.PSTa, ("PSTa",)) if g4 % 2 == 0 else (C.PSTb, ("PSTb",))

                def tf(t, g4=g4, pst=pst, stg=stg):
                    last = None
                    for j in range(4):
                        dc = g4 * 4 + j
                        last = t.transpose(pst[:, j * 128:(j + 1) * 128],
                                           stg[:, dc * 128:(dc + 1) * 128], C.ident[:])
                    return last

                P.emit("tensor", tf, reads=streg + [("ident",)], writes=[preg])
                eng = "vector" if g4 % 2 == 0 else "scalar"

                def evw(e, g4=g4, pst=pst, k=k, wot3d=wot3d):
                    dst = wot3d[:, g4 * 4:(g4 + 1) * 4, k * 128:(k + 1) * 128]
                    srcv = pst[:, 0:512].rearrange("p (c u) -> p c u", c=4, u=128)
                    if hasattr(e, "tensor_copy"):
                        return e.tensor_copy(out=dst, in_=srcv)
                    return e.copy(out=dst, in_=srcv)

                P.emit(eng, evw, reads=[preg], writes=wotreg)
        for tc in range(TC):
            bank = tc % 4

            def mmf(t, tc=tc, bank=bank, wot=wot):
                last = None
                for ic in range(DC):
                    last = t.matmul(PS0[:, bank * 512:(bank + 1) * 512],
                                    C.apre(ic, tc * 128, (tc + 1) * 128),
                                    wot[:, ic * 512:(ic + 1) * 512],
                                    start=(ic == 0), stop=(ic == DC - 1))
                return last

            P.emit("tensor", mmf, reads=wotreg + apre_reads, writes=[("PS0", bank)])
            P.emit("scalar", lambda s, tc=tc, ns=ns, bank=bank: s.activation(
                out=C.resv(tc, ns * 512, (ns + 1) * 512), in_=PS0[:, bank * 512:(bank + 1) * 512],
                func=AF.Identity),
                reads=[("PS0", bank)], writes=C.res_regions(tc))


def _emit_expert(C, half, e):
    P, S, TH, THC = C.P, C.S, C.TH, C.THC
    PS0 = C.PS0
    if e < 8:
        def wg_src(hc):
            return C.wg_d[e].rearrange("(dc p) h -> p dc h", p=128)[:, :, hc * 128:(hc + 1) * 128]

        def wu_src(hc):
            return C.wu_d[e].rearrange("(dc p) h -> p dc h", p=128)[:, :, hc * 128:(hc + 1) * 128]

        wd_base = C.wd_d[e]

        def scale_ap(tcg):
            return C.combine[:, tcg * 8 + e: tcg * 8 + e + 1]
        scale_reg = ("combine",)
    else:
        eh = e - 8

        def wg_src(hc, eh=eh):
            return C.swg_d.rearrange("(dc p) h -> p dc h", p=128)[:, :, eh * HM + hc * 128: eh * HM + (hc + 1) * 128]

        def wu_src(hc, eh=eh):
            return C.swu_d.rearrange("(dc p) h -> p dc h", p=128)[:, :, eh * HM + hc * 128: eh * HM + (hc + 1) * 128]

        wd_base = C.swd_d[eh * HM:(eh + 1) * HM, :]

        def scale_ap(tcg):
            return C.sgate[:, tcg:tcg + 1]
        scale_reg = ("sgate",)

    t0 = half * TH
    NPT = (TH + 511) // 512

    def emit_wd_load(q):
        area = q % 2
        first_bridge = C.__dict__.setdefault("_wd_bridged", False)

        def df(g, q=q, area=area, wd_base=wd_base):
            srcv = wd_base[:, q * 512:(q + 1) * 512]
            srcv = srcv.rearrange("(c p) d -> p c d", p=128)
            dst = C.wd_area_view(area).rearrange("p (c d) -> p c d", c=HMC)
            return g.dma_start(out=dst, in_=srcv)

        extra = [] if first_bridge else C.wd_pages
        C._wd_bridged = True
        P.emit("gpsimd", df, writes=C.wd_regions(area) + extra, dma_sem="wdp%d" % area)

    for hc in range(HMC):
        gs, gsr = C.wgslot[hc % 2], C.wgslot_regions[hc % 2]
        us, usr = C.wuslot[hc % 2], C.wuslot_regions[hc % 2]
        P.emit("gpsimd", lambda g, hc=hc, gs=gs: g.dma_start(
            out=gs.rearrange("p (dc h) -> p dc h", dc=16), in_=wg_src(hc)),
            writes=gsr, dma_sem="wg%d" % (hc % 2))
        P.emit("gpsimd", lambda g, hc=hc, us=us: g.dma_start(
            out=us.rearrange("p (dc h) -> p dc h", dc=16), in_=wu_src(hc)),
            writes=usr, dma_sem="wu%d" % (hc % 2))

        def gmm(t, gs=gs, t0=t0, NPT=NPT, TH=TH):
            last = None
            for dc in range(DC):
                for p in range(NPT):
                    n0 = t0 + p * 512
                    n1 = min(t0 + TH, n0 + 512)
                    last = t.matmul(PS0[:, p * 512: p * 512 + (n1 - n0)], gs[:, dc * 128:(dc + 1) * 128],
                                    C.xnt(dc, n0, n1), start=(dc == 0), stop=(dc == DC - 1))
            return last

        P.emit("tensor", gmm, reads=gsr + C.xnt_cols_regions(t0, t0 + TH), writes=ps0_banks(0, TH))

        def umm(t, us=us, t0=t0, NPT=NPT, TH=TH):
            last = None
            for dc in range(DC):
                for p in range(NPT):
                    n0 = t0 + p * 512
                    n1 = min(t0 + TH, n0 + 512)
                    last = t.matmul(PS0[:, 1024 + p * 512: 1024 + p * 512 + (n1 - n0)],
                                    us[:, dc * 128:(dc + 1) * 128],
                                    C.xnt(dc, n0, n1), start=(dc == 0), stop=(dc == DC - 1))
            return last

        P.emit("tensor", umm, reads=usr + C.xnt_cols_regions(t0, t0 + TH), writes=ps0_banks(1024, 1024 + TH))
        gso, gsor = C.gsil[hc % 2], C.gsil_regions[hc % 2]
        P.emit("scalar", lambda s, gso=gso, TH=TH: s.activation(out=gso[:, 0:TH], in_=PS0[:, 0:TH], func=AF.Silu),
               reads=ps0_banks(0, TH), writes=gsor)
        P.emit("vector", lambda v, hc=hc, gso=gso, TH=TH: v.tensor_tensor(
            out=C.hpr_view(hc, 0, TH), in0=gso[:, 0:TH], in1=PS0[:, 1024:1024 + TH], op=ALU.mult),
            reads=gsor + ps0_banks(1024, 1024 + TH), writes=C.hpr_regions(hc))
        if hc == 3:
            emit_wd_load(0)
        if hc == 6:
            emit_wd_load(1)

    hpr_reads = [r for hc in range(HMC) for r in C.hpr_regions(hc)]
    for q in range(4):
        area = q % 2
        if q >= 2:
            emit_wd_load(q)
        for tc in range(THC):
            tcg = half * THC + tc
            bank = tc % 4

            def dmm(t, tc=tc, bank=bank, area=area):
                last = None
                for hc in range(HMC):
                    last = t.matmul(PS0[:, bank * 512:(bank + 1) * 512],
                                    C.hpr_view(hc, tc * 128, (tc + 1) * 128),
                                    C.wd_view(area, hc, 0, 512),
                                    start=(hc == 0), stop=(hc == HMC - 1))
                return last

            P.emit("tensor", dmm, reads=C.wd_regions(area) + hpr_reads,
                   writes=[("PS0", bank)])
            gso, gsor = C.gsil[tc % 2], C.gsil_regions[tc % 2]
            P.emit("scalar", lambda s, bank=bank, tcg=tcg, gso=gso: s.activation(
                out=gso[:, 0:512], in_=PS0[:, bank * 512:(bank + 1) * 512], func=AF.Identity,
                scale=scale_ap(tcg)),
                reads=[("PS0", bank)] + [scale_reg], writes=gsor)
            P.emit("vector", lambda v, tcg=tcg, q=q, gso=gso: v.tensor_tensor(
                out=C.resv(tcg, q * 512, (q + 1) * 512),
                in0=C.resv(tcg, q * 512, (q + 1) * 512), in1=gso[:, 0:512], op=ALU.add),
                reads=C.res_regions(tcg) + gsor, writes=C.res_regions(tcg))


def _emit_final(C):
    P, TC = C.P, C.TC
    for tcg in range(TC):
        sl = tcg % 2
        xv = C.xtile[sl]
        xreg = C.xtile_regions[sl]
        ov = C.outstage[sl]
        oreg = C.outstage_regions[sl]
        P.emit("sync", lambda s, tcg=tcg, xv=xv: s.dma_start(
            out=xv[:], in_=C.x_d[tcg * 128:(tcg + 1) * 128, :]),
            writes=xreg, dma_sem="xt%d" % sl)
        extra = [("wda", 0), ("wda", 1)] if tcg == 0 else []
        P.emit("vector", lambda v, tcg=tcg, xv=xv, ov=ov: v.tensor_tensor(
            out=ov[:], in0=xv[:], in1=C.resv(tcg, 0, 2048), op=ALU.add),
            reads=xreg + C.res_regions(tcg) + extra, writes=oreg)
        P.emit("sync", lambda s, tcg=tcg, ov=ov: s.dma_start(
            out=C.out_d[tcg * 128:(tcg + 1) * 128, :], in_=ov[:]),
            reads=oreg, dma_sem="osem")
    osem_total = P.counts["osem"]
    handles = P.sem_handles
    P.raw("sync", lambda s: s.wait_ge(handles["osem"], osem_total))
    if P.counts["dbgsem"]:
        dbg_total = P.counts["dbgsem"]
        P.raw("gpsimd", lambda g: g.wait_ge(handles["dbgsem"], dbg_total))



_CACHE = {}


def _get_program(S):
    if S not in _CACHE:
        _CACHE[S] = build_program(S)
    return _CACHE[S]


def kernel(L0_attn_x, L0_attn_Wq, L0_attn_Wk, L0_attn_Wv, L0_attn_Wo,
           L0_ffn_gate_w, L0_ffn_w_gate, L0_ffn_w_up, L0_ffn_w_down,
           L0_ffn_shgate_w, L0_ffn_shw_gate, L0_ffn_shw_up, L0_ffn_shw_down):
    x = np.ascontiguousarray(np.asarray(L0_attn_x, dtype=np.float32))
    B = x.shape[0]
    weights = {
        "L0_attn_Wq": np.ascontiguousarray(np.asarray(L0_attn_Wq, np.float32)),
        "L0_attn_Wk": np.ascontiguousarray(np.asarray(L0_attn_Wk, np.float32)),
        "L0_attn_Wv": np.ascontiguousarray(np.asarray(L0_attn_Wv, np.float32)),
        "L0_attn_Wo": np.ascontiguousarray(np.asarray(L0_attn_Wo, np.float32)),
        "L0_ffn_gate_w": np.ascontiguousarray(np.asarray(L0_ffn_gate_w, np.float32)),
        "L0_ffn_w_gate": np.ascontiguousarray(np.asarray(L0_ffn_w_gate, np.float32)),
        "L0_ffn_w_up": np.ascontiguousarray(np.asarray(L0_ffn_w_up, np.float32)),
        "L0_ffn_w_down": np.ascontiguousarray(np.asarray(L0_ffn_w_down, np.float32)),
        "L0_ffn_shgate_w": np.ascontiguousarray(np.asarray(L0_ffn_shgate_w, np.float32)),
        "L0_ffn_shw_gate": np.ascontiguousarray(np.asarray(L0_ffn_shw_gate, np.float32)),
        "L0_ffn_shw_up": np.ascontiguousarray(np.asarray(L0_ffn_shw_up, np.float32)),
        "L0_ffn_shw_down": np.ascontiguousarray(np.asarray(L0_ffn_shw_down, np.float32)),
    }
    nc, _ = _get_program(x.shape[1])
    in_maps = [dict(weights, x=x[b]) for b in range(B)]
    res = run_bass_kernel_spmd(nc, in_maps, core_ids=list(range(B)))
    out = np.stack([np.asarray(r["out"], dtype=np.float32) for r in res.results], axis=0)
    return out


# version 11 (final)
# speedup vs baseline: 1.0484x; 1.0014x over previous
import math
from contextlib import ExitStack

import numpy as np

import concourse.bass as bass
import concourse.mybir as mybir
from concourse.bass_utils import run_bass_kernel_spmd

F32 = mybir.dt.float32
BF16 = mybir.dt.bfloat16
I32 = mybir.dt.int32
AF = mybir.ActivationFunctionType
ALU = mybir.AluOpType
AX = mybir.AxisListType

D = 2048
DC = 16
NH = 16
HM = 1408
HMC = 11
HS = 2816
NEXP = 10
B_SHIFT = 24.0
SCALE = 1.0 / math.sqrt(128.0)
LN_EPS = 1e-5
PIH = math.pi / 2.0

ENGINES = ("sync", "tensor", "vector", "scalar", "gpsimd")
ROPEB = [("ropebf", i) for i in range(4)]


class Prog:

    def __init__(self, nc):
        self.nc = nc
        self.thunks = {e: [] for e in ENGINES}
        self.sem_handles = {}
        self.counts = {}
        self.clocks = {e: {} for e in ENGINES}
        self.last_write = {}
        self.readers = {}
        self.stats = {e: 0 for e in ENGINES}

    def add_sem(self, name, handle):
        self.sem_handles[name] = handle
        self.counts[name] = 0

    def emit(self, eng, fn, reads=(), writes=(), dma_sem=None):
        need = {}

        def _need(tok):
            if tok is not None:
                s, v = tok
                if need.get(s, 0) < v:
                    need[s] = v

        if eng == "scalar":
            reads = list(reads) + [("cpool",)]
        for r in reads:
            _need(self.last_write.get(r))
        for r in writes:
            _need(self.last_write.get(r))
            for s, v in self.readers.get(r, {}).items():
                _need((s, v))
        clk = self.clocks[eng]
        waits = []
        for s, v in need.items():
            if clk.get(s, 0) < v:
                assert v <= self.counts[s], f"wait on future value {s}>={v}"
                waits.append((s, v))
                clk[s] = v
        if dma_sem is not None:
            semname, inc = dma_sem, 16
        else:
            semname, inc = "e_" + eng, 1
        self.counts[semname] += inc
        tok = (semname, self.counts[semname])
        for r in writes:
            self.last_write[r] = tok
            self.readers[r] = {}
        for r in reads:
            d = self.readers.setdefault(r, {})
            if d.get(semname, 0) < tok[1]:
                d[semname] = tok[1]
        semh = self.sem_handles[semname]
        handles = self.sem_handles
        self.stats[eng] += 1

        def thunk(engobj, waits=tuple(waits), fn=fn, semh=semh, inc=inc, handles=handles):
            for s, v in waits:
                engobj.wait_ge(handles[s], v)
            fn(engobj).then_inc(semh, inc)

        self.thunks[eng].append(thunk)
        return tok

    def raw(self, eng, fn):
        self.thunks[eng].append(fn)

    def set_write_token(self, region, tok):
        self.last_write[region] = tok
        self.readers[region] = {}


def bpages(name, c0, c1):
    return [(name, p) for p in range(c0 // 1024, (c1 + 1023) // 1024)]


def ps0_banks(c0, c1):
    return [("PS0", b) for b in range(c0 // 512, (c1 + 511) // 512)]


def _recip_lp(nc, v, out, in_):
    with nc.allow_low_precision(reason="bf16 softmax reciprocal used as matmul operand"):
        return v.reciprocal(out=out, in_=in_)


class Ctx:
    pass


def build_program(S=2048, debug=()):
    C = Ctx()
    C.S = S
    C.TC = S // 128
    C.QB = min(1024, S)
    C.NQB = S // C.QB
    C.TH = min(1024, S)
    C.NHALF = S // C.TH
    C.THC = C.TH // 128

    nc = bass.Bass("TRN2", target_bir_lowering=False, debug=False)
    C.nc = nc

    C.x_d = nc.dram_tensor("x", [S, D], F32, kind="ExternalInput").ap()
    C.wq_d = nc.dram_tensor("L0_attn_Wq", [D, D], F32, kind="ExternalInput").ap()
    C.wk_d = nc.dram_tensor("L0_attn_Wk", [D, D], F32, kind="ExternalInput").ap()
    C.wv_d = nc.dram_tensor("L0_attn_Wv", [D, D], F32, kind="ExternalInput").ap()
    C.wo_d = nc.dram_tensor("L0_attn_Wo", [D, D], F32, kind="ExternalInput").ap()
    C.gw_d = nc.dram_tensor("L0_ffn_gate_w", [8, D], F32, kind="ExternalInput").ap()
    C.wg_d = nc.dram_tensor("L0_ffn_w_gate", [8, D, HM], F32, kind="ExternalInput").ap()
    C.wu_d = nc.dram_tensor("L0_ffn_w_up", [8, D, HM], F32, kind="ExternalInput").ap()
    C.wd_d = nc.dram_tensor("L0_ffn_w_down", [8, HM, D], F32, kind="ExternalInput").ap()
    C.sgw_d = nc.dram_tensor("L0_ffn_shgate_w", [D], F32, kind="ExternalInput").ap()
    C.swg_d = nc.dram_tensor("L0_ffn_shw_gate", [D, HS], F32, kind="ExternalInput").ap()
    C.swu_d = nc.dram_tensor("L0_ffn_shw_up", [D, HS], F32, kind="ExternalInput").ap()
    C.swd_d = nc.dram_tensor("L0_ffn_shw_down", [HS, D], F32, kind="ExternalInput").ap()
    C.out_d = nc.dram_tensor("out", [S, D], F32, kind="ExternalOutput").ap()
    C.dbg = {}
    dbg_shapes = {
        "xnt": [128, 32768], "apre": [128, 32768], "resattn": [128, 32768],
        "res": [128, 32768], "combine": [128, C.TC * 8], "sgate": [128, C.TC],
        "kt": [128, 2048], "qt": [128, 2048], "vsb": [128, 2048],
        "cos": [128, 2048], "sin": [128, 2048], "hpr": [128, 11264],
    }
    debug = set(debug)
    if "kt" in debug:
        debug |= {"qt", "vsb"}
    if "cos" in debug:
        debug.add("sin")
    if "combine" in debug:
        debug.add("sgate")
    for name in sorted(debug):
        C.dbg[name] = nc.dram_tensor("dbg_" + name, dbg_shapes[name], F32, kind="ExternalOutput").ap()

    st = ExitStack()

    def sb(name, shape, dtype):
        return st.enter_context(nc.sbuf_tensor(name, shape, dtype))

    C.BIG1 = sb("BIG1", [128, 32768], BF16)
    C.BIG2 = sb("BIG2", [128, 32768], BF16)
    C.BIG3 = sb("BIG3", [128, 32768], BF16)
    C.cosT = sb("cosT", [128, 2048], BF16)
    C.sinS = sb("sinS", [128, 2048], BF16)
    C.ident = sb("ident", [128, 128], BF16)
    C.mask128 = sb("mask128", [128, 128], BF16)
    C.ones_c = sb("ones_c", [128, 1], BF16)
    C.ones_r = sb("ones_r", [1, 128], BF16)
    C.cpool = sb("cpool", [128, 4], F32)
    C.invf = sb("invf", [64, 1], F32)
    C.gstage = sb("gstage", [9, 2048], BF16)
    C.gate_wT = sb("gate_wT", [128, DC * 9], BF16)
    C.combine = sb("combine", [128, C.TC * 8], F32)
    C.sgate = sb("sgate", [128, C.TC], F32)
    C.sglog = sb("sglog", [128, C.TC], F32)
    C.rt = sb("rt", [128, 64], F32)
    C.lgb = sb("lgb", [128, 144], F32)
    C.rotM = sb("rotM", [128, 128], BF16)
    C.lnst = sb("lnst", [128, 96], F32)

    C.PS0 = st.enter_context(nc.psum_tensor("PS0", [128, 2048], F32))
    C.PS1 = st.enter_context(nc.psum_tensor("PS1", [128, 1024], F32))
    C.PSTa = st.enter_context(nc.psum_tensor("PSTa", [128, 1024], BF16))
    C.PSTb = st.enter_context(nc.psum_tensor("PSTb", [128, 1024], BF16))

    semnames = (["e_" + e for e in ENGINES]
                + ["xt0", "xt1", "xt2", "xt3", "st0", "st1", "wg0", "wg1", "wu0", "wu1",
                   "wdp0", "wdp1", "gsem", "mrsem", "osem", "dbgsem"])
    P = Prog(nc)
    C.P = P
    for sn in semnames:
        P.add_sem(sn, st.enter_context(nc.semaphore(sn)))

    nc.const_aps.register(F32, float(-B_SHIFT), C.cpool[:, 0:1])
    nc.const_aps.register(F32, float(LN_EPS), C.cpool[:, 1:2])
    nc.const_aps.register(F32, float(PIH), C.cpool[:, 2:3])
    nc.const_aps.register(F32, 0.0, C.cpool[:, 3:4])

    C.cosF = C.BIG2[:, 0:4096].bitcast(F32)
    C.sinF = C.BIG2[:, 4096:8192].bitcast(F32)
    _xt_off = [8192, 12288, 23040, 27136]
    C.xtile = [C.BIG2[:, o: o + 4096].bitcast(F32) for o in _xt_off]
    C.xtile_regions = [bpages("BIG2", o, o + 4096) for o in _xt_off]
    _xn_off = [16384, 18432, 20480]
    C.xnsc = [C.BIG2[:, o: o + 2048] for o in _xn_off]
    C.xnsc_regions = [bpages("BIG2", o, o + 2048) for o in _xn_off]
    C.lndump = C.BIG2[:, 20480:22528]
    C.io_j = C.BIG2[:, 22528:22784].bitcast(I32)
    C.io_i = C.BIG2[:, 22784:23040].bitcast(I32)
    C.outstage = [C.BIG2[:, 23040 + k * 4096: 27136 + k * 4096].bitcast(F32) for k in range(2)]
    C.outstage_regions = [bpages("BIG2", 23040 + k * 4096, 27136 + k * 4096) for k in range(2)]

    C.WT = {"k": C.BIG3[:, 0:2048], "q": C.BIG3[:, 2048:4096], "v": C.BIG3[:, 4096:6144]}
    C.WT_regions = {"k": bpages("BIG3", 0, 2048), "q": bpages("BIG3", 2048, 4096),
                    "v": bpages("BIG3", 4096, 6144)}
    C.stg = [C.BIG3[:, 6144:8192], C.BIG3[:, 8192:10240]]
    C.stg_regions = [bpages("BIG3", 6144, 8192), bpages("BIG3", 8192, 10240)]
    C.k_raw = C.BIG3[:, 10240:12288]
    C.k_raw_regions = bpages("BIG3", 10240, 12288)
    C.t1b = C.BIG3[:, 12288:14336]
    C.t1b_regions = bpages("BIG3", 12288, 14336)
    C.vT_sb = C.BIG3[:, 14336:16384]
    C.vT_regions = bpages("BIG3", 14336, 16384)
    C.kT = C.BIG3[:, 16384:18432]
    C.kT_regions = bpages("BIG3", 16384, 18432)
    C.qT = C.BIG3[:, 18432:20480]
    C.qT_regions = bpages("BIG3", 18432, 20480)
    C.V_sb = C.BIG3[:, 20480:22528]
    C.V_regions = bpages("BIG3", 20480, 22528)
    C.probsT = [C.BIG3[:, 22528:23552], C.BIG3[:, 23552:24576]]
    C.probsT_regions = [bpages("BIG3", 22528, 23552), bpages("BIG3", 23552, 24576)]
    C.denF = C.BIG3[:, 24576:26624].bitcast(F32)
    C.den_regions = bpages("BIG3", 24576, 26624)
    C.rcp = [C.BIG3[:, 26624:27648], C.BIG3[:, 29696:30720], C.BIG3[:, 30720:31744]]
    C.rcp_regions = [bpages("BIG3", 26624, 27648), bpages("BIG3", 29696, 30720),
                     bpages("BIG3", 30720, 31744)]
    C.rep_sb = [C.BIG3[:, 27648:28672], C.BIG3[:, 28672:29696]]
    C.rep_regions = [bpages("BIG3", 27648, 28672), bpages("BIG3", 28672, 29696)]

    C.WoT = [C.BIG1[:, 0:8192], C.BIG1[:, 8192:16384]]
    C.WoT_regions = [bpages("BIG1", 0, 8192), bpages("BIG1", 8192, 16384)]
    C.wstg = [C.BIG1[:, 16384:18432], C.BIG1[:, 18432:20480]]
    C.wstg_regions = [bpages("BIG1", 16384, 18432), bpages("BIG1", 18432, 20480)]

    C.wgslot = [C.BIG2[:, 11264:13312], C.BIG2[:, 13312:15360]]
    C.wgslot_regions = [bpages("BIG2", 11264, 13312), bpages("BIG2", 13312, 15360)]
    C.wuslot = [C.BIG2[:, 15360:17408], C.BIG2[:, 17408:19456]]
    C.wuslot_regions = [bpages("BIG2", 15360, 17408), bpages("BIG2", 17408, 19456)]
    C.gsil = [C.BIG2[:, 19456:20480], C.BIG2[:, 20480:21504]]
    C.gsil_regions = [bpages("BIG2", 19456, 20480), bpages("BIG2", 20480, 21504)]

    def hpr_view(hc, c0, c1):
        return C.BIG2[:, hc * 1024 + c0: hc * 1024 + c1]

    def hpr_regions(hc):
        return bpages("BIG2", hc * 1024, (hc + 1) * 1024)

    def wd_view(area, hc, c0, c1):
        base = 21504 + area * 5632 + hc * 512
        return C.BIG2[:, base + c0: base + c1]

    def wd_area_view(area):
        return C.BIG2[:, 21504 + area * 5632: 21504 + area * 5632 + 5632]

    C.wd_area_view = wd_area_view
    C.wd_pages = bpages("BIG2", 21504, 32768)

    def wd_regions(area):
        return [("wda", area)]

    def xnt(dc, t0, t1):
        return C.BIG1[:, dc * 2048 + t0: dc * 2048 + t1]

    def xnt_regions(dc):
        return bpages("BIG1", dc * 2048, dc * 2048 + S)

    def apre(h, t0, t1):
        return C.BIG2[:, h * 2048 + t0: h * 2048 + t1]

    def apre_regions(h):
        return bpages("BIG2", h * 2048, h * 2048 + S)

    def resv(tcg, c0, c1):
        return C.BIG3[:, tcg * 2048 + c0: tcg * 2048 + c1]

    def res_regions(tcg):
        return bpages("BIG3", tcg * 2048, (tcg + 1) * 2048)

    def xnt_cols_regions(t0, t1):
        out = []
        for dc in range(DC):
            out += bpages("BIG1", dc * 2048 + t0, dc * 2048 + t1)
        return out

    C.xnt_cols_regions = xnt_cols_regions
    C.hpr_view, C.hpr_regions = hpr_view, hpr_regions
    C.wd_view, C.wd_regions = wd_view, wd_regions
    C.xnt, C.xnt_regions = xnt, xnt_regions
    C.apre, C.apre_regions = apre, apre_regions
    C.resv, C.res_regions = resv, res_regions
    C.XNT_ALL = [r for dc in range(DC) for r in xnt_regions(dc)]

    block = st.enter_context(nc.Block())
    with st:
        _emit_all(C)
        for e in ENGINES:
            def runner(engobj, thunks=P.thunks[e]):
                for t in thunks:
                    t(engobj)
            getattr(block, e)(runner)
    return nc, C



def _emit_all(C):
    P = C.P
    _emit_consts(C)
    _load_wslice(C, 0, "k", C.wk_d, 0)
    _load_wslice(C, 0, "q", C.wq_d, 1)
    _load_wslice(C, 0, "v", C.wv_d, 0)
    _emit_ln(C, first=True)
    for tc in range(C.TC):
        _emit_routing_mm(C, tc)
    for tc in range(C.TC):
        _emit_routing_tc(C, tc)
    _emit_routing_final(C)
    if "xnt" in C.dbg:
        P.emit("gpsimd", lambda s: s.dma_start(
            out=C.dbg["xnt"].rearrange("p (c q) -> p c q", c=16, q=2048)[:, :, 0:C.S],
            in_=C.BIG1[:, :].rearrange("p (c q) -> p c q", c=16, q=2048)[:, :, 0:C.S]),
               reads=C.XNT_ALL + [("dbgser",)], writes=[("dbgser",)], dma_sem="dbgsem")
    if "combine" in C.dbg:
        P.emit("gpsimd", lambda s: s.dma_start(out=C.dbg["combine"], in_=C.combine[:]),
               reads=[("combine",)] + [("dbgser",)], writes=[("dbgser",)], dma_sem="dbgsem")
        P.emit("gpsimd", lambda s: s.dma_start(out=C.dbg["sgate"], in_=C.sgate[:]),
               reads=[("sgate",)] + [("dbgser",)], writes=[("dbgser",)], dma_sem="dbgsem")
    for h in range(NH):
        _emit_head(C, h)
    _emit_headnorm(C, NH - 1)

    if "apre" in C.dbg:
        P.emit("gpsimd", lambda s: s.dma_start(
            out=C.dbg["apre"].rearrange("p (c q) -> p c q", c=16, q=2048)[:, :, 0:C.S],
            in_=C.BIG2[:, :].rearrange("p (c q) -> p c q", c=16, q=2048)[:, :, 0:C.S]),
               reads=[r for hh in range(NH) for r in C.apre_regions(hh)] + [("dbgser",)], writes=[("dbgser",)], dma_sem="dbgsem")
    if "kt" in C.dbg:
        P.emit("gpsimd", lambda s: s.dma_start(out=C.dbg["kt"][:, 0:C.S], in_=C.kT[:, 0:C.S]), reads=C.kT_regions + [("dbgser",)], writes=[("dbgser",)], dma_sem="dbgsem")
        P.emit("gpsimd", lambda s: s.dma_start(out=C.dbg["qt"][:, 0:C.S], in_=C.qT[:, 0:C.S]), reads=C.qT_regions + [("dbgser",)], writes=[("dbgser",)], dma_sem="dbgsem")
        P.emit("gpsimd", lambda s: s.dma_start(out=C.dbg["vsb"][:, 0:C.S], in_=C.V_sb[:, 0:C.S]), reads=C.V_regions + [("dbgser",)], writes=[("dbgser",)], dma_sem="dbgsem")
    _emit_wo(C)
    if "resattn" in C.dbg:
        P.emit("gpsimd", lambda s: s.dma_start(out=C.dbg["resattn"][:, 0:C.TC * 2048], in_=C.BIG3[:, 0:C.TC * 2048]),
               reads=[r for t in range(C.TC) for r in C.res_regions(t)] + [("dbgser",)], writes=[("dbgser",)], dma_sem="dbgsem")
    _emit_ln(C, first=False)
    for half in range(C.NHALF):
        for e in range(NEXP):
            _emit_expert(C, half, e)
    if "hpr" in C.dbg:
        P.emit("gpsimd", lambda s: s.dma_start(out=C.dbg["hpr"], in_=C.BIG2[:, 0:11264]),
               reads=[r for hc in range(HMC) for r in C.hpr_regions(hc)] + [("dbgser",)], writes=[("dbgser",)], dma_sem="dbgsem")
    if "res" in C.dbg:
        P.emit("gpsimd", lambda s: s.dma_start(out=C.dbg["res"][:, 0:C.TC * 2048], in_=C.BIG3[:, 0:C.TC * 2048]),
               reads=[r for t in range(C.TC) for r in C.res_regions(t)] + [("dbgser",)], writes=[("dbgser",)], dma_sem="dbgsem")
    _emit_final(C)


def _emit_consts(C):
    P, nc = C.P, C.nc
    P.emit("gpsimd", lambda g: g.iota(C.io_j[:], pattern=[[1, 128]], base=0, channel_multiplier=0),
           writes=[("io", "j")])
    P.emit("gpsimd", lambda g: g.iota(C.io_i[:], pattern=[[0, 128]], base=0, channel_multiplier=1),
           writes=[("io", "i")])
    P.emit("gpsimd", lambda g: g.memset(C.ones_c[:], 1.0), writes=[("ones",)])
    P.emit("gpsimd", lambda g: g.memset(C.ones_r[:], 1.0), writes=[("onesr",)])
    for col, val in ((0, -B_SHIFT), (1, LN_EPS), (2, PIH), (3, 0.0)):
        P.emit("vector", lambda v, col=col, val=val: v.memset(C.cpool[:, col:col + 1], float(val)),
               writes=[("cpool",)])
    P.emit("vector", lambda v: v.tensor_tensor(out=C.ident[:], in0=C.io_j[:], in1=C.io_i[:], op=ALU.is_equal),
           reads=[("io", "j"), ("io", "i")], writes=[("ident",)])
    P.emit("vector", lambda v: v.tensor_tensor(out=C.mask128[:], in0=C.io_j[:], in1=C.io_i[:], op=ALU.is_ge),
           reads=[("io", "j"), ("io", "i")], writes=[("mask",)])

    S = C.S
    cosF, sinF = C.cosF, C.sinF
    CFR, SFR = [("cosF",)], [("sinF",)]
    xt0 = C.xtile[0]
    xt1 = C.xtile[1]
    tAr = C.xtile_regions[0] + C.xtile_regions[1]
    P.emit("scalar", lambda s: s.activation(out=C.invf[:], in_=C.io_i[0:64, 0:1], func=AF.Exp,
                                            scale=-math.log(10000.0) / 64.0),
           reads=[("io", "i")], writes=[("invf",)])
    P.emit("vector", lambda v: v.memset(cosF[0:64, 0:1], 1.0), writes=CFR)
    P.emit("vector", lambda v: v.memset(sinF[0:64, 0:1], 0.0), writes=SFR)
    P.emit("scalar", lambda s: s.activation(out=cosF[0:64, 1:2], in_=C.invf[:], func=AF.Sin, bias=PIH),
           reads=[("invf",), ("cpool",)], writes=CFR)
    P.emit("scalar", lambda s: s.activation(out=sinF[0:64, 1:2], in_=C.invf[:], func=AF.Sin),
           reads=[("invf",)], writes=SFR)
    L = 1
    while L < S - 1:
        n = min(L, S - 1 - L)
        P.emit("vector", lambda v, n=n, L=L: v.tensor_scalar(
            out=xt0[0:64, 0:n], in0=cosF[0:64, 1:1 + n], scalar1=cosF[0:64, L:L + 1], scalar2=None, op0=ALU.mult),
            reads=CFR, writes=tAr)
        P.emit("vector", lambda v, n=n, L=L: v.tensor_scalar(
            out=xt1[0:64, 0:n], in0=sinF[0:64, 1:1 + n], scalar1=sinF[0:64, L:L + 1], scalar2=None, op0=ALU.mult),
            reads=SFR + CFR, writes=tAr)
        P.emit("vector", lambda v, n=n, L=L: v.tensor_tensor(
            out=cosF[0:64, L + 1:L + 1 + n], in0=xt0[0:64, 0:n], in1=xt1[0:64, 0:n], op=ALU.subtract),
            reads=tAr, writes=CFR)
        P.emit("vector", lambda v, n=n, L=L: v.tensor_scalar(
            out=xt0[0:64, 0:n], in0=sinF[0:64, 1:1 + n], scalar1=cosF[0:64, L:L + 1], scalar2=None, op0=ALU.mult),
            reads=SFR + CFR, writes=tAr)
        P.emit("vector", lambda v, n=n, L=L: v.tensor_scalar(
            out=xt1[0:64, 0:n], in0=cosF[0:64, 1:1 + n], scalar1=sinF[0:64, L:L + 1], scalar2=None, op0=ALU.mult),
            reads=SFR + CFR, writes=tAr)
        P.emit("vector", lambda v, n=n, L=L: v.tensor_tensor(
            out=sinF[0:64, L + 1:L + 1 + n], in0=xt0[0:64, 0:n], in1=xt1[0:64, 0:n], op=ALU.add),
            reads=tAr, writes=SFR)
        L += n
    P.emit("gpsimd", lambda g: g.dma_start(out=C.cosT[0:64, 0:S], in_=cosF[0:64, 0:S]),
           reads=CFR, writes=[("ropebf", 0)], dma_sem="mrsem")
    P.emit("gpsimd", lambda g: g.dma_start(out=C.cosT[64:128, 0:S], in_=cosF[0:64, 0:S]),
           reads=CFR, writes=[("ropebf", 1)], dma_sem="mrsem")
    P.emit("gpsimd", lambda g: g.dma_start(out=C.sinS[0:64, 0:S], in_=sinF[0:64, 0:S]),
           reads=SFR, writes=[("ropebf", 2)], dma_sem="mrsem")
    P.emit("gpsimd", lambda g: g.dma_start(out=C.sinS[64:128, 0:S], in_=sinF[0:64, 0:S]),
           reads=SFR, writes=[("ropebf", 3)], dma_sem="mrsem")
    P.emit("vector", lambda v: v.tensor_scalar(out=xt0[:, 0:128], in0=C.io_j[:], scalar1=64.0,
                                               scalar2=None, op0=ALU.add), reads=[("io", "j")], writes=tAr)
    P.emit("vector", lambda v: v.tensor_tensor(out=xt0[:, 128:256], in0=C.io_i[:], in1=xt0[:, 0:128],
                                               op=ALU.is_equal), reads=[("io", "i")] + tAr, writes=tAr)
    P.emit("vector", lambda v: v.tensor_scalar(out=xt0[:, 256:384], in0=C.io_j[:], scalar1=-64.0,
                                               scalar2=None, op0=ALU.add), reads=[("io", "j")], writes=tAr)
    P.emit("vector", lambda v: v.tensor_tensor(out=xt0[:, 384:512], in0=C.io_i[:], in1=xt0[:, 256:384],
                                               op=ALU.is_equal), reads=[("io", "i")] + tAr, writes=tAr)
    P.emit("vector", lambda v: v.tensor_tensor(out=C.rotM[:], in0=xt0[:, 384:512], in1=xt0[:, 128:256],
                                               op=ALU.subtract), reads=tAr, writes=[("rotM",)])
    if "cos" in C.dbg:
        P.emit("gpsimd", lambda s: s.dma_start(out=C.dbg["cos"][:, 0:C.S], in_=C.cosT[:, 0:C.S]), reads=ROPEB + [("dbgser",)], writes=[("dbgser",)], dma_sem="dbgsem")
        P.emit("gpsimd", lambda s: s.dma_start(out=C.dbg["sin"][:, 0:C.S], in_=C.sinS[:, 0:C.S]), reads=ROPEB + [("dbgser",)], writes=[("dbgser",)], dma_sem="dbgsem")

    P.emit("gpsimd", lambda g: g.dma_start(out=C.gstage[0:8, :], in_=C.gw_d),
           writes=[("gstage", 0)], dma_sem="gsem")
    P.emit("gpsimd", lambda g: g.dma_start(out=C.gstage[8:9, :], in_=C.sgw_d.rearrange("(o d) -> o d", o=1)),
           writes=[("gstage", 1)], dma_sem="gsem")
    for dc in range(DC):
        pst, preg = (C.PSTa, ("PSTa",)) if dc % 2 == 0 else (C.PSTb, ("PSTb",))
        P.emit("tensor", lambda t, dc=dc, pst=pst: t.transpose(
            pst[:, 0:9], C.gstage[0:9, dc * 128:(dc + 1) * 128], C.ident[0:9, 0:9]),
            reads=[("gstage", 0), ("gstage", 1), ("ident",)], writes=[preg])
        P.emit("vector", lambda v, dc=dc, pst=pst: v.tensor_copy(
            out=C.gate_wT[:, dc * 9:(dc + 1) * 9], in_=pst[:, 0:9]),
            reads=[preg], writes=[("gate_wT",)])


def _emit_ln(C, first=True, interleave=None):
    P, S, TC = C.P, C.S, C.TC
    lnst = C.lnst
    big1_3d = C.BIG1[:, :].rearrange("p (c q) -> p c q", c=16, q=2048)
    for tc in range(TC):
        sl = tc % 4
        xv = C.xtile[sl]
        xreg = C.xtile_regions[sl]
        sl2 = tc % 2
        c = sl2 * 32
        P.emit("sync", lambda s, tc=tc, xv=xv: s.dma_start(out=xv[:], in_=C.x_d[tc * 128:(tc + 1) * 128, :]),
               writes=xreg, dma_sem="xt%d" % sl)
        if first:
            for bi in range(4):
                P.emit("vector", lambda v, xv=xv, c=c, bi=bi: v.bn_stats(
                    out=lnst[:, c + bi * 6: c + (bi + 1) * 6], in_=xv[:, bi * 512:(bi + 1) * 512]),
                    reads=xreg, writes=[("lnst", sl2, bi)])
            P.emit("vector", lambda v, c=c: v.bn_aggr(out=lnst[:, c + 24:c + 26], in_=lnst[:, c:c + 24]),
                   reads=[("lnst", sl2, bi) for bi in range(4)], writes=[("lnst", sl2, 4)])
            P.emit("scalar", lambda s, c=c: s.activation(out=lnst[:, c + 26:c + 27], in_=lnst[:, c + 25:c + 26],
                                                         func=AF.Ln, bias=LN_EPS),
                   reads=[("lnst", sl2, 4)], writes=[("lnst", sl2, 5)])
            P.emit("scalar", lambda s, c=c, tc=tc: s.activation(
                out=lnst[:, 64 + tc * 2:64 + tc * 2 + 1], in_=lnst[:, c + 26:c + 27],
                func=AF.Exp, scale=-0.5),
                reads=[("lnst", sl2, 5)], writes=[("lnstc", tc, 0)])
            P.emit("vector", lambda v, c=c, tc=tc: v.tensor_tensor(
                out=lnst[:, 64 + tc * 2 + 1:64 + tc * 2 + 2], in0=lnst[:, c + 24:c + 25],
                in1=lnst[:, 64 + tc * 2:64 + tc * 2 + 1], op=ALU.mult),
                reads=[("lnst", sl2, 4), ("lnstc", tc, 0)], writes=[("lnstc", tc, 1)])
            P.emit("vector", lambda v, tc=tc: v.tensor_scalar(
                out=lnst[:, 64 + tc * 2 + 1:64 + tc * 2 + 2],
                in0=lnst[:, 64 + tc * 2 + 1:64 + tc * 2 + 2],
                scalar1=-1.0, scalar2=None, op0=ALU.mult),
                reads=[("lnstc", tc, 1)], writes=[("lnstc", tc, 1)])
        xnv = C.xnsc[tc % 3]
        xnreg = C.xnsc_regions[tc % 3]
        P.emit("scalar", lambda s, xv=xv, xnv=xnv, tc=tc: s.activation(
            out=xnv[:], in_=xv[:], func=AF.Identity,
            scale=lnst[:, 64 + tc * 2:64 + tc * 2 + 1],
            bias=lnst[:, 64 + tc * 2 + 1:64 + tc * 2 + 2]),
            reads=xreg + [("lnstc", tc, 0), ("lnstc", tc, 1)], writes=xnreg)
        for g4 in range(4):
            pst, preg = (C.PSTa, ("PSTa",)) if g4 % 2 == 0 else (C.PSTb, ("PSTb",))

            def tf(t, g4=g4, pst=pst, xnv=xnv):
                last = None
                for j in range(4):
                    dc = g4 * 4 + j
                    last = t.transpose(pst[:, j * 128:(j + 1) * 128],
                                       xnv[:, dc * 128:(dc + 1) * 128], C.ident[:])
                return last

            P.emit("tensor", tf, reads=xnreg + [("ident",)], writes=[preg])
            eng = "vector" if g4 % 2 == 0 else "scalar"

            def evf(e, g4=g4, pst=pst, tc=tc):
                dst = big1_3d[:, g4 * 4:(g4 + 1) * 4, tc * 128:(tc + 1) * 128]
                srcv = pst[:, 0:512].rearrange("p (c u) -> p c u", c=4, u=128)
                if hasattr(e, "tensor_copy"):
                    return e.tensor_copy(out=dst, in_=srcv)
                return e.copy(out=dst, in_=srcv)

            P.emit(eng, evf,
                   reads=[preg], writes=[r for j in range(4) for r in C.xnt_regions(g4 * 4 + j)])
        if interleave is not None:
            interleave(tc)


def _emit_routing_mm(C, tc):
    P = C.P
    PS1 = C.PS1
    xn_reads = C.xnt_cols_regions(tc * 128, (tc + 1) * 128)
    ro = (tc % 2) * 512

    def mmf(t, tc=tc, ro=ro):
        last = None
        for dc in range(DC):
            last = t.matmul(PS1[:, ro:ro + 9], C.xnt(dc, tc * 128, (tc + 1) * 128),
                            C.gate_wT[:, dc * 9:(dc + 1) * 9],
                            start=(dc == 0), stop=(dc == DC - 1))
        return last

    P.emit("tensor", mmf, reads=[("gate_wT",)] + xn_reads, writes=[("PS1", tc % 2)])
    P.emit("scalar", lambda s, ro=ro, tc=tc: s.activation(out=C.lgb[:, tc * 9:tc * 9 + 9],
                                                          in_=PS1[:, ro:ro + 9], func=AF.Identity),
           reads=[("PS1", tc % 2)], writes=[("lgb", tc)])


def _emit_routing_tc(C, tc):
    P = C.P
    rt = C.rt
    R = [("rt",)]
    if True:
        P.emit("vector", lambda v, tc=tc: v.tensor_copy(out=rt[:, 0:9], in_=C.lgb[:, tc * 9:tc * 9 + 9]),
               reads=[("lgb", tc)], writes=R)
        P.emit("vector", lambda v: v.tensor_reduce(out=rt[:, 16:17], in_=rt[:, 0:8], axis=AX.X, op=ALU.max),
               reads=R, writes=R)
        P.emit("vector", lambda v: v.tensor_scalar(out=rt[:, 17:18], in0=rt[:, 16:17], scalar1=-1.0,
                                                   scalar2=None, op0=ALU.mult), reads=R, writes=R)
        P.emit("scalar", lambda s: s.activation(out=rt[:, 24:32], in_=rt[:, 0:8], func=AF.Exp,
                                                bias=rt[:, 17:18]), reads=R, writes=R)
        P.emit("vector", lambda v: v.tensor_reduce(out=rt[:, 32:33], in_=rt[:, 24:32], axis=AX.X, op=ALU.max),
               reads=R, writes=R)
        P.emit("vector", lambda v: v.tensor_scalar(out=rt[:, 33:41], in0=rt[:, 24:32], scalar1=rt[:, 32:33],
                                                   scalar2=None, op0=ALU.is_ge), reads=R, writes=R)
        P.emit("vector", lambda v: v.tensor_reduce(out=rt[:, 41:42], in_=rt[:, 33:41], axis=AX.X, op=ALU.add),
               reads=R, writes=R)
        P.emit("vector", lambda v: v.tensor_scalar(out=rt[:, 42:50], in0=rt[:, 33:41], scalar1=2.0,
                                                   scalar2=None, op0=ALU.mult), reads=R, writes=R)
        P.emit("vector", lambda v: v.tensor_tensor(out=rt[:, 42:50], in0=rt[:, 24:32], in1=rt[:, 42:50],
                                                   op=ALU.subtract), reads=R, writes=R)
        P.emit("vector", lambda v: v.tensor_reduce(out=rt[:, 50:51], in_=rt[:, 42:50], axis=AX.X, op=ALU.max),
               reads=R, writes=R)
        P.emit("vector", lambda v: v.tensor_scalar(out=rt[:, 51:52], in0=rt[:, 41:42], scalar1=2.0,
                                                   scalar2=None, op0=ALU.is_ge), reads=R, writes=R)
        P.emit("vector", lambda v: v.tensor_tensor(out=rt[:, 52:53], in0=rt[:, 32:33], in1=rt[:, 50:51],
                                                   op=ALU.subtract), reads=R, writes=R)
        P.emit("vector", lambda v: v.tensor_tensor(out=rt[:, 52:53], in0=rt[:, 51:52], in1=rt[:, 52:53],
                                                   op=ALU.mult), reads=R, writes=R)
        P.emit("vector", lambda v: v.tensor_tensor(out=rt[:, 53:54], in0=rt[:, 50:51], in1=rt[:, 52:53],
                                                   op=ALU.add), reads=R, writes=R)
        P.emit("vector", lambda v: v.tensor_scalar(out=rt[:, 54:62], in0=rt[:, 24:32], scalar1=rt[:, 53:54],
                                                   scalar2=None, op0=ALU.is_ge), reads=R, writes=R)
        P.emit("vector", lambda v: v.tensor_tensor(out=rt[:, 54:62], in0=rt[:, 24:32], in1=rt[:, 54:62],
                                                   op=ALU.mult), reads=R, writes=R)
        P.emit("vector", lambda v: v.tensor_tensor(out=rt[:, 62:63], in0=rt[:, 32:33], in1=rt[:, 53:54],
                                                   op=ALU.add), reads=R, writes=R)
        P.emit("vector", lambda v: v.reciprocal(out=rt[:, 63:64], in_=rt[:, 62:63]), reads=R, writes=R)
        P.emit("vector", lambda v, tc=tc: v.tensor_scalar(out=C.combine[:, tc * 8:(tc + 1) * 8],
                                                          in0=rt[:, 54:62], scalar1=rt[:, 63:64],
                                                          scalar2=None, op0=ALU.mult),
               reads=R, writes=[("combine",)])
        P.emit("vector", lambda v, tc=tc: v.tensor_copy(out=C.sglog[:, tc:tc + 1], in_=rt[:, 8:9]),
               reads=R, writes=[("sglog",)])


def _emit_routing_final(C):
    C.P.emit("scalar", lambda s: s.activation(out=C.sgate[:], in_=C.sglog[:], func=AF.Sigmoid),
             reads=[("sglog",)], writes=[("sgate",)])


def _load_wslice(C, h, which, src, slot):
    P = C.P
    stg, streg = C.stg[slot], C.stg_regions[slot]
    P.emit("gpsimd", lambda g, src=src, stg=stg, h=h: g.dma_start(
        out=stg[:], in_=src[h * 128:(h + 1) * 128, :]),
        writes=streg, dma_sem="st%d" % slot)
    wt, wtreg = C.WT[which], C.WT_regions[which]
    for g4 in range(4):
        pst, preg = (C.PSTa, ("PSTa",)) if g4 % 2 == 0 else (C.PSTb, ("PSTb",))

        def tf(t, g4=g4, pst=pst, stg=stg):
            last = None
            for j in range(4):
                dc = g4 * 4 + j
                last = t.transpose(pst[:, j * 128:(j + 1) * 128],
                                   stg[:, dc * 128:(dc + 1) * 128], C.ident[:])
            return last

        P.emit("tensor", tf, reads=streg + [("ident",)], writes=[preg])
        if g4 % 2 == 0:
            P.emit("vector", lambda v, g4=g4, pst=pst, wt=wt: v.tensor_copy(
                out=wt[:, g4 * 512:(g4 + 1) * 512], in_=pst[:, 0:512]),
                reads=[preg], writes=wtreg)
        else:
            P.emit("scalar", lambda s, g4=g4, pst=pst, wt=wt: s.copy(
                out=wt[:, g4 * 512:(g4 + 1) * 512], in_=pst[:, 0:512]),
                reads=[preg], writes=wtreg)


def _emit_head(C, h):
    P, S, QB, NQB = C.P, C.S, C.QB, C.NQB
    PS0, PS1 = C.PS0, C.PS1
    NP = (S + 511) // 512

    def load_wslice(which, src, slot):
        _load_wslice(C, h, which, src, slot)

    def proj(which, out_buf, out_regions, rope):
        wt, wtreg = C.WT[which], C.WT_regions[which]
        for p in range(NP):
            n0, n1 = p * 512, min(S, (p + 1) * 512)

            def mmf(t, n0=n0, n1=n1, wt=wt):
                last = None
                for dc in range(DC):
                    last = t.matmul(PS0[:, n0:n1], wt[:, dc * 128:(dc + 1) * 128],
                                    C.xnt(dc, n0, n1), start=(dc == 0), stop=(dc == DC - 1))
                return last

            P.emit("tensor", mmf, reads=wtreg + C.xnt_cols_regions(n0, n1), writes=ps0_banks(n0, n1))
            P.emit("scalar", lambda s, n0=n0, n1=n1: s.activation(
                out=C.k_raw[:, n0:n1], in_=PS0[:, n0:n1], func=AF.Identity),
                reads=ps0_banks(n0, n1), writes=C.k_raw_regions)
        if rope:
            kr, krr = C.k_raw, C.k_raw_regions
            for p in range(NP):
                n0, n1 = p * 512, min(S, (p + 1) * 512)
                P.emit("tensor", lambda t, n0=n0, n1=n1: t.matmul(
                    PS0[:, n0:n1], C.rotM[:], kr[:, n0:n1], start=True, stop=True),
                    reads=[("rotM",)] + krr, writes=ps0_banks(n0, n1))
            P.emit("vector", lambda v: v.tensor_tensor(out=C.t1b[:, 0:S], in0=kr[:, 0:S],
                                                       in1=C.cosT[:, 0:S], op=ALU.mult),
                   reads=krr + ROPEB, writes=C.t1b_regions)
            for p in range(NP):
                n0, n1 = p * 512, min(S, (p + 1) * 512)
                P.emit("vector", lambda v, ob=out_buf, n0=n0, n1=n1: v.tensor_tensor(
                    out=ob[:, n0:n1], in0=PS0[:, n0:n1], in1=C.sinS[:, n0:n1], op=ALU.mult),
                    reads=ps0_banks(n0, n1) + ROPEB, writes=[out_regions[min(p // 2, len(out_regions) - 1)]])
            for hh in range((S + 1023) // 1024):
                a0, a1 = hh * 1024, min(S, (hh + 1) * 1024)
                P.emit("vector", lambda v, ob=out_buf, a0=a0, a1=a1: v.tensor_tensor(
                    out=ob[:, a0:a1], in0=ob[:, a0:a1], in1=C.t1b[:, a0:a1], op=ALU.add),
                    reads=[out_regions[hh]] + C.t1b_regions, writes=[out_regions[hh]])
        else:
            P.emit("vector", lambda v, ob=out_buf: v.tensor_copy(out=ob[:, 0:S], in_=C.k_raw[:, 0:S]),
                   reads=C.k_raw_regions, writes=out_regions)

    proj("k", C.kT, C.kT_regions, rope=True)
    proj("q", C.qT, C.qT_regions, rope=True)
    if h > 0:
        _emit_headnorm(C, h - 1)
    proj("v", C.vT_sb, C.vT_regions, rope=False)

    KCH = S // 128
    for g4 in range((KCH + 3) // 4):
        pst, preg = (C.PSTa, ("PSTa",)) if g4 % 2 == 0 else (C.PSTb, ("PSTb",))
        nblk = min(4, KCH - g4 * 4)

        def tf(t, g4=g4, pst=pst, nblk=nblk):
            last = None
            for j in range(nblk):
                cch = g4 * 4 + j
                last = t.transpose(pst[:, j * 128:(j + 1) * 128],
                                   C.vT_sb[:, cch * 128:(cch + 1) * 128], C.ident[:])
            return last

        P.emit("tensor", tf, reads=C.vT_regions + [("ident",)], writes=[preg])
        if g4 % 2 == 0:
            P.emit("vector", lambda v, g4=g4, pst=pst, nblk=nblk: v.tensor_copy(
                out=C.V_sb[:, g4 * 512:g4 * 512 + nblk * 128], in_=pst[:, 0:nblk * 128]),
                reads=[preg], writes=C.V_regions)
        else:
            P.emit("scalar", lambda s, g4=g4, pst=pst, nblk=nblk: s.copy(
                out=C.V_sb[:, g4 * 512:g4 * 512 + nblk * 128], in_=pst[:, 0:nblk * 128]),
                reads=[preg], writes=C.V_regions)
    if h + 1 < NH:
        load_wslice_next = lambda which, srcw, slot: _load_wslice(C, h + 1, which, srcw, slot)
        load_wslice_next("k", C.wk_d, 0)
        load_wslice_next("q", C.wq_d, 1)
        load_wslice_next("v", C.wv_d, 0)

    for qb in range(NQB):
        q0 = qb * QB
        nch = (q0 + QB) // 128
        scnt = 0

        def pieces_of(cc):
            out = []
            pstart = max(cc * 128, q0)
            while pstart < q0 + QB:
                pend = min(q0 + ((pstart - q0) // 512 + 1) * 512, q0 + QB)
                out.append((pstart, pend))
                pstart = pend
            return out

        for cc in range(nch + 1):
            if cc < nch:
                psl = cc % 2
                pb, pbreg = C.probsT[psl], C.probsT_regions[psl]
                pcs = pieces_of(cc)
                merged = (len(pcs) == 2 and pcs[0][0] == q0 and QB == 1024)
                for (p0, p1) in pcs:
                    sc_off = 1024 + (scnt % 2) * 512
                    scnt += 1
                    P.emit("tensor", lambda t, cc=cc, p0=p0, p1=p1, sc_off=sc_off: t.matmul(
                        PS0[:, sc_off:sc_off + (p1 - p0)], C.kT[:, cc * 128:(cc + 1) * 128],
                        C.qT[:, p0:p1], start=True, stop=True),
                        reads=[C.kT_regions[min(cc // 8, len(C.kT_regions) - 1)],
                               C.qT_regions[min(p0 // 1024, len(C.qT_regions) - 1)]],
                        writes=[("PS0", sc_off // 512)])
                    if not merged:
                        P.emit("scalar", lambda s, p0=p0, p1=p1, sc_off=sc_off, pb=pb, q0=q0: s.activation(
                            out=pb[:, p0 - q0:p1 - q0], in_=PS0[:, sc_off:sc_off + (p1 - p0)],
                            func=AF.Exp, scale=SCALE, bias=-B_SHIFT),
                            reads=[("PS0", sc_off // 512)], writes=pbreg)
                if merged:
                    P.emit("scalar", lambda s, pb=pb: s.activation(
                        out=pb[:, 0:1024], in_=PS0[:, 1024:2048],
                        func=AF.Exp, scale=SCALE, bias=-B_SHIFT),
                        reads=[("PS0", 2), ("PS0", 3)], writes=pbreg)
                if cc * 128 >= q0:
                    d0 = cc * 128 - q0
                    P.emit("vector", lambda v, d0=d0, pb=pb: v.tensor_tensor(
                        out=pb[:, d0:d0 + 128], in0=pb[:, d0:d0 + 128], in1=C.mask128[:], op=ALU.mult),
                        reads=pbreg + [("mask",)], writes=pbreg)
            if cc >= 1:
                ccp = cc - 1
                pb, pbreg = C.probsT[ccp % 2], C.probsT_regions[ccp % 2]
                for (p0, p1) in pieces_of(ccp):
                    bank = (p0 - q0) // 512
                    last_c = min(nch - 1, (q0 + (bank + 1) * 512) // 128 - 1)
                    P.emit("tensor", lambda t, ccp=ccp, p0=p0, p1=p1, pb=pb, q0=q0, last_c=last_c: t.matmul(
                        PS0[:, p0 - q0:p1 - q0], C.V_sb[:, ccp * 128:(ccp + 1) * 128],
                        pb[:, p0 - q0:p1 - q0], start=(ccp == 0), stop=(ccp == last_c)),
                        reads=C.V_regions + pbreg, writes=[("PS0", bank)])
                    P.emit("tensor", lambda t, ccp=ccp, p0=p0, p1=p1, pb=pb, q0=q0, last_c=last_c:
                           t.matmul(PS1[0:1, (p0 - q0):(p1 - q0)],
                                    C.ones_c[:], pb[:, p0 - q0:p1 - q0],
                                    start=(ccp == 0), stop=(ccp == last_c)),
                           reads=[("ones",)] + pbreg, writes=[("PS1", (p0 - q0) // 512)])
        P.emit("scalar", lambda s, q0=q0, h=h, QB=QB: s.activation(
            out=C.apre(h, q0, q0 + QB), in_=PS0[:, 0:QB], func=AF.Identity),
            reads=ps0_banks(0, QB), writes=C.apre_regions(h))
        P.emit("scalar", lambda s, QB=QB: s.activation(
            out=C.denF[0:1, 0:QB], in_=PS1[0:1, 0:QB], func=AF.Identity),
            reads=[("PS1", 0), ("PS1", 1)], writes=C.den_regions)
        slot = (h * NQB + qb) % 3
        P.emit("vector", lambda v, QB=QB, slot=slot: _recip_lp(
            C.nc, v, C.rcp[slot][0:1, 0:QB], C.denF[0:1, 0:QB]),
            reads=C.den_regions, writes=C.rcp_regions[slot])


def _emit_headnorm(C, h):
    P, QB, NQB = C.P, C.QB, C.NQB
    PS0 = C.PS0
    for qb in range(NQB):
        q0 = qb * QB
        slot = (h * NQB + qb) % 3
        npiece = max(1, QB // 512)
        w = min(512, QB)
        for gpi in range(npiece):
            P.emit("tensor", lambda t, gpi=gpi, w=w, slot=slot: t.matmul(
                C.PS1[:, gpi * 512:gpi * 512 + w], C.ones_r[:],
                C.rcp[slot][0:1, gpi * 512:gpi * 512 + w], start=True, stop=True),
                reads=[("onesr",)] + C.rcp_regions[slot], writes=[("PS1", gpi)])
            rsl, rreg = C.rep_sb[gpi % 2], C.rep_regions[gpi % 2]
            P.emit("scalar", lambda s, rsl=rsl, w=w, gpi=gpi: s.activation(
                out=rsl[:, 0:w], in_=C.PS1[:, gpi * 512:gpi * 512 + w], func=AF.Identity),
                reads=[("PS1", gpi)], writes=rreg)
            P.emit("vector", lambda v, gpi=gpi, rsl=rsl, w=w, q0=q0, h=h: v.tensor_tensor(
                out=C.apre(h, q0 + gpi * 512, q0 + gpi * 512 + w),
                in0=C.apre(h, q0 + gpi * 512, q0 + gpi * 512 + w), in1=rsl[:, 0:w], op=ALU.mult),
                reads=C.apre_regions(h) + rreg, writes=C.apre_regions(h))


def _emit_wo(C):
    P, S, TC = C.P, C.S, C.TC
    PS0 = C.PS0
    apre_reads = [r for hh in range(NH) for r in C.apre_regions(hh)]
    for ns in range(4):
        wsl = ns % 2
        wot, wotreg = C.WoT[wsl], C.WoT_regions[wsl]
        wot3d = wot.rearrange("p (c u) -> p c u", c=16, u=512)
        for k in range(4):
            ssl = k % 2
            stg, streg = C.wstg[ssl], C.wstg_regions[ssl]
            P.emit("gpsimd", lambda g, ns=ns, k=k, stg=stg: g.dma_start(
                out=stg[:], in_=C.wo_d[ns * 512 + k * 128: ns * 512 + (k + 1) * 128, :]),
                writes=streg, dma_sem="st%d" % ssl)
            for g4 in range(4):
                pst, preg = (C.PSTa, ("PSTa",)) if g4 % 2 == 0 else (C.PSTb, ("PSTb",))

                def tf(t, g4=g4, pst=pst, stg=stg):
                    last = None
                    for j in range(4):
                        dc = g4 * 4 + j
                        last = t.transpose(pst[:, j * 128:(j + 1) * 128],
                                           stg[:, dc * 128:(dc + 1) * 128], C.ident[:])
                    return last

                P.emit("tensor", tf, reads=streg + [("ident",)], writes=[preg])
                eng = "vector" if g4 % 2 == 0 else "scalar"

                def evw(e, g4=g4, pst=pst, k=k, wot3d=wot3d):
                    dst = wot3d[:, g4 * 4:(g4 + 1) * 4, k * 128:(k + 1) * 128]
                    srcv = pst[:, 0:512].rearrange("p (c u) -> p c u", c=4, u=128)
                    if hasattr(e, "tensor_copy"):
                        return e.tensor_copy(out=dst, in_=srcv)
                    return e.copy(out=dst, in_=srcv)

                P.emit(eng, evw, reads=[preg], writes=wotreg)
        for tc in range(TC):
            bank = tc % 4

            def mmf(t, tc=tc, bank=bank, wot=wot):
                last = None
                for ic in range(DC):
                    last = t.matmul(PS0[:, bank * 512:(bank + 1) * 512],
                                    C.apre(ic, tc * 128, (tc + 1) * 128),
                                    wot[:, ic * 512:(ic + 1) * 512],
                                    start=(ic == 0), stop=(ic == DC - 1))
                return last

            P.emit("tensor", mmf, reads=wotreg + apre_reads, writes=[("PS0", bank)])
            P.emit("scalar", lambda s, tc=tc, ns=ns, bank=bank: s.activation(
                out=C.resv(tc, ns * 512, (ns + 1) * 512), in_=PS0[:, bank * 512:(bank + 1) * 512],
                func=AF.Identity),
                reads=[("PS0", bank)], writes=C.res_regions(tc))


def _emit_expert(C, half, e):
    P, S, TH, THC = C.P, C.S, C.TH, C.THC
    PS0 = C.PS0
    if e < 8:
        def wg_src(hc):
            return C.wg_d[e].rearrange("(dc p) h -> p dc h", p=128)[:, :, hc * 128:(hc + 1) * 128]

        def wu_src(hc):
            return C.wu_d[e].rearrange("(dc p) h -> p dc h", p=128)[:, :, hc * 128:(hc + 1) * 128]

        wd_base = C.wd_d[e]

        def scale_ap(tcg):
            return C.combine[:, tcg * 8 + e: tcg * 8 + e + 1]
        scale_reg = ("combine",)
    else:
        eh = e - 8

        def wg_src(hc, eh=eh):
            return C.swg_d.rearrange("(dc p) h -> p dc h", p=128)[:, :, eh * HM + hc * 128: eh * HM + (hc + 1) * 128]

        def wu_src(hc, eh=eh):
            return C.swu_d.rearrange("(dc p) h -> p dc h", p=128)[:, :, eh * HM + hc * 128: eh * HM + (hc + 1) * 128]

        wd_base = C.swd_d[eh * HM:(eh + 1) * HM, :]

        def scale_ap(tcg):
            return C.sgate[:, tcg:tcg + 1]
        scale_reg = ("sgate",)

    t0 = half * TH
    NPT = (TH + 511) // 512

    def emit_wd_load(q):
        area = q % 2
        first_bridge = C.__dict__.setdefault("_wd_bridged", False)

        def df(g, q=q, area=area, wd_base=wd_base):
            srcv = wd_base[:, q * 512:(q + 1) * 512]
            srcv = srcv.rearrange("(c p) d -> p c d", p=128)
            dst = C.wd_area_view(area).rearrange("p (c d) -> p c d", c=HMC)
            return g.dma_start(out=dst, in_=srcv)

        extra = [] if first_bridge else C.wd_pages
        C._wd_bridged = True
        P.emit("gpsimd", df, writes=C.wd_regions(area) + extra, dma_sem="wdp%d" % area)

    for hc in range(HMC):
        gs, gsr = C.wgslot[hc % 2], C.wgslot_regions[hc % 2]
        us, usr = C.wuslot[hc % 2], C.wuslot_regions[hc % 2]
        P.emit("gpsimd", lambda g, hc=hc, gs=gs: g.dma_start(
            out=gs.rearrange("p (dc h) -> p dc h", dc=16), in_=wg_src(hc)),
            writes=gsr, dma_sem="wg%d" % (hc % 2))
        P.emit("gpsimd", lambda g, hc=hc, us=us: g.dma_start(
            out=us.rearrange("p (dc h) -> p dc h", dc=16), in_=wu_src(hc)),
            writes=usr, dma_sem="wu%d" % (hc % 2))

        def gmm(t, gs=gs, t0=t0, NPT=NPT, TH=TH):
            last = None
            for dc in range(DC):
                for p in range(NPT):
                    n0 = t0 + p * 512
                    n1 = min(t0 + TH, n0 + 512)
                    last = t.matmul(PS0[:, p * 512: p * 512 + (n1 - n0)], gs[:, dc * 128:(dc + 1) * 128],
                                    C.xnt(dc, n0, n1), start=(dc == 0), stop=(dc == DC - 1))
            return last

        P.emit("tensor", gmm, reads=gsr + C.xnt_cols_regions(t0, t0 + TH), writes=ps0_banks(0, TH))

        def umm(t, us=us, t0=t0, NPT=NPT, TH=TH):
            last = None
            for dc in range(DC):
                for p in range(NPT):
                    n0 = t0 + p * 512
                    n1 = min(t0 + TH, n0 + 512)
                    last = t.matmul(PS0[:, 1024 + p * 512: 1024 + p * 512 + (n1 - n0)],
                                    us[:, dc * 128:(dc + 1) * 128],
                                    C.xnt(dc, n0, n1), start=(dc == 0), stop=(dc == DC - 1))
            return last

        P.emit("tensor", umm, reads=usr + C.xnt_cols_regions(t0, t0 + TH), writes=ps0_banks(1024, 1024 + TH))
        gso, gsor = C.gsil[hc % 2], C.gsil_regions[hc % 2]
        P.emit("scalar", lambda s, gso=gso, TH=TH: s.activation(out=gso[:, 0:TH], in_=PS0[:, 0:TH], func=AF.Silu),
               reads=ps0_banks(0, TH), writes=gsor)
        P.emit("vector", lambda v, hc=hc, gso=gso, TH=TH: v.tensor_tensor(
            out=C.hpr_view(hc, 0, TH), in0=gso[:, 0:TH], in1=PS0[:, 1024:1024 + TH], op=ALU.mult),
            reads=gsor + ps0_banks(1024, 1024 + TH), writes=C.hpr_regions(hc))
        if hc == 3:
            emit_wd_load(0)
        if hc == 6:
            emit_wd_load(1)

    hpr_reads = [r for hc in range(HMC) for r in C.hpr_regions(hc)]
    for q in range(4):
        area = q % 2
        if q >= 2:
            emit_wd_load(q)
        for tc in range(THC):
            tcg = half * THC + tc
            bank = tc % 4

            def dmm(t, tc=tc, bank=bank, area=area):
                last = None
                for hc in range(HMC):
                    last = t.matmul(PS0[:, bank * 512:(bank + 1) * 512],
                                    C.hpr_view(hc, tc * 128, (tc + 1) * 128),
                                    C.wd_view(area, hc, 0, 512),
                                    start=(hc == 0), stop=(hc == HMC - 1))
                return last

            P.emit("tensor", dmm, reads=C.wd_regions(area) + hpr_reads,
                   writes=[("PS0", bank)])
            gso, gsor = C.gsil[tc % 2], C.gsil_regions[tc % 2]
            P.emit("scalar", lambda s, bank=bank, tcg=tcg, gso=gso: s.activation(
                out=gso[:, 0:512], in_=PS0[:, bank * 512:(bank + 1) * 512], func=AF.Identity,
                scale=scale_ap(tcg)),
                reads=[("PS0", bank)] + [scale_reg], writes=gsor)
            P.emit("vector", lambda v, tcg=tcg, q=q, gso=gso: v.tensor_tensor(
                out=C.resv(tcg, q * 512, (q + 1) * 512),
                in0=C.resv(tcg, q * 512, (q + 1) * 512), in1=gso[:, 0:512], op=ALU.add),
                reads=C.res_regions(tcg) + gsor, writes=C.res_regions(tcg))


def _emit_final(C):
    P, TC = C.P, C.TC
    for tcg in range(TC):
        sl = tcg % 2
        xv = C.xtile[sl]
        xreg = C.xtile_regions[sl]
        ov = C.outstage[sl]
        oreg = C.outstage_regions[sl]
        P.emit("sync", lambda s, tcg=tcg, xv=xv: s.dma_start(
            out=xv[:], in_=C.x_d[tcg * 128:(tcg + 1) * 128, :]),
            writes=xreg, dma_sem="xt%d" % sl)
        extra = [("wda", 0), ("wda", 1)] if tcg == 0 else []
        P.emit("vector", lambda v, tcg=tcg, xv=xv, ov=ov: v.tensor_tensor(
            out=ov[:], in0=xv[:], in1=C.resv(tcg, 0, 2048), op=ALU.add),
            reads=xreg + C.res_regions(tcg) + extra, writes=oreg)
        P.emit("sync", lambda s, tcg=tcg, ov=ov: s.dma_start(
            out=C.out_d[tcg * 128:(tcg + 1) * 128, :], in_=ov[:]),
            reads=oreg, dma_sem="osem")
    osem_total = P.counts["osem"]
    handles = P.sem_handles
    P.raw("sync", lambda s: s.wait_ge(handles["osem"], osem_total))
    if P.counts["dbgsem"]:
        dbg_total = P.counts["dbgsem"]
        P.raw("gpsimd", lambda g: g.wait_ge(handles["dbgsem"], dbg_total))



_CACHE = {}


def _get_program(S):
    if S not in _CACHE:
        _CACHE[S] = build_program(S)
    return _CACHE[S]


def kernel(L0_attn_x, L0_attn_Wq, L0_attn_Wk, L0_attn_Wv, L0_attn_Wo,
           L0_ffn_gate_w, L0_ffn_w_gate, L0_ffn_w_up, L0_ffn_w_down,
           L0_ffn_shgate_w, L0_ffn_shw_gate, L0_ffn_shw_up, L0_ffn_shw_down):
    x = np.ascontiguousarray(np.asarray(L0_attn_x, dtype=np.float32))
    B = x.shape[0]
    weights = {
        "L0_attn_Wq": np.ascontiguousarray(np.asarray(L0_attn_Wq, np.float32)),
        "L0_attn_Wk": np.ascontiguousarray(np.asarray(L0_attn_Wk, np.float32)),
        "L0_attn_Wv": np.ascontiguousarray(np.asarray(L0_attn_Wv, np.float32)),
        "L0_attn_Wo": np.ascontiguousarray(np.asarray(L0_attn_Wo, np.float32)),
        "L0_ffn_gate_w": np.ascontiguousarray(np.asarray(L0_ffn_gate_w, np.float32)),
        "L0_ffn_w_gate": np.ascontiguousarray(np.asarray(L0_ffn_w_gate, np.float32)),
        "L0_ffn_w_up": np.ascontiguousarray(np.asarray(L0_ffn_w_up, np.float32)),
        "L0_ffn_w_down": np.ascontiguousarray(np.asarray(L0_ffn_w_down, np.float32)),
        "L0_ffn_shgate_w": np.ascontiguousarray(np.asarray(L0_ffn_shgate_w, np.float32)),
        "L0_ffn_shw_gate": np.ascontiguousarray(np.asarray(L0_ffn_shw_gate, np.float32)),
        "L0_ffn_shw_up": np.ascontiguousarray(np.asarray(L0_ffn_shw_up, np.float32)),
        "L0_ffn_shw_down": np.ascontiguousarray(np.asarray(L0_ffn_shw_down, np.float32)),
    }
    nc, _ = _get_program(x.shape[1])
    in_maps = [dict(weights, x=x[b]) for b in range(B)]
    res = run_bass_kernel_spmd(nc, in_maps, core_ids=list(range(B)))
    out = np.stack([np.asarray(r["out"], dtype=np.float32) for r in res.results], axis=0)
    return out


# version 12
# speedup vs baseline: 1.0540x; 1.0053x over previous
import math
from contextlib import ExitStack

import numpy as np

import concourse.bass as bass
import concourse.mybir as mybir
from concourse.bass_utils import run_bass_kernel_spmd

F32 = mybir.dt.float32
BF16 = mybir.dt.bfloat16
I32 = mybir.dt.int32
AF = mybir.ActivationFunctionType
ALU = mybir.AluOpType
AX = mybir.AxisListType

D = 2048
DC = 16
NH = 16
HM = 1408
HMC = 11
HS = 2816
NEXP = 10
B_SHIFT = 24.0
SCALE = 1.0 / math.sqrt(128.0)
LN_EPS = 1e-5
PIH = math.pi / 2.0

ENGINES = ("sync", "tensor", "vector", "scalar", "gpsimd")
ROPEB = [("ropebf", i) for i in range(4)]


class Prog:

    def __init__(self, nc):
        self.nc = nc
        self.thunks = {e: [] for e in ENGINES}
        self.sem_handles = {}
        self.counts = {}
        self.clocks = {e: {} for e in ENGINES}
        self.last_write = {}
        self.readers = {}
        self.stats = {e: 0 for e in ENGINES}

    def add_sem(self, name, handle):
        self.sem_handles[name] = handle
        self.counts[name] = 0

    def emit(self, eng, fn, reads=(), writes=(), dma_sem=None):
        need = {}

        def _need(tok):
            if tok is not None:
                s, v = tok
                if need.get(s, 0) < v:
                    need[s] = v

        if eng == "scalar":
            reads = list(reads) + [("cpool",)]
        for r in reads:
            _need(self.last_write.get(r))
        for r in writes:
            _need(self.last_write.get(r))
            for s, v in self.readers.get(r, {}).items():
                _need((s, v))
        clk = self.clocks[eng]
        waits = []
        for s, v in need.items():
            if clk.get(s, 0) < v:
                assert v <= self.counts[s], f"wait on future value {s}>={v}"
                waits.append((s, v))
                clk[s] = v
        if dma_sem is not None:
            semname, inc = dma_sem, 16
        else:
            semname, inc = "e_" + eng, 1
        self.counts[semname] += inc
        tok = (semname, self.counts[semname])
        for r in writes:
            self.last_write[r] = tok
            self.readers[r] = {}
        for r in reads:
            d = self.readers.setdefault(r, {})
            if d.get(semname, 0) < tok[1]:
                d[semname] = tok[1]
        semh = self.sem_handles[semname]
        handles = self.sem_handles
        self.stats[eng] += 1

        def thunk(engobj, waits=tuple(waits), fn=fn, semh=semh, inc=inc, handles=handles):
            for s, v in waits:
                engobj.wait_ge(handles[s], v)
            fn(engobj).then_inc(semh, inc)

        self.thunks[eng].append(thunk)
        return tok

    def raw(self, eng, fn):
        self.thunks[eng].append(fn)

    def set_write_token(self, region, tok):
        self.last_write[region] = tok
        self.readers[region] = {}


def bpages(name, c0, c1):
    return [(name, p) for p in range(c0 // 1024, (c1 + 1023) // 1024)]


def ps0_banks(c0, c1):
    return [("PS0", b) for b in range(c0 // 512, (c1 + 511) // 512)]


def _recip_lp(nc, v, out, in_):
    with nc.allow_low_precision(reason="bf16 softmax reciprocal used as matmul operand"):
        return v.reciprocal(out=out, in_=in_)


class Ctx:
    pass


def build_program(S=2048, debug=()):
    C = Ctx()
    C.S = S
    C.TC = S // 128
    C.QB = min(1024, S)
    C.NQB = S // C.QB
    C.TH = min(1024, S)
    C.NHALF = S // C.TH
    C.THC = C.TH // 128

    nc = bass.Bass("TRN2", target_bir_lowering=False, debug=False)
    C.nc = nc

    C.x_d = nc.dram_tensor("x", [S, D], F32, kind="ExternalInput").ap()
    C.wq_d = nc.dram_tensor("L0_attn_Wq", [D, D], F32, kind="ExternalInput").ap()
    C.wk_d = nc.dram_tensor("L0_attn_Wk", [D, D], F32, kind="ExternalInput").ap()
    C.wv_d = nc.dram_tensor("L0_attn_Wv", [D, D], F32, kind="ExternalInput").ap()
    C.wo_d = nc.dram_tensor("L0_attn_Wo", [D, D], F32, kind="ExternalInput").ap()
    C.gw_d = nc.dram_tensor("L0_ffn_gate_w", [8, D], F32, kind="ExternalInput").ap()
    C.wg_d = nc.dram_tensor("L0_ffn_w_gate", [8, D, HM], F32, kind="ExternalInput").ap()
    C.wu_d = nc.dram_tensor("L0_ffn_w_up", [8, D, HM], F32, kind="ExternalInput").ap()
    C.wd_d = nc.dram_tensor("L0_ffn_w_down", [8, HM, D], F32, kind="ExternalInput").ap()
    C.sgw_d = nc.dram_tensor("L0_ffn_shgate_w", [D], F32, kind="ExternalInput").ap()
    C.swg_d = nc.dram_tensor("L0_ffn_shw_gate", [D, HS], F32, kind="ExternalInput").ap()
    C.swu_d = nc.dram_tensor("L0_ffn_shw_up", [D, HS], F32, kind="ExternalInput").ap()
    C.swd_d = nc.dram_tensor("L0_ffn_shw_down", [HS, D], F32, kind="ExternalInput").ap()
    C.out_d = nc.dram_tensor("out", [S, D], F32, kind="ExternalOutput").ap()
    C.dbg = {}
    dbg_shapes = {
        "xnt": [128, 32768], "apre": [128, 32768], "resattn": [128, 32768],
        "res": [128, 32768], "combine": [128, C.TC * 8], "sgate": [128, C.TC],
        "kt": [128, 2048], "qt": [128, 2048], "vsb": [128, 2048],
        "cos": [128, 2048], "sin": [128, 2048], "hpr": [128, 11264],
    }
    debug = set(debug)
    if "kt" in debug:
        debug |= {"qt", "vsb"}
    if "cos" in debug:
        debug.add("sin")
    if "combine" in debug:
        debug.add("sgate")
    for name in sorted(debug):
        C.dbg[name] = nc.dram_tensor("dbg_" + name, dbg_shapes[name], F32, kind="ExternalOutput").ap()

    st = ExitStack()

    def sb(name, shape, dtype):
        return st.enter_context(nc.sbuf_tensor(name, shape, dtype))

    C.BIG1 = sb("BIG1", [128, 32768], BF16)
    C.BIG2 = sb("BIG2", [128, 32768], BF16)
    C.BIG3 = sb("BIG3", [128, 32768], BF16)
    C.cosT = sb("cosT", [128, 2048], BF16)
    C.sinS = sb("sinS", [128, 2048], BF16)
    C.ident = sb("ident", [128, 128], BF16)
    C.mask128 = sb("mask128", [128, 128], BF16)
    C.ones_c = sb("ones_c", [128, 1], BF16)
    C.ones_r = sb("ones_r", [1, 128], BF16)
    C.cpool = sb("cpool", [128, 4], F32)
    C.invf = sb("invf", [64, 1], F32)
    C.gstage = sb("gstage", [9, 2048], BF16)
    C.gate_wT = sb("gate_wT", [128, DC * 9], BF16)
    C.combine = sb("combine", [128, C.TC * 8], F32)
    C.sgate = sb("sgate", [128, C.TC], F32)
    C.sglog = sb("sglog", [128, C.TC], F32)
    C.rt = sb("rt", [128, 64], F32)
    C.lgb = sb("lgb", [128, 144], F32)
    C.rotM = sb("rotM", [128, 128], BF16)
    C.lnst = sb("lnst", [128, 96], F32)

    C.PS0 = st.enter_context(nc.psum_tensor("PS0", [128, 2048], F32))
    C.PS1 = st.enter_context(nc.psum_tensor("PS1", [128, 1024], F32))
    C.PSTa = st.enter_context(nc.psum_tensor("PSTa", [128, 1024], BF16))
    C.PSTb = st.enter_context(nc.psum_tensor("PSTb", [128, 1024], BF16))

    semnames = (["e_" + e for e in ENGINES]
                + ["xt0", "xt1", "xt2", "xt3", "st0", "st1", "wg0", "wg1", "wu0", "wu1",
                   "wdp0", "wdp1", "gsem", "mrsem", "osem", "dbgsem"])
    P = Prog(nc)
    C.P = P
    for sn in semnames:
        P.add_sem(sn, st.enter_context(nc.semaphore(sn)))

    nc.const_aps.register(F32, float(-B_SHIFT), C.cpool[:, 0:1])
    nc.const_aps.register(F32, float(LN_EPS), C.cpool[:, 1:2])
    nc.const_aps.register(F32, float(PIH), C.cpool[:, 2:3])
    nc.const_aps.register(F32, 0.0, C.cpool[:, 3:4])

    C.cosF = C.BIG2[:, 0:4096].bitcast(F32)
    C.sinF = C.BIG2[:, 4096:8192].bitcast(F32)
    _xt_off = [8192, 12288, 23040, 27136]
    C.xtile = [C.BIG2[:, o: o + 4096].bitcast(F32) for o in _xt_off]
    C.xtile_regions = [bpages("BIG2", o, o + 4096) for o in _xt_off]
    _xn_off = [16384, 18432, 20480]
    C.xnsc = [C.BIG2[:, o: o + 2048] for o in _xn_off]
    C.xnsc_regions = [bpages("BIG2", o, o + 2048) for o in _xn_off]
    C.lndump = C.BIG2[:, 20480:22528]
    C.io_j = C.BIG2[:, 22528:22784].bitcast(I32)
    C.io_i = C.BIG2[:, 22784:23040].bitcast(I32)
    C.outstage = [C.BIG2[:, 23040 + k * 4096: 27136 + k * 4096].bitcast(F32) for k in range(2)]
    C.outstage_regions = [bpages("BIG2", 23040 + k * 4096, 27136 + k * 4096) for k in range(2)]

    C.WT = {"k": C.BIG3[:, 0:2048], "q": C.BIG3[:, 2048:4096], "v": C.BIG3[:, 4096:6144]}
    C.WT_regions = {"k": bpages("BIG3", 0, 2048), "q": bpages("BIG3", 2048, 4096),
                    "v": bpages("BIG3", 4096, 6144)}
    C.stg = [C.BIG3[:, 6144:8192], C.BIG3[:, 8192:10240]]
    C.stg_regions = [bpages("BIG3", 6144, 8192), bpages("BIG3", 8192, 10240)]
    C.k_raw = C.BIG3[:, 10240:12288]
    C.k_raw_regions = bpages("BIG3", 10240, 12288)
    C.t1b = C.BIG3[:, 12288:14336]
    C.t1b_regions = bpages("BIG3", 12288, 14336)
    C.vT_sb = C.BIG3[:, 14336:16384]
    C.vT_regions = bpages("BIG3", 14336, 16384)
    C.kT = C.BIG3[:, 16384:18432]
    C.kT_regions = bpages("BIG3", 16384, 18432)
    C.qT = C.BIG3[:, 18432:20480]
    C.qT_regions = bpages("BIG3", 18432, 20480)
    C.V_sb = C.BIG3[:, 20480:22528]
    C.V_regions = bpages("BIG3", 20480, 22528)
    C.probsT = [C.BIG3[:, 22528:23552], C.BIG3[:, 23552:24576]]
    C.probsT_regions = [bpages("BIG3", 22528, 23552), bpages("BIG3", 23552, 24576)]
    C.denF = C.BIG3[:, 24576:26624].bitcast(F32)
    C.den_regions = bpages("BIG3", 24576, 26624)
    C.rcp = [C.BIG3[:, 26624:27648], C.BIG3[:, 29696:30720], C.BIG3[:, 30720:31744]]
    C.rcp_regions = [bpages("BIG3", 26624, 27648), bpages("BIG3", 29696, 30720),
                     bpages("BIG3", 30720, 31744)]
    C.rep_sb = [C.BIG3[:, 27648:28672], C.BIG3[:, 28672:29696]]
    C.rep_regions = [bpages("BIG3", 27648, 28672), bpages("BIG3", 28672, 29696)]

    C.WoT = [C.BIG1[:, 0:8192], C.BIG1[:, 8192:16384]]
    C.WoT_regions = [bpages("BIG1", 0, 8192), bpages("BIG1", 8192, 16384)]
    C.wstg = [C.BIG1[:, 16384:18432], C.BIG1[:, 18432:20480]]
    C.wstg_regions = [bpages("BIG1", 16384, 18432), bpages("BIG1", 18432, 20480)]

    C.wgslot = [C.BIG2[:, 11264:13312], C.BIG2[:, 13312:15360]]
    C.wgslot_regions = [bpages("BIG2", 11264, 13312), bpages("BIG2", 13312, 15360)]
    C.wuslot = [C.BIG2[:, 15360:17408], C.BIG2[:, 17408:19456]]
    C.wuslot_regions = [bpages("BIG2", 15360, 17408), bpages("BIG2", 17408, 19456)]
    C.gsil = [C.BIG2[:, 19456:20480], C.BIG2[:, 20480:21504]]
    C.gsil_regions = [bpages("BIG2", 19456, 20480), bpages("BIG2", 20480, 21504)]

    def hpr_view(hc, c0, c1):
        return C.BIG2[:, hc * 1024 + c0: hc * 1024 + c1]

    def hpr_regions(hc):
        return bpages("BIG2", hc * 1024, (hc + 1) * 1024)

    def wd_view(area, hc, c0, c1):
        base = 21504 + area * 5632 + hc * 512
        return C.BIG2[:, base + c0: base + c1]

    def wd_area_view(area):
        return C.BIG2[:, 21504 + area * 5632: 21504 + area * 5632 + 5632]

    C.wd_area_view = wd_area_view
    C.wd_pages = bpages("BIG2", 21504, 32768)

    def wd_regions(area):
        return [("wda", area)]

    def xnt(dc, t0, t1):
        return C.BIG1[:, dc * 2048 + t0: dc * 2048 + t1]

    def xnt_regions(dc):
        return bpages("BIG1", dc * 2048, dc * 2048 + S)

    def apre(h, t0, t1):
        return C.BIG2[:, h * 2048 + t0: h * 2048 + t1]

    def apre_regions(h):
        return bpages("BIG2", h * 2048, h * 2048 + S)

    def resv(tcg, c0, c1):
        return C.BIG3[:, tcg * 2048 + c0: tcg * 2048 + c1]

    def res_regions(tcg):
        return bpages("BIG3", tcg * 2048, (tcg + 1) * 2048)

    def xnt_cols_regions(t0, t1):
        out = []
        for dc in range(DC):
            out += bpages("BIG1", dc * 2048 + t0, dc * 2048 + t1)
        return out

    C.xnt_cols_regions = xnt_cols_regions
    C.hpr_view, C.hpr_regions = hpr_view, hpr_regions
    C.wd_view, C.wd_regions = wd_view, wd_regions
    C.xnt, C.xnt_regions = xnt, xnt_regions
    C.apre, C.apre_regions = apre, apre_regions
    C.resv, C.res_regions = resv, res_regions
    C.XNT_ALL = [r for dc in range(DC) for r in xnt_regions(dc)]

    block = st.enter_context(nc.Block())
    with st:
        _emit_all(C)
        for e in ENGINES:
            def runner(engobj, thunks=P.thunks[e]):
                for t in thunks:
                    t(engobj)
            getattr(block, e)(runner)
    return nc, C



def _emit_all(C):
    P = C.P
    _emit_consts(C)
    _load_wslice(C, 0, "k", C.wk_d, 0)
    _load_wslice(C, 0, "q", C.wq_d, 1)
    _load_wslice(C, 0, "v", C.wv_d, 0)
    _emit_ln(C, first=True)
    for tc in range(C.TC):
        _emit_routing_mm(C, tc)
    for tc in range(C.TC):
        _emit_routing_tc(C, tc)
    _emit_routing_final(C)
    if "xnt" in C.dbg:
        P.emit("gpsimd", lambda s: s.dma_start(
            out=C.dbg["xnt"].rearrange("p (c q) -> p c q", c=16, q=2048)[:, :, 0:C.S],
            in_=C.BIG1[:, :].rearrange("p (c q) -> p c q", c=16, q=2048)[:, :, 0:C.S]),
               reads=C.XNT_ALL + [("dbgser",)], writes=[("dbgser",)], dma_sem="dbgsem")
    if "combine" in C.dbg:
        P.emit("gpsimd", lambda s: s.dma_start(out=C.dbg["combine"], in_=C.combine[:]),
               reads=[("combine",)] + [("dbgser",)], writes=[("dbgser",)], dma_sem="dbgsem")
        P.emit("gpsimd", lambda s: s.dma_start(out=C.dbg["sgate"], in_=C.sgate[:]),
               reads=[("sgate",)] + [("dbgser",)], writes=[("dbgser",)], dma_sem="dbgsem")
    for h in range(NH):
        _emit_head(C, h)
    _emit_headnorm(C, NH - 1)

    if "apre" in C.dbg:
        P.emit("gpsimd", lambda s: s.dma_start(
            out=C.dbg["apre"].rearrange("p (c q) -> p c q", c=16, q=2048)[:, :, 0:C.S],
            in_=C.BIG2[:, :].rearrange("p (c q) -> p c q", c=16, q=2048)[:, :, 0:C.S]),
               reads=[r for hh in range(NH) for r in C.apre_regions(hh)] + [("dbgser",)], writes=[("dbgser",)], dma_sem="dbgsem")
    if "kt" in C.dbg:
        P.emit("gpsimd", lambda s: s.dma_start(out=C.dbg["kt"][:, 0:C.S], in_=C.kT[:, 0:C.S]), reads=C.kT_regions + [("dbgser",)], writes=[("dbgser",)], dma_sem="dbgsem")
        P.emit("gpsimd", lambda s: s.dma_start(out=C.dbg["qt"][:, 0:C.S], in_=C.qT[:, 0:C.S]), reads=C.qT_regions + [("dbgser",)], writes=[("dbgser",)], dma_sem="dbgsem")
        P.emit("gpsimd", lambda s: s.dma_start(out=C.dbg["vsb"][:, 0:C.S], in_=C.V_sb[:, 0:C.S]), reads=C.V_regions + [("dbgser",)], writes=[("dbgser",)], dma_sem="dbgsem")
    _emit_wo(C)
    if "resattn" in C.dbg:
        P.emit("gpsimd", lambda s: s.dma_start(out=C.dbg["resattn"][:, 0:C.TC * 2048], in_=C.BIG3[:, 0:C.TC * 2048]),
               reads=[r for t in range(C.TC) for r in C.res_regions(t)] + [("dbgser",)], writes=[("dbgser",)], dma_sem="dbgsem")
    _emit_ln(C, first=False)
    for half in range(C.NHALF):
        for e in range(NEXP):
            _emit_expert(C, half, e)
    if "hpr" in C.dbg:
        P.emit("gpsimd", lambda s: s.dma_start(out=C.dbg["hpr"], in_=C.BIG2[:, 0:11264]),
               reads=[r for hc in range(HMC) for r in C.hpr_regions(hc)] + [("dbgser",)], writes=[("dbgser",)], dma_sem="dbgsem")
    if "res" in C.dbg:
        P.emit("gpsimd", lambda s: s.dma_start(out=C.dbg["res"][:, 0:C.TC * 2048], in_=C.BIG3[:, 0:C.TC * 2048]),
               reads=[r for t in range(C.TC) for r in C.res_regions(t)] + [("dbgser",)], writes=[("dbgser",)], dma_sem="dbgsem")
    _emit_final(C)


def _emit_consts(C):
    P, nc = C.P, C.nc
    P.emit("gpsimd", lambda g: g.iota(C.io_j[:], pattern=[[1, 128]], base=0, channel_multiplier=0),
           writes=[("io", "j")])
    P.emit("gpsimd", lambda g: g.iota(C.io_i[:], pattern=[[0, 128]], base=0, channel_multiplier=1),
           writes=[("io", "i")])
    P.emit("gpsimd", lambda g: g.memset(C.ones_c[:], 1.0), writes=[("ones",)])
    P.emit("gpsimd", lambda g: g.memset(C.ones_r[:], 1.0), writes=[("onesr",)])
    for col, val in ((0, -B_SHIFT), (1, LN_EPS), (2, PIH), (3, 0.0)):
        P.emit("vector", lambda v, col=col, val=val: v.memset(C.cpool[:, col:col + 1], float(val)),
               writes=[("cpool",)])
    P.emit("vector", lambda v: v.tensor_tensor(out=C.ident[:], in0=C.io_j[:], in1=C.io_i[:], op=ALU.is_equal),
           reads=[("io", "j"), ("io", "i")], writes=[("ident",)])
    P.emit("vector", lambda v: v.tensor_tensor(out=C.mask128[:], in0=C.io_j[:], in1=C.io_i[:], op=ALU.is_ge),
           reads=[("io", "j"), ("io", "i")], writes=[("mask",)])

    S = C.S
    cosF, sinF = C.cosF, C.sinF
    CFR, SFR = [("cosF",)], [("sinF",)]
    xt0 = C.xtile[0]
    xt1 = C.xtile[1]
    tAr = C.xtile_regions[0] + C.xtile_regions[1]
    P.emit("scalar", lambda s: s.activation(out=C.invf[:], in_=C.io_i[0:64, 0:1], func=AF.Exp,
                                            scale=-math.log(10000.0) / 64.0),
           reads=[("io", "i")], writes=[("invf",)])
    P.emit("vector", lambda v: v.memset(cosF[0:64, 0:1], 1.0), writes=CFR)
    P.emit("vector", lambda v: v.memset(sinF[0:64, 0:1], 0.0), writes=SFR)
    P.emit("scalar", lambda s: s.activation(out=cosF[0:64, 1:2], in_=C.invf[:], func=AF.Sin, bias=PIH),
           reads=[("invf",), ("cpool",)], writes=CFR)
    P.emit("scalar", lambda s: s.activation(out=sinF[0:64, 1:2], in_=C.invf[:], func=AF.Sin),
           reads=[("invf",)], writes=SFR)
    L = 1
    while L < S - 1:
        n = min(L, S - 1 - L)
        P.emit("vector", lambda v, n=n, L=L: v.tensor_scalar(
            out=xt0[0:64, 0:n], in0=cosF[0:64, 1:1 + n], scalar1=cosF[0:64, L:L + 1], scalar2=None, op0=ALU.mult),
            reads=CFR, writes=tAr)
        P.emit("vector", lambda v, n=n, L=L: v.tensor_scalar(
            out=xt1[0:64, 0:n], in0=sinF[0:64, 1:1 + n], scalar1=sinF[0:64, L:L + 1], scalar2=None, op0=ALU.mult),
            reads=SFR + CFR, writes=tAr)
        P.emit("vector", lambda v, n=n, L=L: v.tensor_tensor(
            out=cosF[0:64, L + 1:L + 1 + n], in0=xt0[0:64, 0:n], in1=xt1[0:64, 0:n], op=ALU.subtract),
            reads=tAr, writes=CFR)
        P.emit("vector", lambda v, n=n, L=L: v.tensor_scalar(
            out=xt0[0:64, 0:n], in0=sinF[0:64, 1:1 + n], scalar1=cosF[0:64, L:L + 1], scalar2=None, op0=ALU.mult),
            reads=SFR + CFR, writes=tAr)
        P.emit("vector", lambda v, n=n, L=L: v.tensor_scalar(
            out=xt1[0:64, 0:n], in0=cosF[0:64, 1:1 + n], scalar1=sinF[0:64, L:L + 1], scalar2=None, op0=ALU.mult),
            reads=SFR + CFR, writes=tAr)
        P.emit("vector", lambda v, n=n, L=L: v.tensor_tensor(
            out=sinF[0:64, L + 1:L + 1 + n], in0=xt0[0:64, 0:n], in1=xt1[0:64, 0:n], op=ALU.add),
            reads=tAr, writes=SFR)
        L += n
    P.emit("gpsimd", lambda g: g.dma_start(out=C.cosT[0:64, 0:S], in_=cosF[0:64, 0:S]),
           reads=CFR, writes=[("ropebf", 0)], dma_sem="mrsem")
    P.emit("gpsimd", lambda g: g.dma_start(out=C.cosT[64:128, 0:S], in_=cosF[0:64, 0:S]),
           reads=CFR, writes=[("ropebf", 1)], dma_sem="mrsem")
    P.emit("gpsimd", lambda g: g.dma_start(out=C.sinS[0:64, 0:S], in_=sinF[0:64, 0:S]),
           reads=SFR, writes=[("ropebf", 2)], dma_sem="mrsem")
    P.emit("gpsimd", lambda g: g.dma_start(out=C.sinS[64:128, 0:S], in_=sinF[0:64, 0:S]),
           reads=SFR, writes=[("ropebf", 3)], dma_sem="mrsem")
    P.emit("vector", lambda v: v.tensor_scalar(out=xt0[:, 0:128], in0=C.io_j[:], scalar1=64.0,
                                               scalar2=None, op0=ALU.add), reads=[("io", "j")], writes=tAr)
    P.emit("vector", lambda v: v.tensor_tensor(out=xt0[:, 128:256], in0=C.io_i[:], in1=xt0[:, 0:128],
                                               op=ALU.is_equal), reads=[("io", "i")] + tAr, writes=tAr)
    P.emit("vector", lambda v: v.tensor_scalar(out=xt0[:, 256:384], in0=C.io_j[:], scalar1=-64.0,
                                               scalar2=None, op0=ALU.add), reads=[("io", "j")], writes=tAr)
    P.emit("vector", lambda v: v.tensor_tensor(out=xt0[:, 384:512], in0=C.io_i[:], in1=xt0[:, 256:384],
                                               op=ALU.is_equal), reads=[("io", "i")] + tAr, writes=tAr)
    P.emit("vector", lambda v: v.tensor_tensor(out=C.rotM[:], in0=xt0[:, 384:512], in1=xt0[:, 128:256],
                                               op=ALU.subtract), reads=tAr, writes=[("rotM",)])
    if "cos" in C.dbg:
        P.emit("gpsimd", lambda s: s.dma_start(out=C.dbg["cos"][:, 0:C.S], in_=C.cosT[:, 0:C.S]), reads=ROPEB + [("dbgser",)], writes=[("dbgser",)], dma_sem="dbgsem")
        P.emit("gpsimd", lambda s: s.dma_start(out=C.dbg["sin"][:, 0:C.S], in_=C.sinS[:, 0:C.S]), reads=ROPEB + [("dbgser",)], writes=[("dbgser",)], dma_sem="dbgsem")

    P.emit("gpsimd", lambda g: g.dma_start(out=C.gstage[0:8, :], in_=C.gw_d),
           writes=[("gstage", 0)], dma_sem="gsem")
    P.emit("gpsimd", lambda g: g.dma_start(out=C.gstage[8:9, :], in_=C.sgw_d.rearrange("(o d) -> o d", o=1)),
           writes=[("gstage", 1)], dma_sem="gsem")
    for dc in range(DC):
        pst, preg = (C.PSTa, ("PSTa",)) if dc % 2 == 0 else (C.PSTb, ("PSTb",))
        P.emit("tensor", lambda t, dc=dc, pst=pst: t.transpose(
            pst[:, 0:9], C.gstage[0:9, dc * 128:(dc + 1) * 128], C.ident[0:9, 0:9]),
            reads=[("gstage", 0), ("gstage", 1), ("ident",)], writes=[preg])
        P.emit("vector", lambda v, dc=dc, pst=pst: v.tensor_copy(
            out=C.gate_wT[:, dc * 9:(dc + 1) * 9], in_=pst[:, 0:9]),
            reads=[preg], writes=[("gate_wT",)])


def _emit_ln(C, first=True, interleave=None):
    P, S, TC = C.P, C.S, C.TC
    lnst = C.lnst
    big1_3d = C.BIG1[:, :].rearrange("p (c q) -> p c q", c=16, q=2048)
    for tc in range(TC):
        sl = tc % 4
        xv = C.xtile[sl]
        xreg = C.xtile_regions[sl]
        sl2 = tc % 2
        c = sl2 * 32
        P.emit("sync", lambda s, tc=tc, xv=xv: s.dma_start(out=xv[:], in_=C.x_d[tc * 128:(tc + 1) * 128, :]),
               writes=xreg, dma_sem="xt%d" % sl)
        if first:
            for bi in range(4):
                P.emit("vector", lambda v, xv=xv, c=c, bi=bi: v.bn_stats(
                    out=lnst[:, c + bi * 6: c + (bi + 1) * 6], in_=xv[:, bi * 512:(bi + 1) * 512]),
                    reads=xreg, writes=[("lnst", sl2, bi)])
            P.emit("vector", lambda v, c=c: v.bn_aggr(out=lnst[:, c + 24:c + 26], in_=lnst[:, c:c + 24]),
                   reads=[("lnst", sl2, bi) for bi in range(4)], writes=[("lnst", sl2, 4)])
            P.emit("scalar", lambda s, c=c: s.activation(out=lnst[:, c + 26:c + 27], in_=lnst[:, c + 25:c + 26],
                                                         func=AF.Ln, bias=LN_EPS),
                   reads=[("lnst", sl2, 4)], writes=[("lnst", sl2, 5)])
            P.emit("scalar", lambda s, c=c, tc=tc: s.activation(
                out=lnst[:, 64 + tc * 2:64 + tc * 2 + 1], in_=lnst[:, c + 26:c + 27],
                func=AF.Exp, scale=-0.5),
                reads=[("lnst", sl2, 5)], writes=[("lnstc", tc, 0)])
            P.emit("vector", lambda v, c=c, tc=tc: v.tensor_tensor(
                out=lnst[:, 64 + tc * 2 + 1:64 + tc * 2 + 2], in0=lnst[:, c + 24:c + 25],
                in1=lnst[:, 64 + tc * 2:64 + tc * 2 + 1], op=ALU.mult),
                reads=[("lnst", sl2, 4), ("lnstc", tc, 0)], writes=[("lnstc", tc, 1)])
            P.emit("vector", lambda v, tc=tc: v.tensor_scalar(
                out=lnst[:, 64 + tc * 2 + 1:64 + tc * 2 + 2],
                in0=lnst[:, 64 + tc * 2 + 1:64 + tc * 2 + 2],
                scalar1=-1.0, scalar2=None, op0=ALU.mult),
                reads=[("lnstc", tc, 1)], writes=[("lnstc", tc, 1)])
        xnv = C.xnsc[tc % 3]
        xnreg = C.xnsc_regions[tc % 3]
        P.emit("scalar", lambda s, xv=xv, xnv=xnv, tc=tc: s.activation(
            out=xnv[:], in_=xv[:], func=AF.Identity,
            scale=lnst[:, 64 + tc * 2:64 + tc * 2 + 1],
            bias=lnst[:, 64 + tc * 2 + 1:64 + tc * 2 + 2]),
            reads=xreg + [("lnstc", tc, 0), ("lnstc", tc, 1)], writes=xnreg)
        for g8 in range(2):
            pst, preg = (C.PSTa, ("PSTa",)) if g8 % 2 == 0 else (C.PSTb, ("PSTb",))

            def tf(t, g8=g8, pst=pst, xnv=xnv):
                last = None
                for j in range(8):
                    dc = g8 * 8 + j
                    last = t.transpose(pst[:, j * 128:(j + 1) * 128],
                                       xnv[:, dc * 128:(dc + 1) * 128], C.ident[:])
                return last

            P.emit("tensor", tf, reads=xnreg + [("ident",)], writes=[preg])
            eng = "vector" if g8 % 2 == 0 else "scalar"

            def evf(e, g8=g8, pst=pst, tc=tc):
                dst = big1_3d[:, g8 * 8:(g8 + 1) * 8, tc * 128:(tc + 1) * 128]
                srcv = pst[:, 0:1024].rearrange("p (c u) -> p c u", c=8, u=128)
                if hasattr(e, "tensor_copy"):
                    return e.tensor_copy(out=dst, in_=srcv)
                return e.copy(out=dst, in_=srcv)

            P.emit(eng, evf,
                   reads=[preg], writes=[r for j in range(8) for r in C.xnt_regions(g8 * 8 + j)])
        if interleave is not None:
            interleave(tc)


def _emit_routing_mm(C, tc):
    P = C.P
    PS1 = C.PS1
    xn_reads = C.xnt_cols_regions(tc * 128, (tc + 1) * 128)
    ro = (tc % 2) * 512

    def mmf(t, tc=tc, ro=ro):
        last = None
        for dc in range(DC):
            last = t.matmul(PS1[:, ro:ro + 9], C.xnt(dc, tc * 128, (tc + 1) * 128),
                            C.gate_wT[:, dc * 9:(dc + 1) * 9],
                            start=(dc == 0), stop=(dc == DC - 1))
        return last

    P.emit("tensor", mmf, reads=[("gate_wT",)] + xn_reads, writes=[("PS1", tc % 2)])
    P.emit("scalar", lambda s, ro=ro, tc=tc: s.activation(out=C.lgb[:, tc * 9:tc * 9 + 9],
                                                          in_=PS1[:, ro:ro + 9], func=AF.Identity),
           reads=[("PS1", tc % 2)], writes=[("lgb", tc)])


def _emit_routing_tc(C, tc):
    P = C.P
    rt = C.rt
    R = [("rt",)]
    if True:
        P.emit("vector", lambda v, tc=tc: v.tensor_copy(out=rt[:, 0:9], in_=C.lgb[:, tc * 9:tc * 9 + 9]),
               reads=[("lgb", tc)], writes=R)
        P.emit("vector", lambda v: v.tensor_reduce(out=rt[:, 16:17], in_=rt[:, 0:8], axis=AX.X, op=ALU.max),
               reads=R, writes=R)
        P.emit("vector", lambda v: v.tensor_scalar(out=rt[:, 17:18], in0=rt[:, 16:17], scalar1=-1.0,
                                                   scalar2=None, op0=ALU.mult), reads=R, writes=R)
        P.emit("scalar", lambda s: s.activation(out=rt[:, 24:32], in_=rt[:, 0:8], func=AF.Exp,
                                                bias=rt[:, 17:18]), reads=R, writes=R)
        P.emit("vector", lambda v: v.tensor_reduce(out=rt[:, 32:33], in_=rt[:, 24:32], axis=AX.X, op=ALU.max),
               reads=R, writes=R)
        P.emit("vector", lambda v: v.tensor_scalar(out=rt[:, 33:41], in0=rt[:, 24:32], scalar1=rt[:, 32:33],
                                                   scalar2=None, op0=ALU.is_ge), reads=R, writes=R)
        P.emit("vector", lambda v: v.tensor_reduce(out=rt[:, 41:42], in_=rt[:, 33:41], axis=AX.X, op=ALU.add),
               reads=R, writes=R)
        P.emit("vector", lambda v: v.tensor_scalar(out=rt[:, 42:50], in0=rt[:, 33:41], scalar1=2.0,
                                                   scalar2=None, op0=ALU.mult), reads=R, writes=R)
        P.emit("vector", lambda v: v.tensor_tensor(out=rt[:, 42:50], in0=rt[:, 24:32], in1=rt[:, 42:50],
                                                   op=ALU.subtract), reads=R, writes=R)
        P.emit("vector", lambda v: v.tensor_reduce(out=rt[:, 50:51], in_=rt[:, 42:50], axis=AX.X, op=ALU.max),
               reads=R, writes=R)
        P.emit("vector", lambda v: v.tensor_scalar(out=rt[:, 51:52], in0=rt[:, 41:42], scalar1=2.0,
                                                   scalar2=None, op0=ALU.is_ge), reads=R, writes=R)
        P.emit("vector", lambda v: v.tensor_tensor(out=rt[:, 52:53], in0=rt[:, 32:33], in1=rt[:, 50:51],
                                                   op=ALU.subtract), reads=R, writes=R)
        P.emit("vector", lambda v: v.tensor_tensor(out=rt[:, 52:53], in0=rt[:, 51:52], in1=rt[:, 52:53],
                                                   op=ALU.mult), reads=R, writes=R)
        P.emit("vector", lambda v: v.tensor_tensor(out=rt[:, 53:54], in0=rt[:, 50:51], in1=rt[:, 52:53],
                                                   op=ALU.add), reads=R, writes=R)
        P.emit("vector", lambda v: v.tensor_scalar(out=rt[:, 54:62], in0=rt[:, 24:32], scalar1=rt[:, 53:54],
                                                   scalar2=None, op0=ALU.is_ge), reads=R, writes=R)
        P.emit("vector", lambda v: v.tensor_tensor(out=rt[:, 54:62], in0=rt[:, 24:32], in1=rt[:, 54:62],
                                                   op=ALU.mult), reads=R, writes=R)
        P.emit("vector", lambda v: v.tensor_tensor(out=rt[:, 62:63], in0=rt[:, 32:33], in1=rt[:, 53:54],
                                                   op=ALU.add), reads=R, writes=R)
        P.emit("vector", lambda v: v.reciprocal(out=rt[:, 63:64], in_=rt[:, 62:63]), reads=R, writes=R)
        P.emit("vector", lambda v, tc=tc: v.tensor_scalar(out=C.combine[:, tc * 8:(tc + 1) * 8],
                                                          in0=rt[:, 54:62], scalar1=rt[:, 63:64],
                                                          scalar2=None, op0=ALU.mult),
               reads=R, writes=[("combine",)])
        P.emit("vector", lambda v, tc=tc: v.tensor_copy(out=C.sglog[:, tc:tc + 1], in_=rt[:, 8:9]),
               reads=R, writes=[("sglog",)])


def _emit_routing_final(C):
    C.P.emit("scalar", lambda s: s.activation(out=C.sgate[:], in_=C.sglog[:], func=AF.Sigmoid),
             reads=[("sglog",)], writes=[("sgate",)])


def _load_wslice(C, h, which, src, slot):
    P = C.P
    stg, streg = C.stg[slot], C.stg_regions[slot]
    P.emit("gpsimd", lambda g, src=src, stg=stg, h=h: g.dma_start(
        out=stg[:], in_=src[h * 128:(h + 1) * 128, :]),
        writes=streg, dma_sem="st%d" % slot)
    wt, wtreg = C.WT[which], C.WT_regions[which]
    for g8 in range(2):
        pst, preg = (C.PSTa, ("PSTa",)) if g8 % 2 == 0 else (C.PSTb, ("PSTb",))

        def tf(t, g8=g8, pst=pst, stg=stg):
            last = None
            for j in range(8):
                dc = g8 * 8 + j
                last = t.transpose(pst[:, j * 128:(j + 1) * 128],
                                   stg[:, dc * 128:(dc + 1) * 128], C.ident[:])
            return last

        P.emit("tensor", tf, reads=streg + [("ident",)], writes=[preg])
        if g8 % 2 == 0:
            P.emit("vector", lambda v, g8=g8, pst=pst, wt=wt: v.tensor_copy(
                out=wt[:, g8 * 1024:(g8 + 1) * 1024], in_=pst[:, 0:1024]),
                reads=[preg], writes=wtreg)
        else:
            P.emit("scalar", lambda s, g8=g8, pst=pst, wt=wt: s.copy(
                out=wt[:, g8 * 1024:(g8 + 1) * 1024], in_=pst[:, 0:1024]),
                reads=[preg], writes=wtreg)


def _emit_head(C, h):
    P, S, QB, NQB = C.P, C.S, C.QB, C.NQB
    PS0, PS1 = C.PS0, C.PS1
    NP = (S + 511) // 512

    def load_wslice(which, src, slot):
        _load_wslice(C, h, which, src, slot)

    def proj(which, out_buf, out_regions, rope):
        wt, wtreg = C.WT[which], C.WT_regions[which]
        for p in range(NP):
            n0, n1 = p * 512, min(S, (p + 1) * 512)

            def mmf(t, n0=n0, n1=n1, wt=wt):
                last = None
                for dc in range(DC):
                    last = t.matmul(PS0[:, n0:n1], wt[:, dc * 128:(dc + 1) * 128],
                                    C.xnt(dc, n0, n1), start=(dc == 0), stop=(dc == DC - 1))
                return last

            P.emit("tensor", mmf, reads=wtreg + C.xnt_cols_regions(n0, n1), writes=ps0_banks(n0, n1))
            P.emit("scalar", lambda s, n0=n0, n1=n1: s.activation(
                out=C.k_raw[:, n0:n1], in_=PS0[:, n0:n1], func=AF.Identity),
                reads=ps0_banks(n0, n1), writes=C.k_raw_regions)
        if rope:
            kr, krr = C.k_raw, C.k_raw_regions
            for p in range(NP):
                n0, n1 = p * 512, min(S, (p + 1) * 512)
                P.emit("tensor", lambda t, n0=n0, n1=n1: t.matmul(
                    PS0[:, n0:n1], C.rotM[:], kr[:, n0:n1], start=True, stop=True),
                    reads=[("rotM",)] + krr, writes=ps0_banks(n0, n1))
            P.emit("vector", lambda v: v.tensor_tensor(out=C.t1b[:, 0:S], in0=kr[:, 0:S],
                                                       in1=C.cosT[:, 0:S], op=ALU.mult),
                   reads=krr + ROPEB, writes=C.t1b_regions)
            for p in range(NP):
                n0, n1 = p * 512, min(S, (p + 1) * 512)
                P.emit("vector", lambda v, ob=out_buf, n0=n0, n1=n1: v.tensor_tensor(
                    out=ob[:, n0:n1], in0=PS0[:, n0:n1], in1=C.sinS[:, n0:n1], op=ALU.mult),
                    reads=ps0_banks(n0, n1) + ROPEB, writes=[out_regions[min(p // 2, len(out_regions) - 1)]])
            for hh in range((S + 1023) // 1024):
                a0, a1 = hh * 1024, min(S, (hh + 1) * 1024)
                P.emit("vector", lambda v, ob=out_buf, a0=a0, a1=a1: v.tensor_tensor(
                    out=ob[:, a0:a1], in0=ob[:, a0:a1], in1=C.t1b[:, a0:a1], op=ALU.add),
                    reads=[out_regions[hh]] + C.t1b_regions, writes=[out_regions[hh]])
        else:
            P.emit("vector", lambda v, ob=out_buf: v.tensor_copy(out=ob[:, 0:S], in_=C.k_raw[:, 0:S]),
                   reads=C.k_raw_regions, writes=out_regions)

    proj("k", C.kT, C.kT_regions, rope=True)
    proj("q", C.qT, C.qT_regions, rope=True)
    if h > 0:
        _emit_headnorm(C, h - 1)
    proj("v", C.vT_sb, C.vT_regions, rope=False)

    KCH = S // 128
    for g4 in range((KCH + 3) // 4):
        pst, preg = (C.PSTa, ("PSTa",)) if g4 % 2 == 0 else (C.PSTb, ("PSTb",))
        nblk = min(4, KCH - g4 * 4)

        def tf(t, g4=g4, pst=pst, nblk=nblk):
            last = None
            for j in range(nblk):
                cch = g4 * 4 + j
                last = t.transpose(pst[:, j * 128:(j + 1) * 128],
                                   C.vT_sb[:, cch * 128:(cch + 1) * 128], C.ident[:])
            return last

        P.emit("tensor", tf, reads=C.vT_regions + [("ident",)], writes=[preg])
        if g4 % 2 == 0:
            P.emit("vector", lambda v, g4=g4, pst=pst, nblk=nblk: v.tensor_copy(
                out=C.V_sb[:, g4 * 512:g4 * 512 + nblk * 128], in_=pst[:, 0:nblk * 128]),
                reads=[preg], writes=C.V_regions)
        else:
            P.emit("scalar", lambda s, g4=g4, pst=pst, nblk=nblk: s.copy(
                out=C.V_sb[:, g4 * 512:g4 * 512 + nblk * 128], in_=pst[:, 0:nblk * 128]),
                reads=[preg], writes=C.V_regions)
    if h + 1 < NH:
        load_wslice_next = lambda which, srcw, slot: _load_wslice(C, h + 1, which, srcw, slot)
        load_wslice_next("k", C.wk_d, 0)
        load_wslice_next("q", C.wq_d, 1)
        load_wslice_next("v", C.wv_d, 0)

    for qb in range(NQB):
        q0 = qb * QB
        nch = (q0 + QB) // 128
        scnt = 0

        def pieces_of(cc):
            out = []
            pstart = max(cc * 128, q0)
            while pstart < q0 + QB:
                pend = min(q0 + ((pstart - q0) // 512 + 1) * 512, q0 + QB)
                out.append((pstart, pend))
                pstart = pend
            return out

        for cc in range(nch + 1):
            if cc < nch:
                psl = cc % 2
                pb, pbreg = C.probsT[psl], C.probsT_regions[psl]
                pcs = pieces_of(cc)
                merged = (len(pcs) == 2 and pcs[0][0] == q0 and QB == 1024)
                for (p0, p1) in pcs:
                    sc_off = 1024 + (scnt % 2) * 512
                    scnt += 1
                    P.emit("tensor", lambda t, cc=cc, p0=p0, p1=p1, sc_off=sc_off: t.matmul(
                        PS0[:, sc_off:sc_off + (p1 - p0)], C.kT[:, cc * 128:(cc + 1) * 128],
                        C.qT[:, p0:p1], start=True, stop=True),
                        reads=[C.kT_regions[min(cc // 8, len(C.kT_regions) - 1)],
                               C.qT_regions[min(p0 // 1024, len(C.qT_regions) - 1)]],
                        writes=[("PS0", sc_off // 512)])
                    if not merged:
                        P.emit("scalar", lambda s, p0=p0, p1=p1, sc_off=sc_off, pb=pb, q0=q0: s.activation(
                            out=pb[:, p0 - q0:p1 - q0], in_=PS0[:, sc_off:sc_off + (p1 - p0)],
                            func=AF.Exp, scale=SCALE, bias=-B_SHIFT),
                            reads=[("PS0", sc_off // 512)], writes=pbreg)
                if merged:
                    P.emit("scalar", lambda s, pb=pb: s.activation(
                        out=pb[:, 0:1024], in_=PS0[:, 1024:2048],
                        func=AF.Exp, scale=SCALE, bias=-B_SHIFT),
                        reads=[("PS0", 2), ("PS0", 3)], writes=pbreg)
                if cc * 128 >= q0:
                    d0 = cc * 128 - q0
                    P.emit("vector", lambda v, d0=d0, pb=pb: v.tensor_tensor(
                        out=pb[:, d0:d0 + 128], in0=pb[:, d0:d0 + 128], in1=C.mask128[:], op=ALU.mult),
                        reads=pbreg + [("mask",)], writes=pbreg)
            if cc >= 1:
                ccp = cc - 1
                pb, pbreg = C.probsT[ccp % 2], C.probsT_regions[ccp % 2]
                for (p0, p1) in pieces_of(ccp):
                    bank = (p0 - q0) // 512
                    last_c = min(nch - 1, (q0 + (bank + 1) * 512) // 128 - 1)
                    P.emit("tensor", lambda t, ccp=ccp, p0=p0, p1=p1, pb=pb, q0=q0, last_c=last_c: t.matmul(
                        PS0[:, p0 - q0:p1 - q0], C.V_sb[:, ccp * 128:(ccp + 1) * 128],
                        pb[:, p0 - q0:p1 - q0], start=(ccp == 0), stop=(ccp == last_c)),
                        reads=C.V_regions + pbreg, writes=[("PS0", bank)])
                    P.emit("tensor", lambda t, ccp=ccp, p0=p0, p1=p1, pb=pb, q0=q0, last_c=last_c:
                           t.matmul(PS1[0:1, (p0 - q0):(p1 - q0)],
                                    C.ones_c[:], pb[:, p0 - q0:p1 - q0],
                                    start=(ccp == 0), stop=(ccp == last_c)),
                           reads=[("ones",)] + pbreg, writes=[("PS1", (p0 - q0) // 512)])
        P.emit("scalar", lambda s, q0=q0, h=h, QB=QB: s.activation(
            out=C.apre(h, q0, q0 + QB), in_=PS0[:, 0:QB], func=AF.Identity),
            reads=ps0_banks(0, QB), writes=C.apre_regions(h))
        P.emit("scalar", lambda s, QB=QB: s.activation(
            out=C.denF[0:1, 0:QB], in_=PS1[0:1, 0:QB], func=AF.Identity),
            reads=[("PS1", 0), ("PS1", 1)], writes=C.den_regions)
        slot = (h * NQB + qb) % 3
        P.emit("vector", lambda v, QB=QB, slot=slot: _recip_lp(
            C.nc, v, C.rcp[slot][0:1, 0:QB], C.denF[0:1, 0:QB]),
            reads=C.den_regions, writes=C.rcp_regions[slot])


def _emit_headnorm(C, h):
    P, QB, NQB = C.P, C.QB, C.NQB
    PS0 = C.PS0
    for qb in range(NQB):
        q0 = qb * QB
        slot = (h * NQB + qb) % 3
        npiece = max(1, QB // 512)
        w = min(512, QB)
        for gpi in range(npiece):
            P.emit("tensor", lambda t, gpi=gpi, w=w, slot=slot: t.matmul(
                C.PS1[:, gpi * 512:gpi * 512 + w], C.ones_r[:],
                C.rcp[slot][0:1, gpi * 512:gpi * 512 + w], start=True, stop=True),
                reads=[("onesr",)] + C.rcp_regions[slot], writes=[("PS1", gpi)])
            rsl, rreg = C.rep_sb[gpi % 2], C.rep_regions[gpi % 2]
            P.emit("scalar", lambda s, rsl=rsl, w=w, gpi=gpi: s.activation(
                out=rsl[:, 0:w], in_=C.PS1[:, gpi * 512:gpi * 512 + w], func=AF.Identity),
                reads=[("PS1", gpi)], writes=rreg)
            P.emit("vector", lambda v, gpi=gpi, rsl=rsl, w=w, q0=q0, h=h: v.tensor_tensor(
                out=C.apre(h, q0 + gpi * 512, q0 + gpi * 512 + w),
                in0=C.apre(h, q0 + gpi * 512, q0 + gpi * 512 + w), in1=rsl[:, 0:w], op=ALU.mult),
                reads=C.apre_regions(h) + rreg, writes=C.apre_regions(h))


def _emit_wo(C):
    P, S, TC = C.P, C.S, C.TC
    PS0 = C.PS0
    apre_reads = [r for hh in range(NH) for r in C.apre_regions(hh)]
    for ns in range(4):
        wsl = ns % 2
        wot, wotreg = C.WoT[wsl], C.WoT_regions[wsl]
        wot3d = wot.rearrange("p (c u) -> p c u", c=16, u=512)
        for k in range(4):
            ssl = k % 2
            stg, streg = C.wstg[ssl], C.wstg_regions[ssl]
            P.emit("gpsimd", lambda g, ns=ns, k=k, stg=stg: g.dma_start(
                out=stg[:], in_=C.wo_d[ns * 512 + k * 128: ns * 512 + (k + 1) * 128, :]),
                writes=streg, dma_sem="st%d" % ssl)
            for g4 in range(4):
                pst, preg = (C.PSTa, ("PSTa",)) if g4 % 2 == 0 else (C.PSTb, ("PSTb",))

                def tf(t, g4=g4, pst=pst, stg=stg):
                    last = None
                    for j in range(4):
                        dc = g4 * 4 + j
                        last = t.transpose(pst[:, j * 128:(j + 1) * 128],
                                           stg[:, dc * 128:(dc + 1) * 128], C.ident[:])
                    return last

                P.emit("tensor", tf, reads=streg + [("ident",)], writes=[preg])
                eng = "vector" if g4 % 2 == 0 else "scalar"

                def evw(e, g4=g4, pst=pst, k=k, wot3d=wot3d):
                    dst = wot3d[:, g4 * 4:(g4 + 1) * 4, k * 128:(k + 1) * 128]
                    srcv = pst[:, 0:512].rearrange("p (c u) -> p c u", c=4, u=128)
                    if hasattr(e, "tensor_copy"):
                        return e.tensor_copy(out=dst, in_=srcv)
                    return e.copy(out=dst, in_=srcv)

                P.emit(eng, evw, reads=[preg], writes=wotreg)
        for tc in range(TC):
            bank = tc % 4

            def mmf(t, tc=tc, bank=bank, wot=wot):
                last = None
                for ic in range(DC):
                    last = t.matmul(PS0[:, bank * 512:(bank + 1) * 512],
                                    C.apre(ic, tc * 128, (tc + 1) * 128),
                                    wot[:, ic * 512:(ic + 1) * 512],
                                    start=(ic == 0), stop=(ic == DC - 1))
                return last

            P.emit("tensor", mmf, reads=wotreg + apre_reads, writes=[("PS0", bank)])
            P.emit("scalar", lambda s, tc=tc, ns=ns, bank=bank: s.activation(
                out=C.resv(tc, ns * 512, (ns + 1) * 512), in_=PS0[:, bank * 512:(bank + 1) * 512],
                func=AF.Identity),
                reads=[("PS0", bank)], writes=C.res_regions(tc))


def _emit_expert(C, half, e):
    P, S, TH, THC = C.P, C.S, C.TH, C.THC
    PS0 = C.PS0
    if e < 8:
        def wg_src(hc):
            return C.wg_d[e].rearrange("(dc p) h -> p dc h", p=128)[:, :, hc * 128:(hc + 1) * 128]

        def wu_src(hc):
            return C.wu_d[e].rearrange("(dc p) h -> p dc h", p=128)[:, :, hc * 128:(hc + 1) * 128]

        wd_base = C.wd_d[e]

        def scale_ap(tcg):
            return C.combine[:, tcg * 8 + e: tcg * 8 + e + 1]
        scale_reg = ("combine",)
    else:
        eh = e - 8

        def wg_src(hc, eh=eh):
            return C.swg_d.rearrange("(dc p) h -> p dc h", p=128)[:, :, eh * HM + hc * 128: eh * HM + (hc + 1) * 128]

        def wu_src(hc, eh=eh):
            return C.swu_d.rearrange("(dc p) h -> p dc h", p=128)[:, :, eh * HM + hc * 128: eh * HM + (hc + 1) * 128]

        wd_base = C.swd_d[eh * HM:(eh + 1) * HM, :]

        def scale_ap(tcg):
            return C.sgate[:, tcg:tcg + 1]
        scale_reg = ("sgate",)

    t0 = half * TH
    NPT = (TH + 511) // 512

    def emit_wd_load(q):
        area = q % 2
        first_bridge = C.__dict__.setdefault("_wd_bridged", False)

        def df(g, q=q, area=area, wd_base=wd_base):
            srcv = wd_base[:, q * 512:(q + 1) * 512]
            srcv = srcv.rearrange("(c p) d -> p c d", p=128)
            dst = C.wd_area_view(area).rearrange("p (c d) -> p c d", c=HMC)
            return g.dma_start(out=dst, in_=srcv)

        extra = [] if first_bridge else C.wd_pages
        C._wd_bridged = True
        P.emit("gpsimd", df, writes=C.wd_regions(area) + extra, dma_sem="wdp%d" % area)

    for hc in range(HMC):
        gs, gsr = C.wgslot[hc % 2], C.wgslot_regions[hc % 2]
        us, usr = C.wuslot[hc % 2], C.wuslot_regions[hc % 2]
        P.emit("gpsimd", lambda g, hc=hc, gs=gs: g.dma_start(
            out=gs.rearrange("p (dc h) -> p dc h", dc=16), in_=wg_src(hc)),
            writes=gsr, dma_sem="wg%d" % (hc % 2))
        P.emit("gpsimd", lambda g, hc=hc, us=us: g.dma_start(
            out=us.rearrange("p (dc h) -> p dc h", dc=16), in_=wu_src(hc)),
            writes=usr, dma_sem="wu%d" % (hc % 2))

        def gmm(t, gs=gs, t0=t0, NPT=NPT, TH=TH):
            last = None
            for dc in range(DC):
                for p in range(NPT):
                    n0 = t0 + p * 512
                    n1 = min(t0 + TH, n0 + 512)
                    last = t.matmul(PS0[:, p * 512: p * 512 + (n1 - n0)], gs[:, dc * 128:(dc + 1) * 128],
                                    C.xnt(dc, n0, n1), start=(dc == 0), stop=(dc == DC - 1))
            return last

        P.emit("tensor", gmm, reads=gsr + C.xnt_cols_regions(t0, t0 + TH), writes=ps0_banks(0, TH))

        def umm(t, us=us, t0=t0, NPT=NPT, TH=TH):
            last = None
            for dc in range(DC):
                for p in range(NPT):
                    n0 = t0 + p * 512
                    n1 = min(t0 + TH, n0 + 512)
                    last = t.matmul(PS0[:, 1024 + p * 512: 1024 + p * 512 + (n1 - n0)],
                                    us[:, dc * 128:(dc + 1) * 128],
                                    C.xnt(dc, n0, n1), start=(dc == 0), stop=(dc == DC - 1))
            return last

        P.emit("tensor", umm, reads=usr + C.xnt_cols_regions(t0, t0 + TH), writes=ps0_banks(1024, 1024 + TH))
        gso, gsor = C.gsil[hc % 2], C.gsil_regions[hc % 2]
        P.emit("scalar", lambda s, gso=gso, TH=TH: s.activation(out=gso[:, 0:TH], in_=PS0[:, 0:TH], func=AF.Silu),
               reads=ps0_banks(0, TH), writes=gsor)
        P.emit("vector", lambda v, hc=hc, gso=gso, TH=TH: v.tensor_tensor(
            out=C.hpr_view(hc, 0, TH), in0=gso[:, 0:TH], in1=PS0[:, 1024:1024 + TH], op=ALU.mult),
            reads=gsor + ps0_banks(1024, 1024 + TH), writes=C.hpr_regions(hc))
        if hc == 3:
            emit_wd_load(0)
        if hc == 6:
            emit_wd_load(1)

    hpr_reads = [r for hc in range(HMC) for r in C.hpr_regions(hc)]
    for q in range(4):
        area = q % 2
        if q >= 2:
            emit_wd_load(q)
        for tc in range(THC):
            tcg = half * THC + tc
            bank = tc % 4

            def dmm(t, tc=tc, bank=bank, area=area):
                last = None
                for hc in range(HMC):
                    last = t.matmul(PS0[:, bank * 512:(bank + 1) * 512],
                                    C.hpr_view(hc, tc * 128, (tc + 1) * 128),
                                    C.wd_view(area, hc, 0, 512),
                                    start=(hc == 0), stop=(hc == HMC - 1))
                return last

            P.emit("tensor", dmm, reads=C.wd_regions(area) + hpr_reads,
                   writes=[("PS0", bank)])
            gso, gsor = C.gsil[tc % 2], C.gsil_regions[tc % 2]
            P.emit("scalar", lambda s, bank=bank, tcg=tcg, gso=gso: s.activation(
                out=gso[:, 0:512], in_=PS0[:, bank * 512:(bank + 1) * 512], func=AF.Identity,
                scale=scale_ap(tcg)),
                reads=[("PS0", bank)] + [scale_reg], writes=gsor)
            P.emit("vector", lambda v, tcg=tcg, q=q, gso=gso: v.tensor_tensor(
                out=C.resv(tcg, q * 512, (q + 1) * 512),
                in0=C.resv(tcg, q * 512, (q + 1) * 512), in1=gso[:, 0:512], op=ALU.add),
                reads=C.res_regions(tcg) + gsor, writes=C.res_regions(tcg))


def _emit_final(C):
    P, TC = C.P, C.TC
    for tcg in range(TC):
        sl = tcg % 2
        xv = C.xtile[sl]
        xreg = C.xtile_regions[sl]
        ov = C.outstage[sl]
        oreg = C.outstage_regions[sl]
        P.emit("sync", lambda s, tcg=tcg, xv=xv: s.dma_start(
            out=xv[:], in_=C.x_d[tcg * 128:(tcg + 1) * 128, :]),
            writes=xreg, dma_sem="xt%d" % sl)
        extra = [("wda", 0), ("wda", 1)] if tcg == 0 else []
        P.emit("vector", lambda v, tcg=tcg, xv=xv, ov=ov: v.tensor_tensor(
            out=ov[:], in0=xv[:], in1=C.resv(tcg, 0, 2048), op=ALU.add),
            reads=xreg + C.res_regions(tcg) + extra, writes=oreg)
        P.emit("sync", lambda s, tcg=tcg, ov=ov: s.dma_start(
            out=C.out_d[tcg * 128:(tcg + 1) * 128, :], in_=ov[:]),
            reads=oreg, dma_sem="osem")
    osem_total = P.counts["osem"]
    handles = P.sem_handles
    P.raw("sync", lambda s: s.wait_ge(handles["osem"], osem_total))
    if P.counts["dbgsem"]:
        dbg_total = P.counts["dbgsem"]
        P.raw("gpsimd", lambda g: g.wait_ge(handles["dbgsem"], dbg_total))



_CACHE = {}


def _get_program(S):
    if S not in _CACHE:
        _CACHE[S] = build_program(S)
    return _CACHE[S]


def kernel(L0_attn_x, L0_attn_Wq, L0_attn_Wk, L0_attn_Wv, L0_attn_Wo,
           L0_ffn_gate_w, L0_ffn_w_gate, L0_ffn_w_up, L0_ffn_w_down,
           L0_ffn_shgate_w, L0_ffn_shw_gate, L0_ffn_shw_up, L0_ffn_shw_down):
    x = np.ascontiguousarray(np.asarray(L0_attn_x, dtype=np.float32))
    B = x.shape[0]
    weights = {
        "L0_attn_Wq": np.ascontiguousarray(np.asarray(L0_attn_Wq, np.float32)),
        "L0_attn_Wk": np.ascontiguousarray(np.asarray(L0_attn_Wk, np.float32)),
        "L0_attn_Wv": np.ascontiguousarray(np.asarray(L0_attn_Wv, np.float32)),
        "L0_attn_Wo": np.ascontiguousarray(np.asarray(L0_attn_Wo, np.float32)),
        "L0_ffn_gate_w": np.ascontiguousarray(np.asarray(L0_ffn_gate_w, np.float32)),
        "L0_ffn_w_gate": np.ascontiguousarray(np.asarray(L0_ffn_w_gate, np.float32)),
        "L0_ffn_w_up": np.ascontiguousarray(np.asarray(L0_ffn_w_up, np.float32)),
        "L0_ffn_w_down": np.ascontiguousarray(np.asarray(L0_ffn_w_down, np.float32)),
        "L0_ffn_shgate_w": np.ascontiguousarray(np.asarray(L0_ffn_shgate_w, np.float32)),
        "L0_ffn_shw_gate": np.ascontiguousarray(np.asarray(L0_ffn_shw_gate, np.float32)),
        "L0_ffn_shw_up": np.ascontiguousarray(np.asarray(L0_ffn_shw_up, np.float32)),
        "L0_ffn_shw_down": np.ascontiguousarray(np.asarray(L0_ffn_shw_down, np.float32)),
    }
    nc, _ = _get_program(x.shape[1])
    in_maps = [dict(weights, x=x[b]) for b in range(B)]
    res = run_bass_kernel_spmd(nc, in_maps, core_ids=list(range(B)))
    out = np.stack([np.asarray(r["out"], dtype=np.float32) for r in res.results], axis=0)
    return out
